# version 05
# baseline (speedup 1.0000x reference)
import numpy as np
import ml_dtypes
from contextlib import ExitStack

import concourse.bass as bass
import concourse.mybir as mybir
from concourse.bass_utils import run_bass_kernel_spmd

F32 = mybir.dt.float32
BF16 = mybir.dt.bfloat16
U16 = mybir.dt.uint16
AF = mybir.ActivationFunctionType
OP = mybir.AluOpType
NPBF = ml_dtypes.bfloat16

NC = 8
D = 2048
T = 1024
HALO = 8
TH = T + HALO
NCD = 16
F_MLP = 6144
NCF = 48
HF = 8192
NCHF = 64
G2 = 16384

DUMPS = {}


class B:

    ENGS = ("sync", "gpsimd", "tensor", "vector", "scalar")

    def __init__(self, nc):
        self.nc = nc
        self.streams = {e: [] for e in self.ENGS}
        self.semh = {}
        self.semv = {}
        self.dma_sems = ["dw", "dh"]

    def mksem(self, ctx, *names):
        for n in names:
            self.semh[n] = ctx.enter_context(self.nc.semaphore(n))
            self.semv[n] = 0

    def mkdma(self, ctx, *names):
        self.mksem(ctx, *names)
        self.dma_sems.extend(names)

    ENG_SEM = {"tensor": "pe", "vector": "dv", "scalar": "ac"}

    def emit(self, eng, fn, waits=(), inc=None, k=1, pad=False):
        wl = [(s, v) for (s, v) in waits if v and v > 0]
        es = self.ENG_SEM.get(eng)
        if es is not None and fn is not None:
            cv = self.semv[es]
            if cv > 0:
                wl.append((es, cv))
        ws = tuple(wl)
        val = None
        if inc is not None:
            self.semv[inc] += (k if k is not None else 1)
            val = self.semv[inc]
        self.streams[eng].append((fn, ws, inc, k))
        if pad:
            return self.pad(eng)
        return val

    def cur(self, name):
        return self.semv[name]

    def pad(self, eng, pad_ap=None):
        sem = {"tensor": "pe", "vector": "dv", "scalar": "ac"}[eng]
        ap = self.pad_cells[eng]
        if eng == "scalar":
            return self.emit(eng, lambda e: e.copy(ap, ap), inc=sem)
        return self.emit(eng, lambda e: e.tensor_copy(ap, ap), inc=sem)

    def dma_fence(self):
        w = [(s, self.cur(s)) for s in self.dma_sems if self.cur(s) > 0]
        cw = [(s, self.cur(s)) for s in ("pe", "dv", "ac") if self.cur(s) > 0]
        for e in self.ENGS:
            self.emit(e, None, waits=(w + cw) if e in ("sync", "gpsimd") else w)

    def barrier(self, engs=("tensor", "vector", "scalar")):
        marks = {}
        for e, s in (("tensor", "pe"), ("vector", "dv"), ("scalar", "ac")):
            if e in engs:
                if e != "tensor":
                    marks[e] = (s, self.pad(e))
                else:
                    marks[e] = (s, self.cur(s))
        for e in engs:
            waits = [v for k2, v in marks.items() if k2 != e]
            self.emit(e, None, waits=waits)

    def finalize(self, block):
        def emit_stream(eng_name):
            def run(eng):
                last = {}
                for fn, ws, inc, k in self.streams[eng_name]:
                    for (s, v) in ws:
                        if last.get(s, -1) >= v:
                            continue
                        eng.wait_ge(self.semh[s], v)
                        last[s] = v
                    if fn is None:
                        continue
                    ins = fn(eng)
                    if inc is not None:
                        if k is None:
                            ins.then_inc(self.semh[inc])
                        else:
                            ins.then_inc(self.semh[inc], k)
            return run

        block.sync(emit_stream("sync"))
        block.gpsimd(emit_stream("gpsimd"))
        block.tensor(emit_stream("tensor"))
        block.vector(emit_stream("vector"))
        block.scalar(emit_stream("scalar"))


class Slot:
    def __init__(self, ap):
        self.ap = ap
        self.free = None

    def wait_free(self):
        return [self.free] if self.free else []


class DBuf:
    def __init__(self, t):
        self.t = t
        self.loaded = 0
        self.free = None


def build_nc():
    nc = bass.Bass("TRN2", target_bir_lowering=False)

    x_in = nc.dram_tensor("x_in", [TH, D], F32, kind="ExternalInput").ap()
    wR = nc.dram_tensor("wR", [D, D], F32, kind="ExternalInput").ap()
    wL = nc.dram_tensor("wL", [D, D], F32, kind="ExternalInput").ap()
    wGA = nc.dram_tensor("wGA", [D, D], F32, kind="ExternalInput").ap()
    wGX = nc.dram_tensor("wGX", [D, D], F32, kind="ExternalInput").ap()
    wO = nc.dram_tensor("wO", [D, D], F32, kind="ExternalInput").ap()
    wU1 = nc.dram_tensor("wU1", [D, F_MLP], F32, kind="ExternalInput").ap()
    wU2 = nc.dram_tensor("wU2", [D, F_MLP], F32, kind="ExternalInput").ap()
    wDN = nc.dram_tensor("wDN", [F_MLP, D], F32, kind="ExternalInput").ap()
    wGU = nc.dram_tensor("wGU", [D, G2], F32, kind="ExternalInput").ap()
    wPJ = nc.dram_tensor("wPJ", [HF, D], F32, kind="ExternalInput").ap()
    biasq = nc.dram_tensor("biasq", [128, 320], F32, kind="ExternalInput").ap()
    maskv = nc.dram_tensor("maskv", [128, NC], F32, kind="ExternalInput").ap()
    mask1m = nc.dram_tensor("mask1m", [128, NC], F32, kind="ExternalInput").ap()
    id128f = nc.dram_tensor("id128f", [128, 128], F32, kind="ExternalInput").ap()
    id128b = nc.dram_tensor("id128b", [128, 128], BF16, kind="ExternalInput").ap()
    id8f = nc.dram_tensor("id8f", [8, 8], F32, kind="ExternalInput").ap()
    on1f = nc.dram_tensor("on1f", [1, 128], F32, kind="ExternalInput").ap()
    onecb = nc.dram_tensor("onecb", [128, 1], BF16, kind="ExternalInput").ap()
    vG = nc.dram_tensor("vG", [128, 32], BF16, kind="ExternalInput").ap()
    vRb = nc.dram_tensor("vRb", [32, 128], BF16, kind="ExternalInput").ap()
    vRf = nc.dram_tensor("vRf", [32, 128], F32, kind="ExternalInput").ap()
    hmask = nc.dram_tensor("hmask", [128, 1], F32, kind="ExternalInput").ap()

    o_out = nc.dram_tensor("o_out", [T, D], F32, kind="ExternalOutput").ap()

    d_xfm = nc.dram_tensor("d_xfm", [128, NCD * TH], F32).ap()
    d_S = nc.dram_tensor("d_S", [128, NCD * T], F32).ap()
    d_x2 = nc.dram_tensor("d_x2", [128, NCD * T], F32).ap()
    d_wqp = nc.dram_tensor("d_wqp", [HF, D], BF16).ap()
    cc_in_t = nc.dram_tensor("cc_in", [2, D], F32)
    cc_out_t = nc.dram_tensor("cc_out", [2 * NC, D], F32)

    OFF = dict(n1w=0, rb=16, cw0=32, cw1=48, cw2=64, cw3=80, cb=96, gab=112,
               gxb=128, apar=144, lb=160, ob=176, n2w=192, dnb=208,
               u1b=224, u2b=272)

    with ExitStack() as top:
        b = B(nc)
        b.mksem(top, "dw", "dh", "pe", "dv", "ac", "cc")
        b.mkdma(top, "dc", "w0", "w1", "w2", "w3", "pw0", "pw1", "pq0", "pq1",
                "ps0", "ps1", "xt0", "xt1", "xh", "sx", "ce", "cg",
                "xr0", "xr1", "ss0", "ss1", "sr0", "sr1", "xs0", "xs1", "os")

        def sbt(name, shape, dt, ctx=top, side="left"):
            return ctx.enter_context(nc.sbuf_tensor(name, shape, dt, side=side))

        s_bias = sbt("s_bias", [128, 320], F32)
        s_c4 = sbt("s_c4", [128, 16], F32)
        s_gab2 = sbt("s_gab2", [128, 16], F32)
        s_gxb2 = sbt("s_gxb2", [128, 16], F32)
        s_mk = sbt("s_mk", [128, NC], F32)
        s_mk1 = sbt("s_mk1", [128, NC], F32)
        s_idf = sbt("s_idf", [128, 128], F32)
        s_idb = sbt("s_idb", [128, 128], BF16)
        s_id8 = sbt("s_id8", [8, 8], F32)
        s_on1 = sbt("s_on1", [1, 128], F32)
        s_onec = sbt("s_onec", [128, 1], BF16)
        s_vG = sbt("s_vG", [128, 32], BF16)
        s_vRb = sbt("s_vRb", [32, 128], BF16)
        s_vRf = sbt("s_vRf", [32, 128], F32)
        s_r0 = sbt("s_r0", [1, TH], F32)
        s_r1 = sbt("s_r1", [1, TH], F32)
        s_bc = sbt("s_bc", [128, TH], F32)
        s_end = sbt("s_end", [128, 32], F32)
        t16a = sbt("t16a", [128, 16], F32)
        t16b = sbt("t16b", [128, 16], F32)
        s_hm = sbt("s_hm", [128, 1], F32)
        s_rbh = sbt("s_rbh", [128, 16], F32)
        s_padv = sbt("s_padv", [1, 8], F32)
        s_pads = sbt("s_pads", [1, 8], F32)
        b.pad_cells = {"vector": s_padv[0:1, 0:1], "scalar": s_pads[0:1, 0:1]}
        b.emit("vector", lambda e: e.memset(s_padv[:], 0.0), inc="dv")
        _pv = b.emit("vector", lambda e: e.memset(s_pads[:], 0.0), inc="dv")
        b.emit("scalar", None, waits=[("dv", _pv)])
        wb = [DBuf(sbt(f"wb{i}", [128, 2048], BF16)) for i in range(4)]
        for _i, _bf in enumerate(wb):
            _bf.sem = f"w{_i}"
        c_wb = sbt("c_wb", [128, 128], F32)
        pA = top.enter_context(nc.psum_tensor("pA", [128, 1024], F32))
        pB = top.enter_context(nc.psum_tensor("pB", [128, 1024], F32))
        pC = top.enter_context(nc.psum_tensor("pC", [128, 1024], F32))
        pD = top.enter_context(nc.psum_tensor("pD", [128, 1024], F32))

        def bias(col, m):
            return s_bias[:, OFF[col] + m:OFF[col] + m + 1]

        def dump(name, ap_fn):
            if name not in DUMPS:
                return
            shp, ap = ap_fn()
            dt_ = nc.dram_tensor("dump_" + name, shp, ap.dtype, kind="ExternalOutput").ap()
            v = b.emit("sync", lambda e, ap=ap, dt_=dt_: e.dma_start(out=dt_, in_=ap),
                       waits=[("pe", b.cur("pe")), ("dv", b.cur("dv")), ("ac", b.cur("ac"))],
                       inc="dh", k=16)
            for e in ("tensor", "vector", "scalar"):
                b.emit(e, None, waits=[("dh", v)])

        def stream_mm(wdram, nM, nK, rhs_of, windows, slots, evac, wpair,
                      extra=None, interleave=None):
            for m in range(nM):
                buf = wb[wpair[m % 2]]
                w3 = buf.t[:].rearrange("p (k j) -> p k j", k=nK)[:, 0:nK, :]
                lv = b.emit("gpsimd",
                            lambda e, m=m, w3=w3: e.dma_start(
                                out=w3, in_=wdram[:, :, m * 128:(m + 1) * 128]),
                            waits=([buf.free] if buf.free else []),
                            inc=buf.sem, k=16)
                buf.loaded = lv
                slot = slots[m % len(slots)]
                waits = [(buf.sem, lv)] + slot.wait_free()
                gv = None
                for k in range(nK):
                    first = (k == 0)
                    last = (k == nK - 1)
                    for wi, (pcol, width) in enumerate(windows):
                        rhs = rhs_of(k, wi)
                        ap = slot.ap[:, pcol:pcol + width]
                        gv = b.emit("tensor",
                                    lambda e, ap=ap, buf=buf, k=k, rhs=rhs, first=first, last=last:
                                    e.matmul(ap, buf.t[:, k * 128:(k + 1) * 128], rhs,
                                             start=first, stop=last),
                                    waits=(waits if (k == 0 and wi == 0) else []),
                                    inc=("pe" if (last and wi == len(windows) - 1) else None))
                    if extra is not None:
                        gv = extra(m, k, buf, first, last) or gv
                buf.free = ("pe", gv)
                evac(m, slot, gv)
                if interleave is not None:
                    interleave(m)

        ldh = []
        for src, dst in [(biasq, s_bias), (maskv, s_mk), (mask1m, s_mk1),
                         (id128f, s_idf), (id128b, s_idb), (id8f, s_id8),
                         (on1f, s_on1), (onecb, s_onec), (vG, s_vG),
                         (vRb, s_vRb), (vRf, s_vRf), (hmask, s_hm)]:
            ldh.append(b.emit("sync", lambda e, src=src, dst=dst:
                              e.dma_start(out=dst[:], in_=src), inc="dc", k=16))
        w0 = [("dc", ldh[-1])]
        b.emit("tensor", None, waits=w0)
        b.emit("gpsimd", None, waits=w0)
        ap_col = s_bias[:, OFF["apar"]:OFF["apar"] + 16]
        b.emit("scalar", lambda e: e.activation(t16a[:], ap_col, AF.Abs), waits=w0, inc="ac")
        b.emit("scalar", lambda e: e.activation(t16a[:], t16a[:], AF.Exp, scale=np.float32(-1.0)),
               waits=[("ac", b.cur("ac"))], inc="ac")
        b.emit("scalar", lambda e: e.activation(t16a[:], t16a[:], AF.Ln, bias=np.float32(1.0)),
               waits=[("ac", b.cur("ac"))], inc="ac")
        b.emit("scalar", lambda e: e.activation(t16b[:], ap_col, AF.Relu), waits=w0, inc="ac")
        vac = b.cur("ac")
        b.emit("vector", lambda e: e.tensor_tensor(t16a[:], t16a[:], t16b[:], OP.add),
               waits=[("ac", vac)], inc="dv")
        b.emit("vector", lambda e: e.tensor_scalar(s_c4[:], t16a[:], -4.0, None, OP.mult),
               waits=[("dv", b.cur("dv"))], inc="dv")
        b.emit("vector", lambda e: e.tensor_scalar(
            s_gab2[:], s_bias[:, OFF["gab"]:OFF["gab"] + 16], 0.5, None, OP.mult),
            waits=w0, inc="dv")
        b.emit("vector", lambda e: e.tensor_scalar(
            s_gxb2[:], s_bias[:, OFF["gxb"]:OFF["gxb"] + 16], 0.5, None, OP.mult),
            waits=w0, inc="dv")

        b.emit("vector", lambda e: e.tensor_scalar(
            s_rbh[:], s_bias[:, OFF["rb"]:OFF["rb"] + 16], s_hm[:, 0:1], None, OP.mult),
            waits=w0, inc="dv")

        b.emit("vector", lambda e: e.memset(c_wb[:], 777.0), inc="dv")

        slA = Slot(pA[:])
        slB = Slot(pB[:])
        slD = Slot(pD[:])

        sc_pp = ExitStack()
        ppw = [DBuf(sbt(f"ppw{i}", [128, 2048], BF16, ctx=sc_pp)) for i in range(2)]
        for _i, _bf in enumerate(ppw):
            _bf.sem = f"pw{_i}"
        ppa1 = sbt("ppa1", [128, 2048], BF16, ctx=sc_pp)
        c_pp = sbt("c_pp", [128, 128], F32, ctx=sc_pp)
        b.emit("vector", lambda e: e.memset(c_pp[:], 777.0), inc="dv")
        pp_state = {"n": 0, "pe": None, "afree": None}

        def prepass_slab():
            s = pp_state["n"]
            if s >= 64:
                return
            pp_state["n"] = s + 1
            pb = ppw[s % 2]
            lv = b.emit("gpsimd", lambda e, s=s, pb=pb: e.dma_start(
                out=pb.t[:], in_=wPJ[s * 128:(s + 1) * 128, :]),
                waits=([pb.free] if pb.free else []), inc=pb.sem, k=16)
            av = b.emit("vector", lambda e, pb=pb: e.tensor_scalar(
                ppa1[:].bitcast(U16), pb.t[:].bitcast(U16), 0x7FFF, None,
                OP.bitwise_and),
                waits=[(pb.sem, lv)] + ([pp_state["afree"]] if pp_state["afree"] else []),
                inc="dv")
            pb.free = ("dv", av)
            first = (s == 0)
            last = (s == 63)
            gv = None
            for wi, (pt, pcol) in enumerate([(pC, 0), (pC, 512), (pD, 0), (pD, 512)]):
                gv = b.emit("tensor", lambda e, pt=pt, pcol=pcol, wi=wi, first=first, last=last:
                            e.matmul(pt[:, pcol:pcol + 512], s_idb[:],
                                     ppa1[:, wi * 512:(wi + 1) * 512],
                                     start=first, stop=last),
                            waits=[("dv", av)] if wi == 0 else [],
                            inc="pe" if wi == 3 else None)
            pp_state["pe"] = gv
            pp_state["afree"] = ("pe", gv)

        sc_xn = ExitStack()
        xn = sc_xn.enter_context(nc.sbuf_tensor("xn", [128, NCD * TH], BF16, side="left"))
        sc1 = ExitStack()
        x_fm = sc1.enter_context(nc.sbuf_tensor("x_fm", [128, NCD * TH], F32, side="right"))
        sc1b = ExitStack()
        sq1 = sc1b.enter_context(nc.sbuf_tensor("sq1", [128, NCD * TH], BF16, side="right"))
        xt0 = sc1b.enter_context(nc.sbuf_tensor("xt0", [128, 2048], F32, side="right"))
        xt1 = sc1b.enter_context(nc.sbuf_tensor("xt1", [128, 2048], F32, side="right"))
        xh = sc1b.enter_context(nc.sbuf_tensor("xh", [8, 2048], F32, side="right"))

        x3 = x_fm[:].rearrange("p (c t) -> p c t", c=NCD)
        xtb = [DBuf(xt0), DBuf(xt1)]
        xtb[0].sem, xtb[1].sem = "xt0", "xt1"
        for j in range(8):
            tb = xtb[j % 2]
            lv = b.emit("sync", lambda e, j=j, tb=tb: e.dma_start(
                out=tb.t[:], in_=x_in[HALO + j * 128: HALO + (j + 1) * 128, :]),
                waits=([tb.free] if tb.free else []), inc=tb.sem, k=16)
            for g in range(2):
                slot = slA if (2 * j + g) % 2 == 0 else slB
                gv = None
                for c8 in range(8):
                    c = g * 8 + c8
                    gv = b.emit("tensor", lambda e, slot=slot, c8=c8, tb=tb, c=c:
                                e.matmul(slot.ap[:, c8 * 128:(c8 + 1) * 128],
                                         tb.t[:, c * 128:(c + 1) * 128], s_idf[:],
                                         start=True, stop=True),
                                waits=([(tb.sem, lv)] + slot.wait_free()) if c8 == 0 else [],
                                inc="pe" if c8 == 7 else None)
                dst = x3[:, g * 8:(g + 1) * 8, HALO + j * 128: HALO + (j + 1) * 128]
                fv = b.emit("scalar", lambda e, slot=slot, dst=dst: e.copy(
                    dst, slot.ap[:].rearrange("p (c t) -> p c t", c=8)),
                    waits=[("pe", gv)], inc="ac", pad=True)
                slot.free = ("ac", fv)
            tb.free = ("pe", b.cur("pe"))
        lvh = b.emit("sync", lambda e: e.dma_start(out=xh[:], in_=x_in[0:HALO, :]),
                     inc="xh", k=16)
        slC0 = Slot(pC[:, 0:128])
        gv = None
        for c in range(NCD):
            gv = b.emit("tensor", lambda e, c=c: e.matmul(
                pC[:, c * 8:(c + 1) * 8], xh[0:8, c * 128:(c + 1) * 128],
                s_id8[0:8, 0:8], start=True, stop=True),
                waits=([("xh", lvh)] + slC0.wait_free()) if c == 0 else [],
                inc="pe" if c == NCD - 1 else None)
        fv = b.emit("vector", lambda e: e.tensor_copy(
            x3[:, :, 0:HALO], pC[:, 0:128].rearrange("p (c t) -> p c t", c=NCD)),
            waits=[("pe", gv)], inc="dv", pad=True)
        slC0.free = ("dv", fv)
        acp = b.pad("scalar")
        sp_x = b.emit("sync", lambda e: e.dma_start(out=d_xfm, in_=x_fm[:]),
                      waits=[("ac", acp), ("dv", fv)], inc="sx", k=16)
        sqv = b.emit("scalar", lambda e: e.activation(sq1[:], x_fm[:], AF.Square),
                     waits=[("dv", fv)], inc="ac", pad=True)
        gv = None
        for c in range(NCD):
            first = (c == 0)
            last = (c == NCD - 1)
            gv = b.emit("tensor", lambda e, c=c, first=first, last=last: e.matmul(
                pD[0:1, 0:512], s_onec[:], sq1[:, c * TH: c * TH + 512],
                start=first, stop=last),
                waits=([("ac", sqv)] + slD.wait_free()) if first else [], inc=None)
            gv = b.emit("tensor", lambda e, c=c, first=first, last=last: e.matmul(
                pD[0:1, 512:1024], s_onec[:], sq1[:, c * TH + 512: c * TH + 1024],
                start=first, stop=last), inc=None)
            gv = b.emit("tensor", lambda e, c=c, first=first, last=last: e.matmul(
                pC[0:1, 128:136], s_onec[:], sq1[:, c * TH + 1024: c * TH + 1032],
                start=first, stop=last), inc="pe" if last else None)
        b.emit("scalar", lambda e: e.activation(s_r0[0:1, 0:1024], pD[0:1, 0:1024],
                                                AF.Ln, bias=np.float32(1e-8), scale=np.float32(1.0 / D)),
               waits=[("pe", gv)], inc="ac")
        b.emit("scalar", lambda e: e.activation(s_r0[0:1, 1024:1032], pC[0:1, 128:136],
                                                AF.Ln, bias=np.float32(1e-8), scale=np.float32(1.0 / D)),
               inc="ac")
        rv = b.emit("scalar", lambda e: e.activation(s_r1[0:1, :], s_r0[0:1, :],
                                                     AF.Exp, scale=np.float32(-0.5)),
                    waits=[("ac", b.cur("ac"))], inc="ac", pad=True)
        slD.free = ("ac", rv)
        slC0.free = ("ac", rv)
        gv = None
        for (pcol, width) in [(0, 512), (512, 512)]:
            gv = b.emit("tensor", lambda e, pcol=pcol, width=width: e.matmul(
                pD[:, pcol:pcol + width], s_on1[:], s_r1[0:1, pcol:pcol + width],
                start=True, stop=True),
                waits=([("ac", rv)] + slD.wait_free()) if pcol == 0 else [],
                inc=None)
        gv = b.emit("tensor", lambda e: e.matmul(
            pC[:, 128:136], s_on1[:], s_r1[0:1, 1024:1032], start=True, stop=True),
            waits=slC0.wait_free(), inc="pe")
        bv1 = b.emit("vector", lambda e: e.tensor_copy(s_bc[:, 0:1024], pD[:]),
                     waits=[("pe", gv)], inc="dv")
        bv2 = b.emit("vector", lambda e: e.tensor_copy(s_bc[:, 1024:1032], pC[:, 128:136]),
                     inc="dv", pad=True)
        slD.free = ("dv", bv2)
        slC0.free = ("dv", bv2)
        for c in range(NCD):
            b.emit("vector", lambda e, c=c: e.scalar_tensor_tensor(
                xn[:, c * TH:(c + 1) * TH], x_fm[:, c * TH:(c + 1) * TH],
                bias("n1w", c), s_bc[:], OP.mult, OP.mult),
                waits=[("dv", bv2)] if c == 0 else [], inc="dv")
        b.barrier()
        dump("xn", lambda: ([128, NCD * TH], xn[:]))
        b.dma_fence()
        sc1b.close()

        rp = x_fm
        wR3 = wR.rearrange("(k p) m -> p k m", p=128)

        def rhs_xn(k, wi):
            base = k * TH
            if wi == 0:
                return xn[:, base + HALO: base + HALO + 512]
            return xn[:, base + HALO + 512: base + TH]

        halo_state = {"free": None}

        def extra_halo(m, k, buf, first, last):
            w = ([halo_state["free"]] if (first and halo_state["free"]) else [])
            return b.emit("tensor", lambda e, m=m, k=k, buf=buf, first=first, last=last:
                          e.matmul(pC[:, m * 8:(m + 1) * 8],
                                   buf.t[:, k * 128:(k + 1) * 128],
                                   xn[:, k * TH: k * TH + 8],
                                   start=first, stop=last),
                          waits=w, inc="pe" if last else None)

        def evac_rp(m, slot, gv):
            wt = [("pe", gv), ("sx", sp_x)]
            v1 = b.emit("scalar", lambda e, m=m, slot=slot: e.activation(
                rp[:, m * TH + HALO:(m + 1) * TH], slot.ap[:], AF.Identity,
                bias=bias("rb", m)), waits=wt, inc="ac")
            v2 = b.emit("scalar", lambda e, m=m: e.activation(
                rp[:, m * TH: m * TH + 8], pC[:, m * 8:(m + 1) * 8], AF.Identity,
                bias=s_rbh[:, m:m + 1]), inc="ac")
            pv = b.pad("scalar")
            slot.free = ("ac", pv)
            halo_state["free"] = ("ac", pv)

        stream_mm(wR3, NCD, NCD, rhs_xn, [(0, 512), (512, 512)],
                  [slA, slB], evac_rp, (0, 1), extra=extra_halo)
        b.barrier()
        b.dma_fence()
        dump("rp", lambda: ([128, NCD * TH], rp[:]))

        conv = sc_xn.enter_context(nc.sbuf_tensor("conv", [128, NCD * T], BF16, side="left"))
        sc_ct = ExitStack()
        cta = sc_ct.enter_context(nc.sbuf_tensor("cta", [128, T], F32, side="left"))
        ctb = sc_ct.enter_context(nc.sbuf_tensor("ctb", [128, T], F32, side="left"))
        for c in range(NCD):
            base = c * TH
            b.emit("vector", lambda e, c=c, base=base: e.tensor_scalar(
                cta[:], rp[:, base + 5: base + 5 + T], bias("cw0", c), bias("cb", c),
                OP.mult, OP.add), inc="dv")
            b.emit("vector", lambda e, c=c, base=base: e.scalar_tensor_tensor(
                ctb[:], rp[:, base + 6: base + 6 + T], bias("cw1", c), cta[:],
                OP.mult, OP.add), inc="dv")
            b.emit("vector", lambda e, c=c, base=base: e.scalar_tensor_tensor(
                cta[:], rp[:, base + 7: base + 7 + T], bias("cw2", c), ctb[:],
                OP.mult, OP.add), inc="dv")
            b.emit("vector", lambda e, c=c, base=base: e.scalar_tensor_tensor(
                conv[:, c * T:(c + 1) * T], rp[:, base + 8: base + 8 + T],
                bias("cw3", c), cta[:], OP.mult, OP.add), inc="dv")
        b.barrier()
        dump("conv", lambda: ([128, NCD * T], conv[:]))
        b.dma_fence()
        sc_ct.close()
        sc1.close()

        scH = ExitStack()
        h_bf = scH.enter_context(nc.sbuf_tensor("h_bf", [128, NCD * T], BF16, side="right"))
        sc5 = ExitStack()
        P_bf = sc5.enter_context(nc.sbuf_tensor("P_bf", [128, NCD * T], BF16, side="right"))
        a_st = [sc5.enter_context(nc.sbuf_tensor(f"a_st{i}", [128, T], F32, side="right"))
                for i in range(2)]
        y_st = [sc5.enter_context(nc.sbuf_tensor(f"y_st{i}", [128, T], F32, side="right"))
                for i in range(2)]
        i_st = [sc5.enter_context(nc.sbuf_tensor(f"i_st{i}", [128, T], F32, side="right"))
                for i in range(2)]
        m_st = sc5.enter_context(nc.sbuf_tensor("m_st", [128, T], F32, side="right"))
        tmp5 = sc5.enter_context(nc.sbuf_tensor("tmp5", [128, T], F32, side="right"))

        wGA3 = wGA.rearrange("(k p) m -> p k m", p=128)
        wGX3 = wGX.rearrange("(k p) m -> p k m", p=128)
        tmp5_rd = {"v": None}
        for g0 in range(0, NCD, 2):
            grp = [g0, g0 + 1]
            chain = {}
            for ci, c in enumerate(grp):
                for (wmat, pair, slot) in [(wGA3, (0, 1), slA), (wGX3, (2, 3), slB)]:
                    buf = wb[pair[c % 2]]
                    w3 = buf.t[:].rearrange("p (k j) -> p k j", k=NCD)
                    lv = b.emit("gpsimd", lambda e, wmat=wmat, c=c, w3=w3: e.dma_start(
                        out=w3, in_=wmat[:, :, c * 128:(c + 1) * 128]),
                        waits=([buf.free] if buf.free else []), inc=buf.sem, k=16)
                    gv = None
                    for k in range(NCD):
                        for wi, (pcol, t0) in enumerate([(0, 0), (512, 512)]):
                            gv = b.emit("tensor", lambda e, slot=slot, pcol=pcol, buf=buf, k=k, t0=t0:
                                        e.matmul(slot.ap[:, pcol:pcol + 512],
                                                 buf.t[:, k * 128:(k + 1) * 128],
                                                 conv[:, k * T + t0: k * T + t0 + 512],
                                                 start=(k == 0), stop=(k == NCD - 1)),
                                        waits=([(buf.sem, lv)] + slot.wait_free())
                                        if (k == 0 and wi == 0) else [],
                                        inc="pe" if (k == NCD - 1 and wi == 1) else None)
                    buf.free = ("pe", gv)
                    chain[("mm", c, slot is slB)] = gv
                v = b.emit("scalar", lambda e, c=c: e.activation(
                    tmp5[:], slA.ap[:], AF.Tanh, bias=s_gab2[:, c:c + 1], scale=np.float32(0.5)),
                    waits=[("pe", chain[("mm", c, False)])] +
                    ([("dv", tmp5_rd["v"])] if tmp5_rd["v"] else []), inc="ac", pad=True)
                slA.free = ("ac", v)
                v2 = b.emit("vector", lambda e, c=c: e.tensor_scalar(
                    m_st[:], tmp5[:], s_c4[:, c:c + 1], s_c4[:, c:c + 1], OP.mult, OP.add),
                    waits=[("ac", v)], inc="dv", pad=True)
                v3 = b.emit("scalar", lambda e, ci=ci: e.activation(
                    a_st[ci][:], m_st[:], AF.Exp), waits=[("dv", v2)], inc="ac")
                v4 = b.emit("scalar", lambda e, ci=ci: e.activation(
                    tmp5[:], a_st[ci][:], AF.Square), waits=[("ac", v3)], inc="ac", pad=True)
                v5 = b.emit("vector", lambda e, ci=ci: e.tensor_scalar(
                    y_st[ci][:], tmp5[:], -1.0, 1.0, OP.mult, OP.add),
                    waits=[("ac", v4)], inc="dv")
                tmp5_rd["v"] = v5
                v6 = b.emit("scalar", lambda e, c=c, ci=ci: e.activation(
                    i_st[ci][:], slB.ap[:], AF.Tanh, bias=s_gxb2[:, c:c + 1], scale=np.float32(0.5)),
                    waits=[("pe", chain[("mm", c, True)])], inc="ac", pad=True)
                slB.free = ("ac", v6)
            for ci, c in enumerate(grp):
                v6b = b.emit("vector", lambda e, ci=ci: e.tensor_scalar(
                    m_st[:], y_st[ci][:], 0.0, None, OP.max),
                    waits=[("dv", b.cur("dv")), ("ac", b.cur("ac"))], inc="dv")
                v7 = b.emit("scalar", lambda e, ci=ci: e.activation(
                    y_st[ci][:], m_st[:], AF.Sqrt),
                    waits=[("dv", v6b)], inc="ac", pad=True)
                v8 = b.emit("vector", lambda e, ci=ci: e.tensor_tensor(
                    m_st[:], y_st[ci][:], i_st[ci][:], OP.mult),
                    waits=[("ac", v7)], inc="dv")
                v9 = b.emit("vector", lambda e, ci=ci: e.tensor_tensor(
                    i_st[ci][:], y_st[ci][:], m_st[:], OP.add),
                    waits=[("dv", v8)], inc="dv")
                v10 = b.emit("vector", lambda e, c=c, ci=ci: e.scalar_tensor_tensor(
                    tmp5[:], i_st[ci][:], 0.5, conv[:, c * T:(c + 1) * T],
                    OP.mult, OP.mult), waits=[("dv", v9)], inc="dv")
                v11 = b.emit("vector", lambda e, c=c, ci=ci: e.tensor_tensor_scan(
                    h_bf[:, c * T:(c + 1) * T], a_st[ci][:], tmp5[:], 0.0,
                    OP.mult, OP.add), waits=[("dv", v10)], inc="dv")
                tmp5_rd["v"] = v11
                v12 = b.emit("vector", lambda e, c=c, ci=ci: e.tensor_tensor_scan(
                    P_bf[:, c * T:(c + 1) * T], a_st[ci][:], a_st[ci][:], 1.0,
                    OP.mult, OP.bypass), waits=[("dv", v11)], inc="dv")
                b.emit("vector", lambda e, c=c: e.tensor_copy(
                    s_end[:, c:c + 1], P_bf[:, (c + 1) * T - 1:(c + 1) * T]),
                    waits=[("dv", v12)], inc="dv")
                b.emit("vector", lambda e, c=c: e.tensor_copy(
                    s_end[:, 16 + c:16 + c + 1], h_bf[:, (c + 1) * T - 1:(c + 1) * T]),
                    inc="dv")
        cin3 = cc_in_t.ap().rearrange("a (p c) -> a p c", p=128)
        dvp = b.pad("vector")
        e1 = b.emit("sync", lambda e: e.dma_start(out=cin3[0], in_=s_end[:, 0:16]),
                    waits=[("dv", dvp)], inc="ce", k=16)
        e2 = b.emit("sync", lambda e: e.dma_start(out=cin3[1], in_=s_end[:, 16:32]),
                    inc="ce", k=16)
        ccv = b.emit("gpsimd", lambda e: e.collective_compute(
            "AllGather", OP.bypass, replica_groups=[list(range(NC))],
            ins=[cc_in_t.ap().opt()], outs=[cc_out_t.ap().opt()]),
            waits=[("ce", e2)], inc="cc", k=None)
        dump("h_loc", lambda: ([128, NCD * T], h_bf[:]))
        dump("P_loc", lambda: ([128, NCD * T], P_bf[:]))
        for _ in range(10):
            prepass_slab()

        sc5c = ExitStack()
        s_gath = sc5c.enter_context(nc.sbuf_tensor("s_gath", [128, 2 * NC * 16], F32, side="right"))
        s_hin = sc5c.enter_context(nc.sbuf_tensor("s_hin", [128, 16], F32, side="right"))
        s_t0 = sc5c.enter_context(nc.sbuf_tensor("s_t0", [128, 16], F32, side="right"))
        s_t1 = sc5c.enter_context(nc.sbuf_tensor("s_t1", [128, 16], F32, side="right"))
        gat3 = cc_out_t.ap().rearrange("a (p c) -> a p c", p=128)
        gl = None
        for j in range(2 * NC):
            gl = b.emit("sync", lambda e, j=j: e.dma_start(
                out=s_gath[:, j * 16:(j + 1) * 16], in_=gat3[j]),
                waits=[("cc", 1)] if j == 0 else [], inc="cg", k=16)
        hv = b.emit("vector", lambda e: e.memset(s_hin[:], 0.0), inc="dv")
        for r in range(NC - 1):
            pa = slice((2 * r) * 16, (2 * r + 1) * 16)
            ha = slice((2 * r + 1) * 16, (2 * r + 2) * 16)
            b.emit("vector", lambda e, pa=pa, r=r: e.tensor_scalar(
                s_t0[:], s_gath[:, pa], s_mk[:, r:r + 1], s_mk1[:, r:r + 1],
                OP.mult, OP.add), waits=[("cg", gl), ("dv", hv)], inc="dv")
            b.emit("vector", lambda e, ha=ha, r=r: e.tensor_scalar(
                s_t1[:], s_gath[:, ha], s_mk[:, r:r + 1], None, OP.mult), inc="dv")
            b.emit("vector", lambda e: e.tensor_tensor(s_hin[:], s_t0[:], s_hin[:], OP.mult),
                   inc="dv")
            hv = b.emit("vector", lambda e: e.tensor_tensor(s_hin[:], s_hin[:], s_t1[:], OP.add),
                        inc="dv")
        for c in range(NCD):
            b.emit("vector", lambda e, c=c: e.scalar_tensor_tensor(
                h_bf[:, c * T:(c + 1) * T], P_bf[:, c * T:(c + 1) * T],
                s_hin[:, c:c + 1], h_bf[:, c * T:(c + 1) * T], OP.mult, OP.add),
                waits=[("dv", hv)], inc="dv")
        b.barrier()
        dump("h_fix", lambda: ([128, NCD * T], h_bf[:]))
        b.dma_fence()
        sc5c.close()
        sc5.close()

        sc4 = ExitStack()
        left = sc4.enter_context(nc.sbuf_tensor("left", [128, NCD * T], BF16, side="right"))
        wL3 = wL.rearrange("(k p) m -> p k m", p=128)

        def evac_left(m, slot, gv):
            v = b.emit("scalar", lambda e, m=m, slot=slot: e.activation(
                left[:, m * T:(m + 1) * T], slot.ap[:], AF.Gelu, bias=bias("lb", m)),
                waits=[("pe", gv)], inc="ac")
            slot.free = ("ac", b.pad("scalar"))

        stream_mm(wL3, NCD, NCD, rhs_xn, [(0, 512), (512, 512)],
                  [slA, slB], evac_left, (0, 1),
                  interleave=lambda m: prepass_slab())
        b.barrier()
        dump("left", lambda: ([128, NCD * T], left[:]))
        b.dma_fence()
        sc_xn.close()

        for c in range(NCD):
            b.emit("vector", lambda e, c=c: e.tensor_tensor(
                h_bf[:, c * T:(c + 1) * T], left[:, c * T:(c + 1) * T],
                h_bf[:, c * T:(c + 1) * T], OP.mult), inc="dv")
        b.barrier()

        sc7 = ExitStack()
        x1 = sc7.enter_context(nc.sbuf_tensor("x1", [128, NCD * T], F32, side="left"))
        xre = [DBuf(sc7.enter_context(nc.sbuf_tensor(f"xre{i}", [128, T], F32, side="left")))
               for i in range(2)]
        xre[0].sem, xre[1].sem = "xr0", "xr1"
        sst = [sc7.enter_context(nc.sbuf_tensor(f"sst{i}", [128, T], F32, side="left"))
               for i in range(2)]
        sst_free = [None, None]
        wO3 = wO.rearrange("(k p) m -> p k m", p=128)

        def rhs_lr(k, wi):
            return h_bf[:, k * T + wi * 512: k * T + wi * 512 + 512]

        def evac_x1(m, slot, gv):
            xb = xre[m % 2]
            lv = b.emit("sync", lambda e, m=m, xb=xb: e.dma_start(
                out=xb.t[:], in_=d_xfm[:, m * TH + HALO:(m + 1) * TH]),
                waits=([xb.free] if xb.free else []), inc=xb.sem, k=16)
            v1 = b.emit("vector", lambda e, m=m, slot=slot, xb=xb: e.scalar_tensor_tensor(
                x1[:, m * T:(m + 1) * T], slot.ap[:], bias("ob", m), xb.t[:],
                OP.add, OP.add), waits=[("pe", gv), (xb.sem, lv)], inc="dv")
            slot.free = ("dv", b.pad("vector"))
            sbuf = sst[m % 2]
            wt = [("dv", v1)]
            if sst_free[m % 2]:
                wt.append(sst_free[m % 2])
            v2 = b.emit("vector", lambda e, m=m, sbuf=sbuf, xb=xb: e.tensor_tensor(
                sbuf[:], x1[:, m * T:(m + 1) * T], xb.t[:], OP.add),
                waits=wt, inc="dv", pad=True)
            xb.free = ("dv", v2)
            sv = b.emit("sync", lambda e, m=m, sbuf=sbuf: e.dma_start(
                out=d_S[:, m * T:(m + 1) * T], in_=sbuf[:]),
                waits=[("dv", v2)], inc=f"ss{m % 2}", k=16)
            sst_free[m % 2] = (f"ss{m % 2}", sv)

        stream_mm(wO3, NCD, NCD, rhs_lr, [(0, 512), (512, 512)],
                  [slA, slB], evac_x1, (0, 1),
                  interleave=lambda m: prepass_slab())
        while pp_state["n"] < 64:
            prepass_slab()
        b.barrier()
        dump("x1", lambda: ([128, NCD * T], x1[:]))
        b.dma_fence()
        sc4.close()
        scH.close()

        sc8 = ExitStack()
        sq8 = sc8.enter_context(nc.sbuf_tensor("sq8", [128, NCD * T], BF16, side="left"))
        sc_x1n = ExitStack()
        x1n = sc_x1n.enter_context(nc.sbuf_tensor("x1n", [128, NCD * T], BF16, side="right"))
        sqv = b.emit("scalar", lambda e: e.activation(sq8[:], x1[:], AF.Square),
                     waits=[("dv", b.cur("dv"))], inc="ac", pad=True)
        gv = None
        for c in range(NCD):
            first = (c == 0)
            last = (c == NCD - 1)
            b.emit("tensor", lambda e, c=c, first=first, last=last: e.matmul(
                pA[0:1, 0:512], s_onec[:], sq8[:, c * T: c * T + 512],
                start=first, stop=last),
                waits=([("ac", sqv)] + slA.wait_free()) if first else [], inc=None)
            gv = b.emit("tensor", lambda e, c=c, first=first, last=last: e.matmul(
                pA[0:1, 512:1024], s_onec[:], sq8[:, c * T + 512: c * T + 1024],
                start=first, stop=last), inc="pe" if last else None)
        b.emit("scalar", lambda e: e.activation(s_r0[0:1, 0:1024], pA[0:1, 0:1024],
                                                AF.Ln, bias=np.float32(1e-8), scale=np.float32(1.0 / D)),
               waits=[("pe", gv)], inc="ac")
        rv = b.emit("scalar", lambda e: e.activation(s_r1[0:1, 0:1024], s_r0[0:1, 0:1024],
                                                     AF.Exp, scale=np.float32(-0.5)),
                    waits=[("ac", b.cur("ac"))], inc="ac", pad=True)
        slA.free = ("ac", rv)
        gv = None
        for (pcol, width) in [(0, 512), (512, 512)]:
            gv = b.emit("tensor", lambda e, pcol=pcol, width=width: e.matmul(
                pB[:, pcol:pcol + width], s_on1[:], s_r1[0:1, pcol:pcol + width],
                start=True, stop=True),
                waits=([("ac", rv)] + slB.wait_free()) if pcol == 0 else [],
                inc="pe" if pcol == 512 else None)
        bv = b.emit("vector", lambda e: e.tensor_copy(s_bc[:, 0:1024], pB[:]),
                    waits=[("pe", gv)], inc="dv", pad=True)
        slB.free = ("dv", bv)
        for c in range(NCD):
            b.emit("vector", lambda e, c=c: e.scalar_tensor_tensor(
                x1n[:, c * T:(c + 1) * T], x1[:, c * T:(c + 1) * T],
                bias("n2w", c), s_bc[:, 0:1024], OP.mult, OP.mult),
                waits=[("dv", bv)] if c == 0 else [], inc="dv")
        b.barrier()
        dump("x1n", lambda: ([128, NCD * T], x1n[:]))
        dump("c_pp", lambda: ([128, 128], c_pp[:]))
        b.dma_fence()
        sc8.close()
        sc7.close()
        sc_pp.close()

        sc_prod = ExitStack()
        prod = sc_prod.enter_context(nc.sbuf_tensor("prod", [128, NCF * T], BF16, side="left"))
        sc_ps = ExitStack()
        s_pjs = sc_ps.enter_context(nc.sbuf_tensor("s_pjs", [128, 2048], BF16, side="right"))
        s_pjr = sc_ps.enter_context(nc.sbuf_tensor("s_pjr", [128, 2048], F32, side="right"))
        sc9 = ExitStack()
        u1s = [sc9.enter_context(nc.sbuf_tensor(f"u1s{i}", [128, T], BF16, side="right"))
               for i in range(2)]
        u2s = [sc9.enter_context(nc.sbuf_tensor(f"u2s{i}", [128, T], BF16, side="right"))
               for i in range(2)]
        pqw = [DBuf(sc9.enter_context(nc.sbuf_tensor(f"pqw{i}", [128, 2048], BF16, side="right")))
               for i in range(2)]
        pqw[0].sem, pqw[1].sem = "pq0", "pq1"
        pqt = [sc9.enter_context(nc.sbuf_tensor(f"pqt{i}", [128, 2048], BF16, side="right"))
               for i in range(2)]

        fv1 = b.emit("vector", lambda e: e.tensor_scalar(
            s_pjs[:, 0:1024], pC[:], 1.0 / 64, 1e-8, OP.mult, OP.max),
            waits=[("pe", pp_state["pe"])], inc="dv")
        fv2 = b.emit("vector", lambda e: e.tensor_scalar(
            s_pjs[:, 1024:2048], pD[:], 1.0 / 64, 1e-8, OP.mult, OP.max), inc="dv", pad=True)
        b.emit("vector", lambda e: e.reciprocal(s_pjr[:, 0:1024], s_pjs[:, 0:1024]),
               waits=[("dv", fv2)], inc="dv")
        rcv = b.emit("vector", lambda e: e.reciprocal(s_pjr[:, 1024:2048], s_pjs[:, 1024:2048]),
                     inc="dv")
        slC = Slot(pC[:])
        slC.free = ("dv", fv2)
        slD2 = Slot(pD[:])
        slD2.free = ("dv", fv2)

        pq_ctr = {"n": 0}

        def proj_quant_slab():
            s = pq_ctr["n"]
            if s >= 64:
                return
            pq_ctr["n"] = s + 1
            pb = pqw[s % 2]
            lv = b.emit("gpsimd", lambda e, s=s, pb=pb: e.dma_start(
                out=pb.t[:], in_=wPJ[s * 128:(s + 1) * 128, :]),
                waits=([pb.free] if pb.free else []), inc=pb.sem, k=16)
            tq = pqt[s % 2]
            v1 = b.emit("vector", lambda e, pb=pb, tq=tq: e.tensor_tensor(
                tq[:], pb.t[:], s_pjr[:], OP.mult),
                waits=[(pb.sem, lv), ("dv", rcv)], inc="dv")
            v2 = b.emit("vector", lambda e, pb=pb, tq=tq: e.tensor_scalar(
                pb.t[:], tq[:], 160.0, None, OP.add), waits=[("dv", v1)], inc="dv")
            v3 = b.emit("vector", lambda e, pb=pb, tq=tq: e.tensor_scalar(
                tq[:], pb.t[:], -160.0, 1.0, OP.add, OP.min), waits=[("dv", v2)], inc="dv")
            v4 = b.emit("vector", lambda e, pb=pb, tq=tq: e.scalar_tensor_tensor(
                pb.t[:], tq[:], -1.0, s_pjs[:], OP.max, OP.mult),
                waits=[("dv", v3)], inc="dv", pad=True)
            sv = b.emit("sync", lambda e, s=s, pb=pb: e.dma_start(
                out=d_wqp[s * 128:(s + 1) * 128, :], in_=pb.t[:]),
                waits=[("dv", v4)], inc=f"ps{s % 2}", k=16)
            pb.free = (f"ps{s % 2}", sv)

        wU13 = wU1.rearrange("(k p) m -> p k m", p=128)
        wU23 = wU2.rearrange("(k p) m -> p k m", p=128)
        u_slots1 = [slA, slC]
        u_slots2 = [slB, slD2]
        u1_free = [None, None]
        for fc in range(NCF):
            gv1 = gv2 = None
            slot1 = slot2 = None
            for (wmat, pair, slots) in [(wU13, (0, 1), u_slots1), (wU23, (2, 3), u_slots2)]:
                buf = wb[pair[fc % 2]]
                w3 = buf.t[:].rearrange("p (k j) -> p k j", k=NCD)
                lv = b.emit("gpsimd", lambda e, wmat=wmat, fc=fc, w3=w3: e.dma_start(
                    out=w3, in_=wmat[:, :, fc * 128:(fc + 1) * 128]),
                    waits=([buf.free] if buf.free else []), inc=buf.sem, k=16)
                slot = slots[fc % 2]
                gv = None
                for k in range(NCD):
                    for wi in range(2):
                        gv = b.emit("tensor", lambda e, slot=slot, buf=buf, k=k, wi=wi:
                                    e.matmul(slot.ap[:, wi * 512:(wi + 1) * 512],
                                             buf.t[:, k * 128:(k + 1) * 128],
                                             x1n[:, k * T + wi * 512: k * T + wi * 512 + 512],
                                             start=(k == 0), stop=(k == NCD - 1)),
                                    waits=([(buf.sem, lv)] + slot.wait_free())
                                    if (k == 0 and wi == 0) else [],
                                    inc="pe" if (k == NCD - 1 and wi == 1) else None)
                buf.free = ("pe", gv)
                if wmat is wU13:
                    gv1, slot1 = gv, slot
                else:
                    gv2, slot2 = gv, slot
            ub1 = u1s[fc % 2]
            ub2 = u2s[fc % 2]
            v1 = b.emit("scalar", lambda e, fc=fc, slot1=slot1, ub1=ub1: e.activation(
                ub1[:], slot1.ap[:], AF.Gelu,
                bias=s_bias[:, OFF["u1b"] + fc:OFF["u1b"] + fc + 1]),
                waits=[("pe", gv1)] + ([u1_free[fc % 2]] if u1_free[fc % 2] else []),
                inc="ac", pad=True)
            slot1.free = ("ac", v1)
            v2 = b.emit("vector", lambda e, fc=fc, slot2=slot2, ub2=ub2: e.tensor_scalar(
                ub2[:], slot2.ap[:], s_bias[:, OFF["u2b"] + fc:OFF["u2b"] + fc + 1], None,
                OP.add), waits=[("pe", gv2)], inc="dv")
            slot2.free = ("dv", b.pad("vector"))
            pv = b.emit("vector", lambda e, fc=fc, ub1=ub1, ub2=ub2: e.tensor_tensor(
                prod[:, fc * T:(fc + 1) * T], ub1[:], ub2[:], OP.mult),
                waits=[("ac", v1), ("dv", v2)], inc="dv")
            u1_free[fc % 2] = ("dv", pv)
            proj_quant_slab()
            if fc % 3 == 0:
                proj_quant_slab()
        while pq_ctr["n"] < 64:
            proj_quant_slab()
        b.barrier()
        dump("prod", lambda: ([128, NCF * T], prod[:]))
        b.dma_fence()
        sc9.close()
        sc_ps.close()
        sc_x1n.close()

        sc_x2 = ExitStack()
        x2bf = sc_x2.enter_context(nc.sbuf_tensor("x2bf", [128, NCD * T], BF16, side="right"))
        sc10 = ExitStack()
        lnin = sc10.enter_context(nc.sbuf_tensor("lnin", [128, NCD * 256], F32, side="right"))
        sqa = [sc10.enter_context(nc.sbuf_tensor(f"sqa{i}", [128, 256], BF16, side="right"))
               for i in range(2)]
        sqb = [sc10.enter_context(nc.sbuf_tensor(f"sqb{i}", [128, 256], BF16, side="right"))
               for i in range(2)]
        sre = [DBuf(sc10.enter_context(nc.sbuf_tensor(f"sre{i}", [128, 256], F32, side="right")))
               for i in range(2)]
        sre[0].sem, sre[1].sem = "sr0", "sr1"
        x2t = [sc10.enter_context(nc.sbuf_tensor(f"x2t{i}", [128, 256], F32, side="right"))
               for i in range(2)]
        x2t_free = [None, None]
        bcm = sc10.enter_context(nc.sbuf_tensor("bcm", [128, 256], F32, side="right"))
        bcr = sc10.enter_context(nc.sbuf_tensor("bcr", [128, 256], F32, side="right"))

        psum_of = {0: pA, 1: pB, 2: pC, 3: pD}
        slot4 = (slA, slB, slC, slD2)
        for tg in range(4):
            mm_done = None
            for k in range(NCF):
                buf = wb[k % 2]
                lv = b.emit("gpsimd", lambda e, k=k, buf=buf: e.dma_start(
                    out=buf.t[:], in_=wDN[k * 128:(k + 1) * 128, :]),
                    waits=([buf.free] if buf.free else []), inc=buf.sem, k=16)
                gv = None
                for m in range(NCD):
                    pt = psum_of[m // 4]
                    pcol = (m % 4) * 256
                    gv = b.emit("tensor", lambda e, pt=pt, pcol=pcol, buf=buf, m=m, k=k, tg=tg:
                                e.matmul(pt[:, pcol:pcol + 256],
                                         buf.t[:, m * 128:(m + 1) * 128],
                                         prod[:, k * T + tg * 256: k * T + tg * 256 + 256],
                                         start=(k == 0 and m % 2 == 0),
                                         stop=(k == NCF - 1),
                                         skip_group_check=True),
                                waits=([(buf.sem, lv)] +
                                       (slA.wait_free() + slB.wait_free() +
                                        slC.wait_free() + slD2.wait_free()
                                        if k == 0 else []))
                                if (m == 0) else [],
                                inc="pe" if m == NCD - 1 else None)
                buf.free = ("pe", gv)
                mm_done = gv
            evv = None
            for m in range(NCD):
                pt = psum_of[m // 4]
                pcol = (m % 4) * 256
                sb_ = sre[m % 2]
                lv = b.emit("sync", lambda e, m=m, tg=tg, sb_=sb_: e.dma_start(
                    out=sb_.t[:], in_=d_S[:, m * T + tg * 256: m * T + tg * 256 + 256]),
                    waits=([sb_.free] if sb_.free else []), inc=sb_.sem, k=16)
                evv = b.emit("vector", lambda e, m=m, pt=pt, pcol=pcol, sb_=sb_:
                             e.scalar_tensor_tensor(
                                 lnin[:, m * 256:(m + 1) * 256], pt[:, pcol:pcol + 256],
                                 bias("dnb", m), sb_.t[:], OP.add, OP.add),
                             waits=[("pe", mm_done), (sb_.sem, lv)], inc="dv",
                             pad=(m == NCD - 1))
                sb_.free = ("dv", evv)
            gv = None
            sq_rd = None
            for c in range(NCD):
                first = (c == 0)
                last = (c == NCD - 1)
                sa_ = sqa[c % 2]
                sb2 = sqb[c % 2]
                cvv = b.emit("vector", lambda e, c=c, sa_=sa_: e.tensor_copy(
                    sa_[:], lnin[:, c * 256:(c + 1) * 256]),
                    waits=([("dv", evv)] if first else []) +
                    ([sq_rd] if sq_rd else []), inc="dv", pad=True)
                sqv = b.emit("scalar", lambda e, sa_=sa_, sb2=sb2: e.activation(
                    sb2[:], sa_[:], AF.Square), waits=[("dv", cvv)], inc="ac", pad=True)
                b.emit("tensor", lambda e, c=c, sa_=sa_, first=first, last=last: e.matmul(
                    pA[0:1, 0:256], s_onec[:], sa_[:], start=first, stop=last),
                    waits=([("dv", cvv)] + (slA.wait_free() if first else [])), inc=None)
                gv = b.emit("tensor", lambda e, c=c, sb2=sb2, first=first, last=last: e.matmul(
                    pA[0:1, 512:768], s_onec[:], sb2[:], start=first, stop=last),
                    waits=[("ac", sqv)], inc="pe")
                sq_rd = ("pe", gv)
            vv1 = b.emit("vector", lambda e: e.tensor_scalar(
                s_r0[0:1, 0:256], pA[0:1, 0:256], 1.0 / D, None, OP.mult),
                waits=[("pe", gv)], inc="dv", pad=True)
            vv2 = b.emit("scalar", lambda e: e.activation(s_r0[0:1, 256:512],
                                                    s_r0[0:1, 0:256], AF.Square),
                   waits=[("dv", vv1)], inc="ac", pad=True)
            vv3 = b.emit("vector", lambda e: e.scalar_tensor_tensor(
                s_r1[0:1, 0:256], pA[0:1, 512:768], 1.0 / D, s_r0[0:1, 256:512],
                OP.mult, OP.subtract), waits=[("ac", vv2)], inc="dv")
            vv3 = b.emit("vector", lambda e: e.tensor_scalar(
                s_r1[0:1, 0:256], s_r1[0:1, 0:256], 0.0, None, OP.max),
                inc="dv", pad=True)
            vv4 = b.emit("scalar", lambda e: e.activation(
                s_r1[0:1, 256:512], s_r1[0:1, 0:256], AF.Ln, bias=np.float32(1e-5)),
                waits=[("dv", vv3)], inc="ac")
            rvv = b.emit("scalar", lambda e: e.activation(
                s_r1[0:1, 512:768], s_r1[0:1, 256:512], AF.Exp, scale=np.float32(-0.5)),
                waits=[("ac", b.cur("ac"))], inc="ac", pad=True)
            slA.free = ("ac", rvv)
            g1 = b.emit("tensor", lambda e: e.matmul(
                pB[:, 0:256], s_on1[:], s_r0[0:1, 0:256], start=True, stop=True),
                waits=[("ac", rvv)] + slB.wait_free(), inc=None)
            g2 = b.emit("tensor", lambda e: e.matmul(
                pB[:, 256:512], s_on1[:], s_r1[0:1, 512:768], start=True, stop=True),
                inc="pe")
            bv1 = b.emit("vector", lambda e: e.tensor_copy(bcm[:], pB[:, 0:256]),
                         waits=[("pe", g2)], inc="dv")
            bv2 = b.emit("vector", lambda e: e.tensor_copy(bcr[:], pB[:, 256:512]),
                         inc="dv", pad=True)
            slB.free = ("dv", bv2)
            apv = None
            for m in range(NCD):
                xt = x2t[m % 2]
                wt = [("dv", bv2)]
                if x2t_free[m % 2]:
                    wt.append(x2t_free[m % 2])
                b.emit("vector", lambda e, m=m, xt=xt: e.tensor_tensor(
                    xt[:], lnin[:, m * 256:(m + 1) * 256], bcm[:], OP.subtract),
                    waits=wt, inc="dv")
                apv = b.emit("vector", lambda e, m=m, xt=xt: e.tensor_tensor(
                    xt[:], xt[:], bcr[:], OP.mult), inc="dv", pad=True)
                sv = b.emit("sync", lambda e, m=m, tg=tg, xt=xt: e.dma_start(
                    out=d_x2[:, m * T + tg * 256: m * T + tg * 256 + 256], in_=xt[:]),
                    waits=[("dv", apv)], inc=f"xs{m % 2}", k=16)
                apv = b.emit("vector", lambda e, m=m, tg=tg, xt=xt: e.tensor_copy(
                    x2bf[:, m * T + tg * 256: m * T + tg * 256 + 256], xt[:]),
                    inc="dv")
                x2t_free[m % 2] = (f"xs{m % 2}", sv)
            for s_ in slot4:
                s_.free = ("dv", apv)
        b.barrier()
        dump("lnin3", lambda: ([128, NCD * 256], lnin[:]))
        dump("bcm3", lambda: ([128, 256], bcm[:]))
        dump("bcr3", lambda: ([128, 256], bcr[:]))
        dump("x2", lambda: ([128, NCD * T], x2bf[:]))
        b.dma_fence()
        sc10.close()
        sc_prod.close()

        sc11 = ExitStack()
        hbf = sc11.enter_context(nc.sbuf_tensor("hbf", [128, NCHF * T], BF16, side="left"))
        sc11b = ExitStack()
        qs = [sc11b.enter_context(nc.sbuf_tensor(f"qs{i}", [128, 128], BF16, side="right"))
              for i in range(2)]
        qr = [sc11b.enter_context(nc.sbuf_tensor(f"qr{i}", [128, 128], F32, side="right"))
              for i in range(2)]
        q32s = [sc11b.enter_context(nc.sbuf_tensor(f"q32s{i}", [32, 128], BF16, side="right"))
                for i in range(2)]
        q32r = [sc11b.enter_context(nc.sbuf_tensor(f"q32r{i}", [32, 128], F32, side="right"))
                for i in range(2)]
        gs_st = sc11b.enter_context(nc.sbuf_tensor("gs_st", [128, 512], F32, side="right"))
        qw = [wb[0], wb[1]]
        qt = [wb[2], wb[3]]
        wGU3 = wGU.rearrange("(k p) h -> p k h", p=128)
        qctr = {"i": 0}
        qfree = {}

        def quant_block(blk):
            i = qctr["i"] % 2
            qctr["i"] += 1
            pb = qw[i]
            tqb = qt[i]
            tq = tqb.t
            lv = b.emit("gpsimd", lambda e, blk=blk, pb=pb: e.dma_start(
                out=pb.t[:].rearrange("p (k j) -> p k j", k=NCD),
                in_=wGU3[:, :, blk * 128:(blk + 1) * 128]),
                waits=([pb.free] if pb.free else []), inc=pb.sem, k=16)
            av = b.emit("vector", lambda e, pb=pb, tq=tq: e.tensor_scalar(
                tq[:].bitcast(U16), pb.t[:].bitcast(U16), 0x7FFF, None, OP.bitwise_and),
                waits=[(pb.sem, lv)] + ([tqb.free] if tqb.free else []), inc="dv", pad=True)
            pgs = pC if i == 0 else pD
            prev = qfree.get(("gs", i))
            gv = None
            for k in range(NCD):
                gv = b.emit("tensor", lambda e, pgs=pgs, tq=tq, k=k: e.matmul(
                    pgs[0:32, 0:128], s_vG[:], tq[:, k * 128:(k + 1) * 128],
                    start=(k == 0), stop=(k == NCD - 1)),
                    waits=([("dv", av)] + ([prev] if prev else [])) if k == 0 else [],
                    inc="pe" if k == NCD - 1 else None)
            v1 = b.emit("vector", lambda e, pgs=pgs, i=i: e.tensor_scalar(
                q32s[i][:], pgs[0:32, 0:128], 1.0 / 64, 1e-8, OP.mult, OP.max),
                waits=[("pe", gv)], inc="dv")
            v2 = b.emit("vector", lambda e, i=i: e.reciprocal(q32r[i][:], q32s[i][:]),
                        waits=[("dv", v1)], inc="dv", pad=True)
            qfree[("gs", i)] = ("dv", v2)
            g1 = b.emit("tensor", lambda e, pgs=pgs, i=i: e.matmul(
                pgs[:, 512:640], s_vRb[0:32, :], q32s[i][:], start=True, stop=True),
                waits=[("dv", v2)] + ([qfree[("rep", i)]] if ("rep", i) in qfree else []),
                inc=None)
            g2 = b.emit("tensor", lambda e, pgs=pgs, i=i: e.matmul(
                pgs[:, 640:768], s_vRf[0:32, :], q32r[i][:], start=True, stop=True),
                inc="pe")
            r1 = b.emit("vector", lambda e, pgs=pgs, i=i: e.tensor_copy(
                qs[i][:], pgs[:, 512:640]), waits=[("pe", g2)], inc="dv")
            r2 = b.emit("vector", lambda e, pgs=pgs, i=i: e.tensor_copy(
                qr[i][:], pgs[:, 640:768]), inc="dv", pad=True)
            qfree[("rep", i)] = ("dv", r2)
            s3b = qs[i][:].rearrange("p (o j) -> p o j", o=1).broadcast_to([128, NCD, 128])
            r3b = qr[i][:].rearrange("p (o j) -> p o j", o=1).broadcast_to([128, NCD, 128])
            w3v = pb.t[:].rearrange("p (k j) -> p k j", k=NCD)
            t3v = tq[:].rearrange("p (k j) -> p k j", k=NCD)
            v3 = b.emit("vector", lambda e, t3v=t3v, w3v=w3v, r3b=r3b: e.tensor_tensor(
                t3v, w3v, r3b, OP.mult), waits=[("dv", r2), ("pe", gv)], inc="dv")
            v4 = b.emit("vector", lambda e, pb=pb, tq=tq: e.tensor_scalar(
                pb.t[:], tq[:], 160.0, None, OP.add), waits=[("dv", v3)], inc="dv")
            v5 = b.emit("vector", lambda e, pb=pb, tq=tq: e.tensor_scalar(
                tq[:], pb.t[:], -160.0, 1.0, OP.add, OP.min), waits=[("dv", v4)], inc="dv")
            v6 = b.emit("vector", lambda e, pb=pb, t3v=t3v, s3b=s3b:
                        e.scalar_tensor_tensor(
                            pb.t[:].rearrange("p (k j) -> p k j", k=NCD), t3v, -1.0,
                            s3b, OP.max, OP.mult), waits=[("dv", v5)], inc="dv", pad=True)
            tqb.free = ("dv", v6)
            return pb, v6

        for j in range(NCHF):
            bg, vg = quant_block(j)
            gv1 = None
            for k in range(NCD):
                for wi in range(2):
                    gv1 = b.emit("tensor", lambda e, bg=bg, k=k, wi=wi: e.matmul(
                        pA[:, wi * 512:(wi + 1) * 512], bg.t[:, k * 128:(k + 1) * 128],
                        x2bf[:, k * T + wi * 512: k * T + wi * 512 + 512],
                        start=(k == 0), stop=(k == NCD - 1)),
                        waits=([("dv", vg)] + slA.wait_free()) if (k == 0 and wi == 0) else [],
                        inc="pe" if (k == NCD - 1 and wi == 1) else None)
            bg.free = ("pe", gv1)
            bu, vu = quant_block(64 + j)
            gv2 = None
            for k in range(NCD):
                for wi in range(2):
                    gv2 = b.emit("tensor", lambda e, bu=bu, k=k, wi=wi: e.matmul(
                        pB[:, wi * 512:(wi + 1) * 512], bu.t[:, k * 128:(k + 1) * 128],
                        x2bf[:, k * T + wi * 512: k * T + wi * 512 + 512],
                        start=(k == 0), stop=(k == NCD - 1)),
                        waits=([("dv", vu)] + slB.wait_free()) if (k == 0 and wi == 0) else [],
                        inc="pe" if (k == NCD - 1 and wi == 1) else None)
            bu.free = ("pe", gv2)
            hv = qfree.get("hv")
            for wi in range(2):
                sv = b.emit("scalar", lambda e, wi=wi: e.activation(
                    gs_st[:], pA[:, wi * 512:(wi + 1) * 512], AF.Silu),
                    waits=([("pe", gv1), ("pe", gv2)] if wi == 0 else []) +
                    ([("dv", hv)] if hv else []), inc="ac", pad=True)
                hv = b.emit("vector", lambda e, j=j, wi=wi: e.tensor_tensor(
                    hbf[:, j * T + wi * 512: j * T + wi * 512 + 512], gs_st[:],
                    pB[:, wi * 512:(wi + 1) * 512], OP.mult),
                    waits=[("ac", sv)], inc="dv")
            qfree["hv"] = hv
            pvv = b.pad("vector")
            slA.free = ("dv", pvv)
            slB.free = ("dv", pvv)
        b.barrier()
        dump("hffn", lambda: ([128, NCHF * T], hbf[:]))
        b.dma_fence()
        sc11b.close()
        sc_x2.close()

        sq12 = sc11.enter_context(nc.sbuf_tensor("sq12", [128, T], BF16, side="right"))
        bcf = sc11.enter_context(nc.sbuf_tensor("bcf", [128, T], BF16, side="right"))
        gv = None
        for c in range(NCHF):
            sv = b.emit("scalar", lambda e, c=c: e.activation(
                sq12[:], hbf[:, c * T:(c + 1) * T], AF.Square),
                waits=([("pe", gv)] if gv else []) +
                (slC.wait_free() if c == 0 else []), inc="ac", pad=True)
            for wi in range(2):
                gv = b.emit("tensor", lambda e, c=c, wi=wi: e.matmul(
                    pC[0:1, wi * 512:(wi + 1) * 512], s_onec[:],
                    sq12[:, wi * 512:(wi + 1) * 512],
                    start=(c == 0), stop=(c == NCHF - 1)),
                    waits=[("ac", sv)] if wi == 0 else [],
                    inc="pe" if (wi == 1) else None)
        b.emit("scalar", lambda e: e.activation(
            s_r0[0:1, 0:1024], pC[0:1, 0:1024], AF.Ln, bias=np.float32(1.1920929e-07),
            scale=np.float32(1.0 / HF)), waits=[("pe", gv)], inc="ac")
        rv = b.emit("scalar", lambda e: e.activation(
            s_r1[0:1, 0:1024], s_r0[0:1, 0:1024], AF.Exp, scale=np.float32(-0.5)),
            waits=[("ac", b.cur("ac"))], inc="ac", pad=True)
        g1 = b.emit("tensor", lambda e: e.matmul(
            pC[:, 0:512], s_on1[:], s_r1[0:1, 0:512], start=True, stop=True),
            waits=[("ac", rv)], inc=None)
        g2 = b.emit("tensor", lambda e: e.matmul(
            pC[:, 512:1024], s_on1[:], s_r1[0:1, 512:1024], start=True, stop=True),
            inc="pe")
        bv = b.emit("vector", lambda e: e.tensor_copy(bcf[:], pC[:]),
                    waits=[("pe", g2)], inc="dv", pad=True)
        slC.free = ("dv", bv)
        for c in range(NCHF):
            b.emit("vector", lambda e, c=c: e.tensor_tensor(
                hbf[:, c * T:(c + 1) * T], bcf[:], hbf[:, c * T:(c + 1) * T], OP.mult),
                waits=[("dv", bv)] if c == 0 else [], inc="dv")
        b.barrier()
        dump("hn", lambda: ([128, NCHF * T], hbf[:]))
        b.dma_fence()

        sc13 = ExitStack()
        lnin2 = sc13.enter_context(nc.sbuf_tensor("lnin2", [128, NCD * 256], F32, side="right"))
        x2re = [DBuf(sc13.enter_context(nc.sbuf_tensor(f"x2re{i}", [128, 256], F32, side="right")))
                for i in range(2)]
        x2re[0].sem, x2re[1].sem = "xr0", "xr1"
        sqa2 = [sc13.enter_context(nc.sbuf_tensor(f"sqa2{i}", [128, 256], BF16, side="right"))
                for i in range(2)]
        sqb2 = [sc13.enter_context(nc.sbuf_tensor(f"sqb2{i}", [128, 256], BF16, side="right"))
                for i in range(2)]
        bcm2 = sc13.enter_context(nc.sbuf_tensor("bcm2", [128, 256], F32, side="right"))
        bcr2 = sc13.enter_context(nc.sbuf_tensor("bcr2", [128, 256], F32, side="right"))
        otm = sc13.enter_context(nc.sbuf_tensor("otm", [128, 2048], F32, side="right"))
        otm_free = None
        ttmp = sc13.enter_context(nc.sbuf_tensor("ttmp", [128, 256], F32, side="right"))

        for tg in range(4):
            mm_done = None
            for k in range(NCHF):
                buf = wb[k % 2]
                lv = b.emit("gpsimd", lambda e, k=k, buf=buf: e.dma_start(
                    out=buf.t[:], in_=d_wqp[k * 128:(k + 1) * 128, :]),
                    waits=([buf.free] if buf.free else []), inc=buf.sem, k=16)
                gv = None
                for m in range(NCD):
                    pt = psum_of[m // 4]
                    pcol = (m % 4) * 256
                    gv = b.emit("tensor", lambda e, pt=pt, pcol=pcol, buf=buf, m=m, k=k, tg=tg:
                                e.matmul(pt[:, pcol:pcol + 256],
                                         buf.t[:, m * 128:(m + 1) * 128],
                                         hbf[:, k * T + tg * 256: k * T + tg * 256 + 256],
                                         start=(k == 0 and m % 2 == 0),
                                         stop=(k == NCHF - 1),
                                         skip_group_check=True),
                                waits=([(buf.sem, lv)] +
                                       (slA.wait_free() + slB.wait_free() +
                                        slC.wait_free() + slD2.wait_free()
                                        if k == 0 else []))
                                if m == 0 else [],
                                inc="pe" if m == NCD - 1 else None)
                buf.free = ("pe", gv)
                mm_done = gv
            evv = None
            for m in range(NCD):
                pt = psum_of[m // 4]
                pcol = (m % 4) * 256
                xb = x2re[m % 2]
                lv = b.emit("sync", lambda e, m=m, tg=tg, xb=xb: e.dma_start(
                    out=xb.t[:], in_=d_x2[:, m * T + tg * 256: m * T + tg * 256 + 256]),
                    waits=([xb.free] if xb.free else []), inc=xb.sem, k=16)
                evv = b.emit("vector", lambda e, m=m, pt=pt, pcol=pcol, xb=xb:
                             e.tensor_tensor(lnin2[:, m * 256:(m + 1) * 256],
                                             pt[:, pcol:pcol + 256], xb.t[:], OP.add),
                             waits=[("pe", mm_done), (xb.sem, lv)], inc="dv",
                             pad=(m == NCD - 1))
                xb.free = ("dv", evv)
            gv = None
            sq_rd = None
            for c in range(NCD):
                first = (c == 0)
                last = (c == NCD - 1)
                sa_ = sqa2[c % 2]
                sb2 = sqb2[c % 2]
                cvv = b.emit("vector", lambda e, c=c, sa_=sa_: e.tensor_copy(
                    sa_[:], lnin2[:, c * 256:(c + 1) * 256]),
                    waits=([("dv", evv)] if first else []) +
                    ([sq_rd] if sq_rd else []), inc="dv", pad=True)
                sqv = b.emit("scalar", lambda e, sa_=sa_, sb2=sb2: e.activation(
                    sb2[:], sa_[:], AF.Square), waits=[("dv", cvv)], inc="ac", pad=True)
                b.emit("tensor", lambda e, c=c, sa_=sa_, first=first, last=last: e.matmul(
                    pA[0:1, 0:256], s_onec[:], sa_[:], start=first, stop=last),
                    waits=([("dv", cvv)] + (slA.wait_free() if first else [])), inc=None)
                gv = b.emit("tensor", lambda e, c=c, sb2=sb2, first=first, last=last: e.matmul(
                    pA[0:1, 512:768], s_onec[:], sb2[:], start=first, stop=last),
                    waits=[("ac", sqv)], inc="pe")
                sq_rd = ("pe", gv)
            vv1 = b.emit("vector", lambda e: e.tensor_scalar(
                s_r0[0:1, 0:256], pA[0:1, 0:256], 1.0 / D, None, OP.mult),
                waits=[("pe", gv)], inc="dv", pad=True)
            vv2 = b.emit("scalar", lambda e: e.activation(s_r0[0:1, 256:512],
                                                    s_r0[0:1, 0:256], AF.Square),
                   waits=[("dv", vv1)], inc="ac", pad=True)
            vv3 = b.emit("vector", lambda e: e.scalar_tensor_tensor(
                s_r1[0:1, 0:256], pA[0:1, 512:768], 1.0 / D, s_r0[0:1, 256:512],
                OP.mult, OP.subtract), waits=[("ac", vv2)], inc="dv")
            vv3 = b.emit("vector", lambda e: e.tensor_scalar(
                s_r1[0:1, 0:256], s_r1[0:1, 0:256], 0.0, None, OP.max),
                inc="dv", pad=True)
            vv4 = b.emit("scalar", lambda e: e.activation(
                s_r1[0:1, 256:512], s_r1[0:1, 0:256], AF.Ln, bias=np.float32(1e-5)),
                waits=[("dv", vv3)], inc="ac")
            rvv = b.emit("scalar", lambda e: e.activation(
                s_r1[0:1, 512:768], s_r1[0:1, 256:512], AF.Exp, scale=np.float32(-0.5)),
                waits=[("ac", b.cur("ac"))], inc="ac", pad=True)
            slA.free = ("ac", rvv)
            g1 = b.emit("tensor", lambda e: e.matmul(
                pB[:, 0:256], s_on1[:], s_r0[0:1, 0:256], start=True, stop=True),
                waits=[("ac", rvv)] + slB.wait_free(), inc=None)
            g2 = b.emit("tensor", lambda e: e.matmul(
                pB[:, 256:512], s_on1[:], s_r1[0:1, 512:768], start=True, stop=True),
                inc="pe")
            bv1 = b.emit("vector", lambda e: e.tensor_copy(bcm2[:], pB[:, 0:256]),
                         waits=[("pe", g2)], inc="dv")
            bv2 = b.emit("vector", lambda e: e.tensor_copy(bcr2[:], pB[:, 256:512]),
                         inc="dv", pad=True)
            slB.free = ("dv", bv2)
            apv = None
            for m in range(NCD):
                b.emit("vector", lambda e, m=m: e.tensor_tensor(
                    ttmp[:], lnin2[:, m * 256:(m + 1) * 256], bcm2[:], OP.subtract),
                    waits=[("dv", bv2)] if m == 0 else [], inc="dv")
                apv = b.emit("vector", lambda e, m=m: e.tensor_tensor(
                    lnin2[:, m * 256:(m + 1) * 256], ttmp[:], bcr2[:], OP.mult),
                    inc="dv")
            apv = b.pad("vector")
            for ts in range(2):
                g = None
                for m in range(NCD):
                    pt = pC if m < 8 else pD
                    pcol = (m % 8) * 128
                    g = b.emit("tensor", lambda e, pt=pt, pcol=pcol, m=m, ts=ts: e.matmul(
                        pt[:, pcol:pcol + 128],
                        lnin2[:, m * 256 + ts * 128: m * 256 + (ts + 1) * 128],
                        s_idf[:], start=True, stop=True),
                        waits=([("dv", apv)] + slC.wait_free() + slD2.wait_free() +
                               ([otm_free] if otm_free else []))
                        if m == 0 else [],
                        inc="pe" if m == NCD - 1 else None)
                t1 = b.emit("scalar", lambda e: e.copy(otm[:, 0:1024], pC[:]),
                            waits=[("pe", g)], inc="ac")
                t2 = b.emit("scalar", lambda e: e.copy(otm[:, 1024:2048], pD[:]),
                            inc="ac", pad=True)
                slC.free = ("ac", t2)
                slD2.free = ("ac", t2)
                sv = b.emit("sync", lambda e, tg=tg, ts=ts: e.dma_start(
                    out=o_out[tg * 256 + ts * 128: tg * 256 + (ts + 1) * 128, :],
                    in_=otm[:]), waits=[("ac", t2)], inc="os", k=16)
                otm_free = ("os", sv)

        dump("c_wb", lambda: ([128, 128], c_wb[:]))
        _fw = [(s, b.cur(s)) for s in b.dma_sems if b.cur(s) > 0]
        b.emit("sync", None, waits=_fw)
        b.emit("gpsimd", None, waits=_fw)

        sc13.close()
        sc11.close()
        with nc.Block() as block:
            b.finalize(block)

    return nc



def _host_prep(inputs):
    d_ord = np.array([(k % 32) * 64 + k // 32 for k in range(D)], dtype=np.int64)
    h_ord = np.array([(k % 128) * 64 + k // 128 for k in range(HF)], dtype=np.int64)

    f32 = lambda a: np.ascontiguousarray(np.asarray(a), dtype=np.float32)
    x = f32(inputs["L0_attn_x"])[0]
    x = x[:, d_ord]
    wR_ = np.ascontiguousarray(f32(inputs["L0_attn_right_W"])[d_ord, :])
    wL_ = np.ascontiguousarray(f32(inputs["L0_attn_left_W"])[d_ord, :])
    wGA_ = f32(inputs["L0_attn_ga_W"])
    wGX_ = f32(inputs["L0_attn_gx_W"])
    wO_ = np.ascontiguousarray(f32(inputs["L0_attn_out_W"])[:, d_ord])
    wU1_ = np.ascontiguousarray(f32(inputs["L0_attn_up1_W"])[d_ord, :])
    wU2_ = np.ascontiguousarray(f32(inputs["L0_attn_up2_W"])[d_ord, :])
    wDN_ = np.ascontiguousarray(f32(inputs["L0_attn_down_W"])[:, d_ord])
    gu = f32(inputs["L0_ffn_w_gate_up"])
    gu = np.concatenate([gu[0:HF][h_ord], gu[HF:][h_ord]], axis=0)[:, d_ord]
    wGU_ = np.ascontiguousarray(gu.T)
    pj = f32(inputs["L0_ffn_w_proj"])[d_ord][:, h_ord]
    wPJ_ = np.ascontiguousarray(pj.T)

    def cols16(v):
        return np.ascontiguousarray(np.asarray(v).astype(np.float32).reshape(-1, 128).T)

    bq = np.zeros((128, 320), dtype=np.float32)
    bq[:, 0:16] = cols16(f32(inputs["L0_attn_norm1_w"])[d_ord])
    bq[:, 16:32] = cols16(inputs["L0_attn_right_b"])
    cw = f32(inputs["L0_attn_conv_w"])
    for k in range(4):
        bq[:, 32 + 16 * k:48 + 16 * k] = cols16(cw[k])
    bq[:, 96:112] = cols16(inputs["L0_attn_conv_b"])
    bq[:, 112:128] = cols16(inputs["L0_attn_ga_b"])
    bq[:, 128:144] = cols16(inputs["L0_attn_gx_b"])
    bq[:, 144:160] = cols16(inputs["L0_attn_a_param"])
    bq[:, 160:176] = cols16(inputs["L0_attn_left_b"])
    bq[:, 176:192] = cols16(f32(inputs["L0_attn_out_b"])[d_ord])
    bq[:, 192:208] = cols16(f32(inputs["L0_attn_norm2_w"])[d_ord])
    bq[:, 208:224] = cols16(f32(inputs["L0_attn_down_b"])[d_ord])
    bq[:, 224:272] = cols16(inputs["L0_attn_up1_b"])
    bq[:, 272:320] = cols16(inputs["L0_attn_up2_b"])

    id128f_ = np.eye(128, dtype=np.float32)
    id128b_ = np.eye(128, dtype=np.float32).astype(NPBF)
    id8f_ = np.eye(8, dtype=np.float32)
    on1f_ = np.ones((1, 128), dtype=np.float32)
    onecb_ = np.ones((128, 1), dtype=NPBF)
    vGf = np.zeros((128, 32), dtype=np.float32)
    for q in range(128):
        vGf[q, q % 32] = 1.0
    vRb_ = np.ascontiguousarray(vGf.T).astype(NPBF)
    vRf_ = np.ascontiguousarray(vGf.T).astype(np.float32)
    vG_ = vGf.astype(NPBF)

    shared = dict(wR=wR_, wL=wL_, wGA=wGA_, wGX=wGX_, wO=wO_, wU1=wU1_, wU2=wU2_,
                  wDN=wDN_, wGU=wGU_, wPJ=wPJ_, biasq=bq, id128f=id128f_,
                  id128b=id128b_, id8f=id8f_, on1f=on1f_, onecb=onecb_,
                  vG=vG_, vRb=vRb_, vRf=vRf_)

    in_maps = []
    for c in range(NC):
        sh = np.zeros((TH, D), dtype=np.float32)
        sh[HALO:] = x[c * T:(c + 1) * T]
        if c > 0:
            sh[0:HALO] = x[c * T - HALO: c * T]
        mk = np.zeros((128, NC), dtype=np.float32)
        mk[:, 0:c] = 1.0
        m = dict(shared)
        m["x_in"] = sh
        m["hmask"] = np.full((128, 1), 0.0 if c == 0 else 1.0, dtype=np.float32)
        m["maskv"] = mk
        m["mask1m"] = 1.0 - mk
        in_maps.append(m)
    return in_maps, d_ord


def kernel(**inputs):
    in_maps, d_ord = _host_prep(inputs)
    nc = build_nc()
    res = run_bass_kernel_spmd(nc, in_maps, core_ids=list(range(NC)))
    out = np.concatenate([res.results[c]["o_out"] for c in range(NC)], axis=0)
    inv = np.empty_like(d_ord)
    inv[d_ord] = np.arange(D)
    out = out[:, inv]
    return out.reshape(1, 8192, D).astype(np.float32)


# version 06
# speedup vs baseline: 1.0701x; 1.0701x over previous
import numpy as np
import ml_dtypes
from contextlib import ExitStack

import concourse.bass as bass
import concourse.mybir as mybir
from concourse.bass_utils import run_bass_kernel_spmd

F32 = mybir.dt.float32
BF16 = mybir.dt.bfloat16
U16 = mybir.dt.uint16
AF = mybir.ActivationFunctionType
OP = mybir.AluOpType
NPBF = ml_dtypes.bfloat16

NC = 8
D = 2048
T = 1024
HALO = 8
TH = T + HALO
NCD = 16
F_MLP = 6144
NCF = 48
HF = 8192
NCHF = 64
G2 = 16384

DUMPS = {}


class B:

    ENGS = ("sync", "gpsimd", "tensor", "vector", "scalar")

    def __init__(self, nc):
        self.nc = nc
        self.streams = {e: [] for e in self.ENGS}
        self.semh = {}
        self.semv = {}
        self.dma_sems = ["dw", "dh"]

    def mksem(self, ctx, *names):
        for n in names:
            self.semh[n] = ctx.enter_context(self.nc.semaphore(n))
            self.semv[n] = 0

    def mkdma(self, ctx, *names):
        self.mksem(ctx, *names)
        self.dma_sems.extend(names)

    ENG_SEM = {"tensor": "pe", "vector": "dv", "scalar": "ac"}

    def emit(self, eng, fn, waits=(), inc=None, k=1, pad=False):
        wl = [(s, v) for (s, v) in waits if v and v > 0]
        es = self.ENG_SEM.get(eng)
        if es is not None and fn is not None:
            cv = self.semv[es]
            if cv > 0:
                wl.append((es, cv))
        ws = tuple(wl)
        val = None
        if inc is not None:
            self.semv[inc] += (k if k is not None else 1)
            val = self.semv[inc]
        self.streams[eng].append((fn, ws, inc, k))
        if pad:
            return self.pad(eng)
        return val

    def cur(self, name):
        return self.semv[name]

    def pad(self, eng, pad_ap=None):
        sem = {"tensor": "pe", "vector": "dv", "scalar": "ac"}[eng]
        ap = self.pad_cells[eng]
        if eng == "scalar":
            return self.emit(eng, lambda e: e.copy(ap, ap), inc=sem)
        return self.emit(eng, lambda e: e.tensor_copy(ap, ap), inc=sem)

    def dma_fence(self):
        w = [(s, self.cur(s)) for s in self.dma_sems if self.cur(s) > 0]
        cw = [(s, self.cur(s)) for s in ("pe", "dv", "ac") if self.cur(s) > 0]
        for e in self.ENGS:
            self.emit(e, None, waits=(w + cw) if e in ("sync", "gpsimd") else w)

    def barrier(self, engs=("tensor", "vector", "scalar")):
        marks = {}
        for e, s in (("tensor", "pe"), ("vector", "dv"), ("scalar", "ac")):
            if e in engs:
                if e != "tensor":
                    marks[e] = (s, self.pad(e))
                else:
                    marks[e] = (s, self.cur(s))
        for e in engs:
            waits = [v for k2, v in marks.items() if k2 != e]
            self.emit(e, None, waits=waits)

    def finalize(self, block):
        def emit_stream(eng_name):
            def run(eng):
                last = {}
                for fn, ws, inc, k in self.streams[eng_name]:
                    for (s, v) in ws:
                        if last.get(s, -1) >= v:
                            continue
                        eng.wait_ge(self.semh[s], v)
                        last[s] = v
                    if fn is None:
                        continue
                    ins = fn(eng)
                    if inc is not None:
                        if k is None:
                            ins.then_inc(self.semh[inc])
                        else:
                            ins.then_inc(self.semh[inc], k)
            return run

        block.sync(emit_stream("sync"))
        block.gpsimd(emit_stream("gpsimd"))
        block.tensor(emit_stream("tensor"))
        block.vector(emit_stream("vector"))
        block.scalar(emit_stream("scalar"))


class Slot:
    def __init__(self, ap):
        self.ap = ap
        self.free = None

    def wait_free(self):
        return [self.free] if self.free else []


class DBuf:
    def __init__(self, t):
        self.t = t
        self.loaded = 0
        self.free = None


def build_nc():
    nc = bass.Bass("TRN2", target_bir_lowering=False)

    x_in = nc.dram_tensor("x_in", [TH, D], F32, kind="ExternalInput").ap()
    wR = nc.dram_tensor("wR", [D, D], F32, kind="ExternalInput").ap()
    wL = nc.dram_tensor("wL", [D, D], F32, kind="ExternalInput").ap()
    wGA = nc.dram_tensor("wGA", [D, D], F32, kind="ExternalInput").ap()
    wGX = nc.dram_tensor("wGX", [D, D], F32, kind="ExternalInput").ap()
    wO = nc.dram_tensor("wO", [D, D], F32, kind="ExternalInput").ap()
    wU1 = nc.dram_tensor("wU1", [D, F_MLP], F32, kind="ExternalInput").ap()
    wU2 = nc.dram_tensor("wU2", [D, F_MLP], F32, kind="ExternalInput").ap()
    wDN = nc.dram_tensor("wDN", [F_MLP, D], F32, kind="ExternalInput").ap()
    wGU = nc.dram_tensor("wGU", [D, G2], F32, kind="ExternalInput").ap()
    wPJ = nc.dram_tensor("wPJ", [HF, D], F32, kind="ExternalInput").ap()
    biasq = nc.dram_tensor("biasq", [128, 320], F32, kind="ExternalInput").ap()
    maskv = nc.dram_tensor("maskv", [128, NC], F32, kind="ExternalInput").ap()
    mask1m = nc.dram_tensor("mask1m", [128, NC], F32, kind="ExternalInput").ap()
    id128f = nc.dram_tensor("id128f", [128, 128], F32, kind="ExternalInput").ap()
    id128b = nc.dram_tensor("id128b", [128, 128], BF16, kind="ExternalInput").ap()
    id8f = nc.dram_tensor("id8f", [8, 8], F32, kind="ExternalInput").ap()
    on1f = nc.dram_tensor("on1f", [1, 128], F32, kind="ExternalInput").ap()
    onecb = nc.dram_tensor("onecb", [128, 1], BF16, kind="ExternalInput").ap()
    vG = nc.dram_tensor("vG", [128, 32], BF16, kind="ExternalInput").ap()
    vRb = nc.dram_tensor("vRb", [32, 128], BF16, kind="ExternalInput").ap()
    vRf = nc.dram_tensor("vRf", [32, 128], F32, kind="ExternalInput").ap()
    hmask = nc.dram_tensor("hmask", [128, 1], F32, kind="ExternalInput").ap()

    o_out = nc.dram_tensor("o_out", [T, D], F32, kind="ExternalOutput").ap()

    d_xfm = nc.dram_tensor("d_xfm", [128, NCD * TH], F32).ap()
    d_S = nc.dram_tensor("d_S", [128, NCD * T], F32).ap()
    d_x2 = nc.dram_tensor("d_x2", [128, NCD * T], F32).ap()
    d_wqp = nc.dram_tensor("d_wqp", [HF, D], BF16).ap()
    cc_in_t = nc.dram_tensor("cc_in", [2, D], F32)
    cc_out_t = nc.dram_tensor("cc_out", [2 * NC, D], F32)

    OFF = dict(n1w=0, rb=16, cw0=32, cw1=48, cw2=64, cw3=80, cb=96, gab=112,
               gxb=128, apar=144, lb=160, ob=176, n2w=192, dnb=208,
               u1b=224, u2b=272)

    with ExitStack() as top:
        b = B(nc)
        b.mksem(top, "dw", "dh", "pe", "dv", "ac", "cc")
        b.mkdma(top, "dc", "w0", "w1", "w2", "w3", "pw0", "pw1", "pq0", "pq1",
                "ps0", "ps1", "xt0", "xt1", "xh", "sx", "ce", "cg",
                "xr0", "xr1", "ss0", "ss1", "sr0", "sr1", "xs0", "xs1", "os")

        def sbt(name, shape, dt, ctx=top, side="left"):
            return ctx.enter_context(nc.sbuf_tensor(name, shape, dt, side=side))

        s_bias = sbt("s_bias", [128, 320], F32)
        s_c4 = sbt("s_c4", [128, 16], F32)
        s_gab2 = sbt("s_gab2", [128, 16], F32)
        s_gxb2 = sbt("s_gxb2", [128, 16], F32)
        s_mk = sbt("s_mk", [128, NC], F32)
        s_mk1 = sbt("s_mk1", [128, NC], F32)
        s_idf = sbt("s_idf", [128, 128], F32)
        s_idb = sbt("s_idb", [128, 128], BF16)
        s_id8 = sbt("s_id8", [8, 8], F32)
        s_on1 = sbt("s_on1", [1, 128], F32)
        s_onec = sbt("s_onec", [128, 1], BF16)
        s_vG = sbt("s_vG", [128, 32], BF16)
        s_vRb = sbt("s_vRb", [32, 128], BF16)
        s_vRf = sbt("s_vRf", [32, 128], F32)
        s_r0 = sbt("s_r0", [1, TH], F32)
        s_r1 = sbt("s_r1", [1, TH], F32)
        s_bc = sbt("s_bc", [128, TH], F32)
        s_end = sbt("s_end", [128, 32], F32)
        t16a = sbt("t16a", [128, 16], F32)
        t16b = sbt("t16b", [128, 16], F32)
        s_hm = sbt("s_hm", [128, 1], F32)
        s_rbh = sbt("s_rbh", [128, 16], F32)
        s_padv = sbt("s_padv", [1, 8], F32)
        s_pads = sbt("s_pads", [1, 8], F32)
        b.pad_cells = {"vector": s_padv[0:1, 0:1], "scalar": s_pads[0:1, 0:1]}
        b.emit("vector", lambda e: e.memset(s_padv[:], 0.0), inc="dv")
        _pv = b.emit("vector", lambda e: e.memset(s_pads[:], 0.0), inc="dv")
        b.emit("scalar", None, waits=[("dv", _pv)])
        wb = [DBuf(sbt(f"wb{i}", [128, 2048], BF16)) for i in range(4)]
        for _i, _bf in enumerate(wb):
            _bf.sem = f"w{_i}"
        c_wb = sbt("c_wb", [128, 128], F32)
        pA = top.enter_context(nc.psum_tensor("pA", [128, 1024], F32))
        pB = top.enter_context(nc.psum_tensor("pB", [128, 1024], F32))
        pC = top.enter_context(nc.psum_tensor("pC", [128, 1024], F32))
        pD = top.enter_context(nc.psum_tensor("pD", [128, 1024], F32))

        def bias(col, m):
            return s_bias[:, OFF[col] + m:OFF[col] + m + 1]

        def dump(name, ap_fn):
            if name not in DUMPS:
                return
            shp, ap = ap_fn()
            dt_ = nc.dram_tensor("dump_" + name, shp, ap.dtype, kind="ExternalOutput").ap()
            v = b.emit("sync", lambda e, ap=ap, dt_=dt_: e.dma_start(out=dt_, in_=ap),
                       waits=[("pe", b.cur("pe")), ("dv", b.cur("dv")), ("ac", b.cur("ac"))],
                       inc="dh", k=16)
            for e in ("tensor", "vector", "scalar"):
                b.emit(e, None, waits=[("dh", v)])

        def stream_mm(wdram, nM, nK, rhs_of, windows, slots, evac, wpair,
                      extra=None, interleave=None):
            for m in range(nM):
                buf = wb[wpair[m % 2]]
                w3 = buf.t[:].rearrange("p (k j) -> p k j", k=nK)[:, 0:nK, :]
                lv = b.emit("gpsimd",
                            lambda e, m=m, w3=w3: e.dma_start(
                                out=w3, in_=wdram[:, :, m * 128:(m + 1) * 128]),
                            waits=([buf.free] if buf.free else []),
                            inc=buf.sem, k=16)
                buf.loaded = lv
                slot = slots[m % len(slots)]
                waits = [(buf.sem, lv)] + slot.wait_free()
                gv = None
                for k in range(nK):
                    first = (k == 0)
                    last = (k == nK - 1)
                    for wi, (pcol, width) in enumerate(windows):
                        rhs = rhs_of(k, wi)
                        ap = slot.ap[:, pcol:pcol + width]
                        gv = b.emit("tensor",
                                    lambda e, ap=ap, buf=buf, k=k, rhs=rhs, first=first, last=last:
                                    e.matmul(ap, buf.t[:, k * 128:(k + 1) * 128], rhs,
                                             start=first, stop=last),
                                    waits=(waits if (k == 0 and wi == 0) else []),
                                    inc=("pe" if (last and wi == len(windows) - 1) else None))
                    if extra is not None:
                        gv = extra(m, k, buf, first, last) or gv
                buf.free = ("pe", gv)
                evac(m, slot, gv)
                if interleave is not None:
                    interleave(m)

        ldh = []
        for src, dst in [(biasq, s_bias), (maskv, s_mk), (mask1m, s_mk1),
                         (id128f, s_idf), (id128b, s_idb), (id8f, s_id8),
                         (on1f, s_on1), (onecb, s_onec), (vG, s_vG),
                         (vRb, s_vRb), (vRf, s_vRf), (hmask, s_hm)]:
            ldh.append(b.emit("sync", lambda e, src=src, dst=dst:
                              e.dma_start(out=dst[:], in_=src), inc="dc", k=16))
        w0 = [("dc", ldh[-1])]
        b.emit("tensor", None, waits=w0)
        b.emit("gpsimd", None, waits=w0)
        ap_col = s_bias[:, OFF["apar"]:OFF["apar"] + 16]
        b.emit("scalar", lambda e: e.activation(t16a[:], ap_col, AF.Abs), waits=w0, inc="ac")
        b.emit("scalar", lambda e: e.activation(t16a[:], t16a[:], AF.Exp, scale=np.float32(-1.0)),
               waits=[("ac", b.cur("ac"))], inc="ac")
        b.emit("scalar", lambda e: e.activation(t16a[:], t16a[:], AF.Ln, bias=np.float32(1.0)),
               waits=[("ac", b.cur("ac"))], inc="ac")
        b.emit("scalar", lambda e: e.activation(t16b[:], ap_col, AF.Relu), waits=w0, inc="ac")
        vac = b.cur("ac")
        b.emit("vector", lambda e: e.tensor_tensor(t16a[:], t16a[:], t16b[:], OP.add),
               waits=[("ac", vac)], inc="dv")
        b.emit("vector", lambda e: e.tensor_scalar(s_c4[:], t16a[:], -4.0, None, OP.mult),
               waits=[("dv", b.cur("dv"))], inc="dv")
        b.emit("vector", lambda e: e.tensor_scalar(
            s_gab2[:], s_bias[:, OFF["gab"]:OFF["gab"] + 16], 0.5, None, OP.mult),
            waits=w0, inc="dv")
        b.emit("vector", lambda e: e.tensor_scalar(
            s_gxb2[:], s_bias[:, OFF["gxb"]:OFF["gxb"] + 16], 0.5, None, OP.mult),
            waits=w0, inc="dv")

        b.emit("vector", lambda e: e.tensor_scalar(
            s_rbh[:], s_bias[:, OFF["rb"]:OFF["rb"] + 16], s_hm[:, 0:1], None, OP.mult),
            waits=w0, inc="dv")

        b.emit("vector", lambda e: e.memset(c_wb[:], 777.0), inc="dv")

        slA = Slot(pA[:])
        slB = Slot(pB[:])
        slD = Slot(pD[:])

        sc_pp = ExitStack()
        ppw = [DBuf(sbt(f"ppw{i}", [128, 2048], BF16, ctx=sc_pp)) for i in range(2)]
        for _i, _bf in enumerate(ppw):
            _bf.sem = f"pw{_i}"
        ppa1 = sbt("ppa1", [128, 2048], BF16, ctx=sc_pp)
        c_pp = sbt("c_pp", [128, 128], F32, ctx=sc_pp)
        b.emit("vector", lambda e: e.memset(c_pp[:], 777.0), inc="dv")
        pp_state = {"n": 0, "pe": None, "afree": None}

        def prepass_slab():
            s = pp_state["n"]
            if s >= 64:
                return
            pp_state["n"] = s + 1
            pb = ppw[s % 2]
            lv = b.emit("gpsimd", lambda e, s=s, pb=pb: e.dma_start(
                out=pb.t[:], in_=wPJ[s * 128:(s + 1) * 128, :]),
                waits=([pb.free] if pb.free else []), inc=pb.sem, k=16)
            av = b.emit("vector", lambda e, pb=pb: e.tensor_scalar(
                ppa1[:].bitcast(U16), pb.t[:].bitcast(U16), 0x7FFF, None,
                OP.bitwise_and),
                waits=[(pb.sem, lv)] + ([pp_state["afree"]] if pp_state["afree"] else []),
                inc="dv")
            pb.free = ("dv", av)
            first = (s == 0)
            last = (s == 63)
            gv = None
            for wi, (pt, pcol) in enumerate([(pC, 0), (pC, 512), (pD, 0), (pD, 512)]):
                gv = b.emit("tensor", lambda e, pt=pt, pcol=pcol, wi=wi, first=first, last=last:
                            e.matmul(pt[:, pcol:pcol + 512], s_idb[:],
                                     ppa1[:, wi * 512:(wi + 1) * 512],
                                     start=first, stop=last),
                            waits=[("dv", av)] if wi == 0 else [],
                            inc="pe" if wi == 3 else None)
            pp_state["pe"] = gv
            pp_state["afree"] = ("pe", gv)

        sc_xn = ExitStack()
        xn = sc_xn.enter_context(nc.sbuf_tensor("xn", [128, NCD * TH], BF16, side="left"))
        sc1 = ExitStack()
        x_fm = sc1.enter_context(nc.sbuf_tensor("x_fm", [128, NCD * TH], F32, side="right"))
        sc1b = ExitStack()
        sq1 = sc1b.enter_context(nc.sbuf_tensor("sq1", [128, NCD * TH], BF16, side="right"))
        xt0 = sc1b.enter_context(nc.sbuf_tensor("xt0", [128, 2048], F32, side="right"))
        xt1 = sc1b.enter_context(nc.sbuf_tensor("xt1", [128, 2048], F32, side="right"))
        xh = sc1b.enter_context(nc.sbuf_tensor("xh", [8, 2048], F32, side="right"))

        x3 = x_fm[:].rearrange("p (c t) -> p c t", c=NCD)
        xtb = [DBuf(xt0), DBuf(xt1)]
        xtb[0].sem, xtb[1].sem = "xt0", "xt1"
        for j in range(8):
            tb = xtb[j % 2]
            lv = b.emit("sync", lambda e, j=j, tb=tb: e.dma_start(
                out=tb.t[:], in_=x_in[HALO + j * 128: HALO + (j + 1) * 128, :]),
                waits=([tb.free] if tb.free else []), inc=tb.sem, k=16)
            for g in range(2):
                slot = slA if (2 * j + g) % 2 == 0 else slB
                gv = None
                for c8 in range(8):
                    c = g * 8 + c8
                    gv = b.emit("tensor", lambda e, slot=slot, c8=c8, tb=tb, c=c:
                                e.matmul(slot.ap[:, c8 * 128:(c8 + 1) * 128],
                                         tb.t[:, c * 128:(c + 1) * 128], s_idf[:],
                                         start=True, stop=True),
                                waits=([(tb.sem, lv)] + slot.wait_free()) if c8 == 0 else [],
                                inc="pe" if c8 == 7 else None)
                dst = x3[:, g * 8:(g + 1) * 8, HALO + j * 128: HALO + (j + 1) * 128]
                fv = b.emit("scalar", lambda e, slot=slot, dst=dst: e.copy(
                    dst, slot.ap[:].rearrange("p (c t) -> p c t", c=8)),
                    waits=[("pe", gv)], inc="ac", pad=True)
                slot.free = ("ac", fv)
            tb.free = ("pe", b.cur("pe"))
        lvh = b.emit("sync", lambda e: e.dma_start(out=xh[:], in_=x_in[0:HALO, :]),
                     inc="xh", k=16)
        slC0 = Slot(pC[:, 0:128])
        gv = None
        for c in range(NCD):
            gv = b.emit("tensor", lambda e, c=c: e.matmul(
                pC[:, c * 8:(c + 1) * 8], xh[0:8, c * 128:(c + 1) * 128],
                s_id8[0:8, 0:8], start=True, stop=True),
                waits=([("xh", lvh)] + slC0.wait_free()) if c == 0 else [],
                inc="pe" if c == NCD - 1 else None)
        fv = b.emit("vector", lambda e: e.tensor_copy(
            x3[:, :, 0:HALO], pC[:, 0:128].rearrange("p (c t) -> p c t", c=NCD)),
            waits=[("pe", gv)], inc="dv", pad=True)
        slC0.free = ("dv", fv)
        acp = b.pad("scalar")
        sp_x = b.emit("sync", lambda e: e.dma_start(out=d_xfm, in_=x_fm[:]),
                      waits=[("ac", acp), ("dv", fv)], inc="sx", k=16)
        sqv = b.emit("scalar", lambda e: e.activation(sq1[:], x_fm[:], AF.Square),
                     waits=[("dv", fv)], inc="ac", pad=True)
        gv = None
        for c in range(NCD):
            first = (c == 0)
            last = (c == NCD - 1)
            gv = b.emit("tensor", lambda e, c=c, first=first, last=last: e.matmul(
                pD[0:1, 0:512], s_onec[:], sq1[:, c * TH: c * TH + 512],
                start=first, stop=last),
                waits=([("ac", sqv)] + slD.wait_free()) if first else [], inc=None)
            gv = b.emit("tensor", lambda e, c=c, first=first, last=last: e.matmul(
                pD[0:1, 512:1024], s_onec[:], sq1[:, c * TH + 512: c * TH + 1024],
                start=first, stop=last), inc=None)
            gv = b.emit("tensor", lambda e, c=c, first=first, last=last: e.matmul(
                pC[0:1, 128:136], s_onec[:], sq1[:, c * TH + 1024: c * TH + 1032],
                start=first, stop=last), inc="pe" if last else None)
        b.emit("scalar", lambda e: e.activation(s_r0[0:1, 0:1024], pD[0:1, 0:1024],
                                                AF.Ln, bias=np.float32(1e-8), scale=np.float32(1.0 / D)),
               waits=[("pe", gv)], inc="ac")
        b.emit("scalar", lambda e: e.activation(s_r0[0:1, 1024:1032], pC[0:1, 128:136],
                                                AF.Ln, bias=np.float32(1e-8), scale=np.float32(1.0 / D)),
               inc="ac")
        rv = b.emit("scalar", lambda e: e.activation(s_r1[0:1, :], s_r0[0:1, :],
                                                     AF.Exp, scale=np.float32(-0.5)),
                    waits=[("ac", b.cur("ac"))], inc="ac", pad=True)
        slD.free = ("ac", rv)
        slC0.free = ("ac", rv)
        gv = None
        for (pcol, width) in [(0, 512), (512, 512)]:
            gv = b.emit("tensor", lambda e, pcol=pcol, width=width: e.matmul(
                pD[:, pcol:pcol + width], s_on1[:], s_r1[0:1, pcol:pcol + width],
                start=True, stop=True),
                waits=([("ac", rv)] + slD.wait_free()) if pcol == 0 else [],
                inc=None)
        gv = b.emit("tensor", lambda e: e.matmul(
            pC[:, 128:136], s_on1[:], s_r1[0:1, 1024:1032], start=True, stop=True),
            waits=slC0.wait_free(), inc="pe")
        bv1 = b.emit("vector", lambda e: e.tensor_copy(s_bc[:, 0:1024], pD[:]),
                     waits=[("pe", gv)], inc="dv")
        bv2 = b.emit("vector", lambda e: e.tensor_copy(s_bc[:, 1024:1032], pC[:, 128:136]),
                     inc="dv", pad=True)
        slD.free = ("dv", bv2)
        slC0.free = ("dv", bv2)
        for c in range(NCD):
            b.emit("vector", lambda e, c=c: e.scalar_tensor_tensor(
                xn[:, c * TH:(c + 1) * TH], x_fm[:, c * TH:(c + 1) * TH],
                bias("n1w", c), s_bc[:], OP.mult, OP.mult),
                waits=[("dv", bv2)] if c == 0 else [], inc="dv")
        b.barrier()
        dump("xn", lambda: ([128, NCD * TH], xn[:]))
        b.dma_fence()
        sc1b.close()

        rp = x_fm
        wR3 = wR.rearrange("(k p) m -> p k m", p=128)

        def rhs_xn(k, wi):
            base = k * TH
            if wi == 0:
                return xn[:, base + HALO: base + HALO + 512]
            return xn[:, base + HALO + 512: base + TH]

        halo_state = {"free": None}

        def extra_halo(m, k, buf, first, last):
            w = ([halo_state["free"]] if (first and halo_state["free"]) else [])
            return b.emit("tensor", lambda e, m=m, k=k, buf=buf, first=first, last=last:
                          e.matmul(pC[:, m * 8:(m + 1) * 8],
                                   buf.t[:, k * 128:(k + 1) * 128],
                                   xn[:, k * TH: k * TH + 8],
                                   start=first, stop=last),
                          waits=w, inc="pe" if last else None)

        def evac_rp(m, slot, gv):
            wt = [("pe", gv), ("sx", sp_x)]
            v1 = b.emit("scalar", lambda e, m=m, slot=slot: e.activation(
                rp[:, m * TH + HALO:(m + 1) * TH], slot.ap[:], AF.Identity,
                bias=bias("rb", m)), waits=wt, inc="ac")
            v2 = b.emit("scalar", lambda e, m=m: e.activation(
                rp[:, m * TH: m * TH + 8], pC[:, m * 8:(m + 1) * 8], AF.Identity,
                bias=s_rbh[:, m:m + 1]), inc="ac")
            pv = b.pad("scalar")
            slot.free = ("ac", pv)
            halo_state["free"] = ("ac", pv)

        stream_mm(wR3, NCD, NCD, rhs_xn, [(0, 512), (512, 512)],
                  [slA, slB], evac_rp, (0, 1), extra=extra_halo)
        b.barrier()
        b.dma_fence()
        dump("rp", lambda: ([128, NCD * TH], rp[:]))

        conv = sc_xn.enter_context(nc.sbuf_tensor("conv", [128, NCD * T], BF16, side="left"))
        sc_ct = ExitStack()
        cta = sc_ct.enter_context(nc.sbuf_tensor("cta", [128, T], F32, side="left"))
        ctb = sc_ct.enter_context(nc.sbuf_tensor("ctb", [128, T], F32, side="left"))
        for c in range(NCD):
            base = c * TH
            b.emit("vector", lambda e, c=c, base=base: e.tensor_scalar(
                cta[:], rp[:, base + 5: base + 5 + T], bias("cw0", c), bias("cb", c),
                OP.mult, OP.add), inc="dv")
            b.emit("vector", lambda e, c=c, base=base: e.scalar_tensor_tensor(
                ctb[:], rp[:, base + 6: base + 6 + T], bias("cw1", c), cta[:],
                OP.mult, OP.add), inc="dv")
            b.emit("vector", lambda e, c=c, base=base: e.scalar_tensor_tensor(
                cta[:], rp[:, base + 7: base + 7 + T], bias("cw2", c), ctb[:],
                OP.mult, OP.add), inc="dv")
            b.emit("vector", lambda e, c=c, base=base: e.scalar_tensor_tensor(
                conv[:, c * T:(c + 1) * T], rp[:, base + 8: base + 8 + T],
                bias("cw3", c), cta[:], OP.mult, OP.add), inc="dv")
        b.barrier()
        dump("conv", lambda: ([128, NCD * T], conv[:]))
        b.dma_fence()
        sc_ct.close()
        sc1.close()

        scH = ExitStack()
        h_bf = scH.enter_context(nc.sbuf_tensor("h_bf", [128, NCD * T], BF16, side="right"))
        sc5 = ExitStack()
        P_bf = sc5.enter_context(nc.sbuf_tensor("P_bf", [128, NCD * T], BF16, side="right"))
        a_st = [sc5.enter_context(nc.sbuf_tensor(f"a_st{i}", [128, T], F32, side="right"))
                for i in range(2)]
        y_st = [sc5.enter_context(nc.sbuf_tensor(f"y_st{i}", [128, T], F32, side="right"))
                for i in range(2)]
        i_st = [sc5.enter_context(nc.sbuf_tensor(f"i_st{i}", [128, T], F32, side="right"))
                for i in range(2)]
        m_st = sc5.enter_context(nc.sbuf_tensor("m_st", [128, T], F32, side="right"))
        tmp5 = sc5.enter_context(nc.sbuf_tensor("tmp5", [128, T], F32, side="right"))

        wGA3 = wGA.rearrange("(k p) m -> p k m", p=128)
        wGX3 = wGX.rearrange("(k p) m -> p k m", p=128)
        tmp5_rd = {"v": None}
        for g0 in range(0, NCD, 2):
            grp = [g0, g0 + 1]
            chain = {}
            for ci, c in enumerate(grp):
                for (wmat, pair, slot) in [(wGA3, (0, 1), slA), (wGX3, (2, 3), slB)]:
                    buf = wb[pair[c % 2]]
                    w3 = buf.t[:].rearrange("p (k j) -> p k j", k=NCD)
                    lv = b.emit("gpsimd", lambda e, wmat=wmat, c=c, w3=w3: e.dma_start(
                        out=w3, in_=wmat[:, :, c * 128:(c + 1) * 128]),
                        waits=([buf.free] if buf.free else []), inc=buf.sem, k=16)
                    gv = None
                    for k in range(NCD):
                        for wi, (pcol, t0) in enumerate([(0, 0), (512, 512)]):
                            gv = b.emit("tensor", lambda e, slot=slot, pcol=pcol, buf=buf, k=k, t0=t0:
                                        e.matmul(slot.ap[:, pcol:pcol + 512],
                                                 buf.t[:, k * 128:(k + 1) * 128],
                                                 conv[:, k * T + t0: k * T + t0 + 512],
                                                 start=(k == 0), stop=(k == NCD - 1)),
                                        waits=([(buf.sem, lv)] + slot.wait_free())
                                        if (k == 0 and wi == 0) else [],
                                        inc="pe" if (k == NCD - 1 and wi == 1) else None)
                    buf.free = ("pe", gv)
                    chain[("mm", c, slot is slB)] = gv
                v = b.emit("scalar", lambda e, c=c: e.activation(
                    tmp5[:], slA.ap[:], AF.Tanh, bias=s_gab2[:, c:c + 1], scale=np.float32(0.5)),
                    waits=[("pe", chain[("mm", c, False)])] +
                    ([("dv", tmp5_rd["v"])] if tmp5_rd["v"] else []), inc="ac", pad=True)
                slA.free = ("ac", v)
                v2 = b.emit("vector", lambda e, c=c: e.tensor_scalar(
                    m_st[:], tmp5[:], s_c4[:, c:c + 1], s_c4[:, c:c + 1], OP.mult, OP.add),
                    waits=[("ac", v)], inc="dv", pad=True)
                v3 = b.emit("scalar", lambda e, ci=ci: e.activation(
                    a_st[ci][:], m_st[:], AF.Exp), waits=[("dv", v2)], inc="ac")
                v4 = b.emit("scalar", lambda e, ci=ci: e.activation(
                    tmp5[:], a_st[ci][:], AF.Square), waits=[("ac", v3)], inc="ac", pad=True)
                v5 = b.emit("vector", lambda e, ci=ci: e.tensor_scalar(
                    y_st[ci][:], tmp5[:], -1.0, 1.0, OP.mult, OP.add),
                    waits=[("ac", v4)], inc="dv")
                tmp5_rd["v"] = v5
                v6 = b.emit("scalar", lambda e, c=c, ci=ci: e.activation(
                    i_st[ci][:], slB.ap[:], AF.Tanh, bias=s_gxb2[:, c:c + 1], scale=np.float32(0.5)),
                    waits=[("pe", chain[("mm", c, True)])], inc="ac", pad=True)
                slB.free = ("ac", v6)
            for ci, c in enumerate(grp):
                v6b = b.emit("vector", lambda e, ci=ci: e.tensor_scalar(
                    m_st[:], y_st[ci][:], 0.0, None, OP.max),
                    waits=[("dv", b.cur("dv")), ("ac", b.cur("ac"))], inc="dv")
                v7 = b.emit("scalar", lambda e, ci=ci: e.activation(
                    y_st[ci][:], m_st[:], AF.Sqrt),
                    waits=[("dv", v6b)], inc="ac", pad=True)
                v8 = b.emit("vector", lambda e, ci=ci: e.tensor_tensor(
                    m_st[:], y_st[ci][:], i_st[ci][:], OP.mult),
                    waits=[("ac", v7)], inc="dv")
                v9 = b.emit("vector", lambda e, ci=ci: e.tensor_tensor(
                    i_st[ci][:], y_st[ci][:], m_st[:], OP.add),
                    waits=[("dv", v8)], inc="dv")
                v10 = b.emit("vector", lambda e, c=c, ci=ci: e.scalar_tensor_tensor(
                    tmp5[:], i_st[ci][:], 0.5, conv[:, c * T:(c + 1) * T],
                    OP.mult, OP.mult), waits=[("dv", v9)], inc="dv")
                v11 = b.emit("vector", lambda e, c=c, ci=ci: e.tensor_tensor_scan(
                    h_bf[:, c * T:(c + 1) * T], a_st[ci][:], tmp5[:], 0.0,
                    OP.mult, OP.add), waits=[("dv", v10)], inc="dv")
                tmp5_rd["v"] = v11
                v12 = b.emit("vector", lambda e, c=c, ci=ci: e.tensor_tensor_scan(
                    P_bf[:, c * T:(c + 1) * T], a_st[ci][:], a_st[ci][:], 1.0,
                    OP.mult, OP.bypass), waits=[("dv", v11)], inc="dv")
                b.emit("vector", lambda e, c=c: e.tensor_copy(
                    s_end[:, c:c + 1], P_bf[:, (c + 1) * T - 1:(c + 1) * T]),
                    waits=[("dv", v12)], inc="dv")
                b.emit("vector", lambda e, c=c: e.tensor_copy(
                    s_end[:, 16 + c:16 + c + 1], h_bf[:, (c + 1) * T - 1:(c + 1) * T]),
                    inc="dv")
        cin3 = cc_in_t.ap().rearrange("a (p c) -> a p c", p=128)
        dvp = b.pad("vector")
        e1 = b.emit("sync", lambda e: e.dma_start(out=cin3[0], in_=s_end[:, 0:16]),
                    waits=[("dv", dvp)], inc="ce", k=16)
        e2 = b.emit("sync", lambda e: e.dma_start(out=cin3[1], in_=s_end[:, 16:32]),
                    inc="ce", k=16)
        ccv = b.emit("gpsimd", lambda e: e.collective_compute(
            "AllGather", OP.bypass, replica_groups=[list(range(NC))],
            ins=[cc_in_t.ap().opt()], outs=[cc_out_t.ap().opt()]),
            waits=[("ce", e2)], inc="cc", k=None)
        dump("h_loc", lambda: ([128, NCD * T], h_bf[:]))
        dump("P_loc", lambda: ([128, NCD * T], P_bf[:]))
        for _ in range(10):
            prepass_slab()

        sc5c = ExitStack()
        s_gath = sc5c.enter_context(nc.sbuf_tensor("s_gath", [128, 2 * NC * 16], F32, side="right"))
        s_hin = sc5c.enter_context(nc.sbuf_tensor("s_hin", [128, 16], F32, side="right"))
        s_t0 = sc5c.enter_context(nc.sbuf_tensor("s_t0", [128, 16], F32, side="right"))
        s_t1 = sc5c.enter_context(nc.sbuf_tensor("s_t1", [128, 16], F32, side="right"))
        gat3 = cc_out_t.ap().rearrange("a (p c) -> a p c", p=128)
        gl = None
        for j in range(2 * NC):
            gl = b.emit("sync", lambda e, j=j: e.dma_start(
                out=s_gath[:, j * 16:(j + 1) * 16], in_=gat3[j]),
                waits=[("cc", 1)] if j == 0 else [], inc="cg", k=16)
        hv = b.emit("vector", lambda e: e.memset(s_hin[:], 0.0), inc="dv")
        for r in range(NC - 1):
            pa = slice((2 * r) * 16, (2 * r + 1) * 16)
            ha = slice((2 * r + 1) * 16, (2 * r + 2) * 16)
            b.emit("vector", lambda e, pa=pa, r=r: e.tensor_scalar(
                s_t0[:], s_gath[:, pa], s_mk[:, r:r + 1], s_mk1[:, r:r + 1],
                OP.mult, OP.add), waits=[("cg", gl), ("dv", hv)], inc="dv")
            b.emit("vector", lambda e, ha=ha, r=r: e.tensor_scalar(
                s_t1[:], s_gath[:, ha], s_mk[:, r:r + 1], None, OP.mult), inc="dv")
            b.emit("vector", lambda e: e.tensor_tensor(s_hin[:], s_t0[:], s_hin[:], OP.mult),
                   inc="dv")
            hv = b.emit("vector", lambda e: e.tensor_tensor(s_hin[:], s_hin[:], s_t1[:], OP.add),
                        inc="dv")
        for c in range(NCD):
            b.emit("vector", lambda e, c=c: e.scalar_tensor_tensor(
                h_bf[:, c * T:(c + 1) * T], P_bf[:, c * T:(c + 1) * T],
                s_hin[:, c:c + 1], h_bf[:, c * T:(c + 1) * T], OP.mult, OP.add),
                waits=[("dv", hv)], inc="dv")
        b.barrier()
        dump("h_fix", lambda: ([128, NCD * T], h_bf[:]))
        b.dma_fence()
        sc5c.close()
        sc5.close()

        sc4 = ExitStack()
        left = sc4.enter_context(nc.sbuf_tensor("left", [128, NCD * T], BF16, side="right"))
        wL3 = wL.rearrange("(k p) m -> p k m", p=128)

        def evac_left(m, slot, gv):
            v = b.emit("scalar", lambda e, m=m, slot=slot: e.activation(
                left[:, m * T:(m + 1) * T], slot.ap[:], AF.Gelu, bias=bias("lb", m)),
                waits=[("pe", gv)], inc="ac")
            slot.free = ("ac", b.pad("scalar"))

        stream_mm(wL3, NCD, NCD, rhs_xn, [(0, 512), (512, 512)],
                  [slA, slB], evac_left, (0, 1),
                  interleave=lambda m: prepass_slab())
        b.barrier()
        dump("left", lambda: ([128, NCD * T], left[:]))
        b.dma_fence()
        sc_xn.close()

        for c in range(NCD):
            b.emit("vector", lambda e, c=c: e.tensor_tensor(
                h_bf[:, c * T:(c + 1) * T], left[:, c * T:(c + 1) * T],
                h_bf[:, c * T:(c + 1) * T], OP.mult), inc="dv")
        b.barrier()

        sc7 = ExitStack()
        x1 = sc7.enter_context(nc.sbuf_tensor("x1", [128, NCD * T], F32, side="left"))
        xre = [DBuf(sc7.enter_context(nc.sbuf_tensor(f"xre{i}", [128, T], F32, side="left")))
               for i in range(2)]
        xre[0].sem, xre[1].sem = "xr0", "xr1"
        sst = [sc7.enter_context(nc.sbuf_tensor(f"sst{i}", [128, T], F32, side="left"))
               for i in range(2)]
        sst_free = [None, None]
        wO3 = wO.rearrange("(k p) m -> p k m", p=128)

        def rhs_lr(k, wi):
            return h_bf[:, k * T + wi * 512: k * T + wi * 512 + 512]

        def evac_x1(m, slot, gv):
            xb = xre[m % 2]
            lv = b.emit("sync", lambda e, m=m, xb=xb: e.dma_start(
                out=xb.t[:], in_=d_xfm[:, m * TH + HALO:(m + 1) * TH]),
                waits=([xb.free] if xb.free else []), inc=xb.sem, k=16)
            v1 = b.emit("vector", lambda e, m=m, slot=slot, xb=xb: e.scalar_tensor_tensor(
                x1[:, m * T:(m + 1) * T], slot.ap[:], bias("ob", m), xb.t[:],
                OP.add, OP.add), waits=[("pe", gv), (xb.sem, lv)], inc="dv")
            slot.free = ("dv", b.pad("vector"))
            sbuf = sst[m % 2]
            wt = [("dv", v1)]
            if sst_free[m % 2]:
                wt.append(sst_free[m % 2])
            v2 = b.emit("vector", lambda e, m=m, sbuf=sbuf, xb=xb: e.tensor_tensor(
                sbuf[:], x1[:, m * T:(m + 1) * T], xb.t[:], OP.add),
                waits=wt, inc="dv", pad=True)
            xb.free = ("dv", v2)
            sv = b.emit("sync", lambda e, m=m, sbuf=sbuf: e.dma_start(
                out=d_S[:, m * T:(m + 1) * T], in_=sbuf[:]),
                waits=[("dv", v2)], inc=f"ss{m % 2}", k=16)
            sst_free[m % 2] = (f"ss{m % 2}", sv)

        stream_mm(wO3, NCD, NCD, rhs_lr, [(0, 512), (512, 512)],
                  [slA, slB], evac_x1, (0, 1),
                  interleave=lambda m: prepass_slab())
        while pp_state["n"] < 64:
            prepass_slab()
        b.barrier()
        dump("x1", lambda: ([128, NCD * T], x1[:]))
        b.dma_fence()
        sc4.close()
        scH.close()

        sc8 = ExitStack()
        sq8 = sc8.enter_context(nc.sbuf_tensor("sq8", [128, NCD * T], BF16, side="left"))
        sc_x1n = ExitStack()
        x1n = sc_x1n.enter_context(nc.sbuf_tensor("x1n", [128, NCD * T], BF16, side="right"))
        sqv = b.emit("scalar", lambda e: e.activation(sq8[:], x1[:], AF.Square),
                     waits=[("dv", b.cur("dv"))], inc="ac", pad=True)
        gv = None
        for c in range(NCD):
            first = (c == 0)
            last = (c == NCD - 1)
            b.emit("tensor", lambda e, c=c, first=first, last=last: e.matmul(
                pA[0:1, 0:512], s_onec[:], sq8[:, c * T: c * T + 512],
                start=first, stop=last),
                waits=([("ac", sqv)] + slA.wait_free()) if first else [], inc=None)
            gv = b.emit("tensor", lambda e, c=c, first=first, last=last: e.matmul(
                pA[0:1, 512:1024], s_onec[:], sq8[:, c * T + 512: c * T + 1024],
                start=first, stop=last), inc="pe" if last else None)
        b.emit("scalar", lambda e: e.activation(s_r0[0:1, 0:1024], pA[0:1, 0:1024],
                                                AF.Ln, bias=np.float32(1e-8), scale=np.float32(1.0 / D)),
               waits=[("pe", gv)], inc="ac")
        rv = b.emit("scalar", lambda e: e.activation(s_r1[0:1, 0:1024], s_r0[0:1, 0:1024],
                                                     AF.Exp, scale=np.float32(-0.5)),
                    waits=[("ac", b.cur("ac"))], inc="ac", pad=True)
        slA.free = ("ac", rv)
        gv = None
        for (pcol, width) in [(0, 512), (512, 512)]:
            gv = b.emit("tensor", lambda e, pcol=pcol, width=width: e.matmul(
                pB[:, pcol:pcol + width], s_on1[:], s_r1[0:1, pcol:pcol + width],
                start=True, stop=True),
                waits=([("ac", rv)] + slB.wait_free()) if pcol == 0 else [],
                inc="pe" if pcol == 512 else None)
        bv = b.emit("vector", lambda e: e.tensor_copy(s_bc[:, 0:1024], pB[:]),
                    waits=[("pe", gv)], inc="dv", pad=True)
        slB.free = ("dv", bv)
        for c in range(NCD):
            b.emit("vector", lambda e, c=c: e.scalar_tensor_tensor(
                x1n[:, c * T:(c + 1) * T], x1[:, c * T:(c + 1) * T],
                bias("n2w", c), s_bc[:, 0:1024], OP.mult, OP.mult),
                waits=[("dv", bv)] if c == 0 else [], inc="dv")
        b.barrier()
        dump("x1n", lambda: ([128, NCD * T], x1n[:]))
        dump("c_pp", lambda: ([128, 128], c_pp[:]))
        b.dma_fence()
        sc8.close()
        sc7.close()
        sc_pp.close()

        sc_prod = ExitStack()
        prod = sc_prod.enter_context(nc.sbuf_tensor("prod", [128, NCF * T], BF16, side="left"))
        sc_ps = ExitStack()
        s_pjs = sc_ps.enter_context(nc.sbuf_tensor("s_pjs", [128, 2048], BF16, side="right"))
        s_pjr = sc_ps.enter_context(nc.sbuf_tensor("s_pjr", [128, 2048], F32, side="right"))
        sc9 = ExitStack()
        u1s = [sc9.enter_context(nc.sbuf_tensor(f"u1s{i}", [128, T], BF16, side="right"))
               for i in range(2)]
        u2s = [sc9.enter_context(nc.sbuf_tensor(f"u2s{i}", [128, T], BF16, side="right"))
               for i in range(2)]
        pqw = [DBuf(sc9.enter_context(nc.sbuf_tensor(f"pqw{i}", [128, 2048], BF16, side="right")))
               for i in range(2)]
        pqw[0].sem, pqw[1].sem = "pq0", "pq1"
        pqt = [sc9.enter_context(nc.sbuf_tensor(f"pqt{i}", [128, 2048], BF16, side="right"))
               for i in range(2)]

        fv1 = b.emit("vector", lambda e: e.tensor_scalar(
            s_pjs[:, 0:1024], pC[:], 1.0 / 64, 1e-8, OP.mult, OP.max),
            waits=[("pe", pp_state["pe"])], inc="dv")
        fv2 = b.emit("vector", lambda e: e.tensor_scalar(
            s_pjs[:, 1024:2048], pD[:], 1.0 / 64, 1e-8, OP.mult, OP.max), inc="dv", pad=True)
        b.emit("vector", lambda e: e.reciprocal(s_pjr[:, 0:1024], s_pjs[:, 0:1024]),
               waits=[("dv", fv2)], inc="dv")
        rcv = b.emit("vector", lambda e: e.reciprocal(s_pjr[:, 1024:2048], s_pjs[:, 1024:2048]),
                     inc="dv")
        slC = Slot(pC[:])
        slC.free = ("dv", fv2)
        slD2 = Slot(pD[:])
        slD2.free = ("dv", fv2)

        pq_ctr = {"n": 0}

        def proj_quant_slab():
            s = pq_ctr["n"]
            if s >= 64:
                return
            pq_ctr["n"] = s + 1
            pb = pqw[s % 2]
            lv = b.emit("gpsimd", lambda e, s=s, pb=pb: e.dma_start(
                out=pb.t[:], in_=wPJ[s * 128:(s + 1) * 128, :]),
                waits=([pb.free] if pb.free else []), inc=pb.sem, k=16)
            tq = pqt[s % 2]
            v1 = b.emit("vector", lambda e, pb=pb, tq=tq: e.tensor_tensor(
                tq[:], pb.t[:], s_pjr[:], OP.mult),
                waits=[(pb.sem, lv), ("dv", rcv)], inc="dv")
            v2 = b.emit("vector", lambda e, pb=pb, tq=tq: e.tensor_scalar(
                pb.t[:], tq[:], 160.0, None, OP.add), waits=[("dv", v1)], inc="dv")
            v3 = b.emit("vector", lambda e, pb=pb, tq=tq: e.tensor_scalar(
                tq[:], pb.t[:], -160.0, 1.0, OP.add, OP.min), waits=[("dv", v2)], inc="dv")
            v4 = b.emit("vector", lambda e, pb=pb, tq=tq: e.scalar_tensor_tensor(
                pb.t[:], tq[:], -1.0, s_pjs[:], OP.max, OP.mult),
                waits=[("dv", v3)], inc="dv", pad=True)
            sv = b.emit("sync", lambda e, s=s, pb=pb: e.dma_start(
                out=d_wqp[s * 128:(s + 1) * 128, :], in_=pb.t[:]),
                waits=[("dv", v4)], inc=f"ps{s % 2}", k=16)
            pb.free = (f"ps{s % 2}", sv)

        wU13 = wU1.rearrange("(k p) m -> p k m", p=128)
        wU23 = wU2.rearrange("(k p) m -> p k m", p=128)
        u_slots1 = [slA, slC]
        u_slots2 = [slB, slD2]
        u1_free = [None, None]
        for fc in range(NCF):
            gv1 = gv2 = None
            slot1 = slot2 = None
            for (wmat, pair, slots) in [(wU13, (0, 1), u_slots1), (wU23, (2, 3), u_slots2)]:
                buf = wb[pair[fc % 2]]
                w3 = buf.t[:].rearrange("p (k j) -> p k j", k=NCD)
                lv = b.emit("gpsimd", lambda e, wmat=wmat, fc=fc, w3=w3: e.dma_start(
                    out=w3, in_=wmat[:, :, fc * 128:(fc + 1) * 128]),
                    waits=([buf.free] if buf.free else []), inc=buf.sem, k=16)
                slot = slots[fc % 2]
                gv = None
                for k in range(NCD):
                    for wi in range(2):
                        gv = b.emit("tensor", lambda e, slot=slot, buf=buf, k=k, wi=wi:
                                    e.matmul(slot.ap[:, wi * 512:(wi + 1) * 512],
                                             buf.t[:, k * 128:(k + 1) * 128],
                                             x1n[:, k * T + wi * 512: k * T + wi * 512 + 512],
                                             start=(k == 0), stop=(k == NCD - 1)),
                                    waits=([(buf.sem, lv)] + slot.wait_free())
                                    if (k == 0 and wi == 0) else [],
                                    inc="pe" if (k == NCD - 1 and wi == 1) else None)
                buf.free = ("pe", gv)
                if wmat is wU13:
                    gv1, slot1 = gv, slot
                else:
                    gv2, slot2 = gv, slot
            ub1 = u1s[fc % 2]
            ub2 = u2s[fc % 2]
            v1 = b.emit("scalar", lambda e, fc=fc, slot1=slot1, ub1=ub1: e.activation(
                ub1[:], slot1.ap[:], AF.Gelu,
                bias=s_bias[:, OFF["u1b"] + fc:OFF["u1b"] + fc + 1]),
                waits=[("pe", gv1)] + ([u1_free[fc % 2]] if u1_free[fc % 2] else []),
                inc="ac", pad=True)
            slot1.free = ("ac", v1)
            v2 = b.emit("vector", lambda e, fc=fc, slot2=slot2, ub2=ub2: e.tensor_scalar(
                ub2[:], slot2.ap[:], s_bias[:, OFF["u2b"] + fc:OFF["u2b"] + fc + 1], None,
                OP.add), waits=[("pe", gv2)], inc="dv")
            slot2.free = ("dv", b.pad("vector"))
            pv = b.emit("vector", lambda e, fc=fc, ub1=ub1, ub2=ub2: e.tensor_tensor(
                prod[:, fc * T:(fc + 1) * T], ub1[:], ub2[:], OP.mult),
                waits=[("ac", v1), ("dv", v2)], inc="dv")
            u1_free[fc % 2] = ("dv", pv)
            proj_quant_slab()
            if fc % 3 == 0:
                proj_quant_slab()
        while pq_ctr["n"] < 64:
            proj_quant_slab()
        b.barrier()
        dump("prod", lambda: ([128, NCF * T], prod[:]))
        b.dma_fence()
        sc9.close()
        sc_ps.close()
        sc_x1n.close()

        sc_x2 = ExitStack()
        x2bf = sc_x2.enter_context(nc.sbuf_tensor("x2bf", [128, NCD * T], BF16, side="right"))
        sc10 = ExitStack()
        lnin = sc10.enter_context(nc.sbuf_tensor("lnin", [128, NCD * 256], F32, side="right"))
        sqa = [sc10.enter_context(nc.sbuf_tensor(f"sqa{i}", [128, 256], BF16, side="right"))
               for i in range(2)]
        sqb = [sc10.enter_context(nc.sbuf_tensor(f"sqb{i}", [128, 256], BF16, side="right"))
               for i in range(2)]
        sre = [DBuf(sc10.enter_context(nc.sbuf_tensor(f"sre{i}", [128, 256], F32, side="right")))
               for i in range(2)]
        sre[0].sem, sre[1].sem = "sr0", "sr1"
        x2t = [sc10.enter_context(nc.sbuf_tensor(f"x2t{i}", [128, 256], F32, side="right"))
               for i in range(2)]
        x2t_free = [None, None]
        bcm = sc10.enter_context(nc.sbuf_tensor("bcm", [128, 256], F32, side="right"))
        bcr = sc10.enter_context(nc.sbuf_tensor("bcr", [128, 256], F32, side="right"))

        psum_of = {0: pA, 1: pB, 2: pC, 3: pD}
        slot4 = (slA, slB, slC, slD2)
        for tg in range(4):
            mm_done = None
            for k in range(NCF):
                buf = wb[k % 2]
                lv = b.emit("gpsimd", lambda e, k=k, buf=buf: e.dma_start(
                    out=buf.t[:], in_=wDN[k * 128:(k + 1) * 128, :]),
                    waits=([buf.free] if buf.free else []), inc=buf.sem, k=16)
                gv = None
                for m in range(NCD):
                    pt = psum_of[m // 4]
                    pcol = (m % 4) * 256
                    gv = b.emit("tensor", lambda e, pt=pt, pcol=pcol, buf=buf, m=m, k=k, tg=tg:
                                e.matmul(pt[:, pcol:pcol + 256],
                                         buf.t[:, m * 128:(m + 1) * 128],
                                         prod[:, k * T + tg * 256: k * T + tg * 256 + 256],
                                         start=(k == 0 and m % 2 == 0),
                                         stop=(k == NCF - 1),
                                         skip_group_check=True),
                                waits=([(buf.sem, lv)] +
                                       (slA.wait_free() + slB.wait_free() +
                                        slC.wait_free() + slD2.wait_free()
                                        if k == 0 else []))
                                if (m == 0) else [],
                                inc="pe" if m == NCD - 1 else None)
                buf.free = ("pe", gv)
                mm_done = gv
            evv = None
            for m in range(NCD):
                pt = psum_of[m // 4]
                pcol = (m % 4) * 256
                sb_ = sre[m % 2]
                lv = b.emit("sync", lambda e, m=m, tg=tg, sb_=sb_: e.dma_start(
                    out=sb_.t[:], in_=d_S[:, m * T + tg * 256: m * T + tg * 256 + 256]),
                    waits=([sb_.free] if sb_.free else []), inc=sb_.sem, k=16)
                evv = b.emit("vector", lambda e, m=m, pt=pt, pcol=pcol, sb_=sb_:
                             e.scalar_tensor_tensor(
                                 lnin[:, m * 256:(m + 1) * 256], pt[:, pcol:pcol + 256],
                                 bias("dnb", m), sb_.t[:], OP.add, OP.add),
                             waits=[("pe", mm_done), (sb_.sem, lv)], inc="dv",
                             pad=(m == NCD - 1))
                sb_.free = ("dv", evv)
            gv = None
            sq_rd = None
            for c in range(NCD):
                first = (c == 0)
                last = (c == NCD - 1)
                sa_ = sqa[c % 2]
                sb2 = sqb[c % 2]
                cvv = b.emit("vector", lambda e, c=c, sa_=sa_: e.tensor_copy(
                    sa_[:], lnin[:, c * 256:(c + 1) * 256]),
                    waits=([("dv", evv)] if first else []) +
                    ([sq_rd] if sq_rd else []), inc="dv", pad=True)
                sqv = b.emit("scalar", lambda e, sa_=sa_, sb2=sb2: e.activation(
                    sb2[:], sa_[:], AF.Square), waits=[("dv", cvv)], inc="ac", pad=True)
                b.emit("tensor", lambda e, c=c, sa_=sa_, first=first, last=last: e.matmul(
                    pA[0:1, 0:256], s_onec[:], sa_[:], start=first, stop=last),
                    waits=([("dv", cvv)] + (slA.wait_free() if first else [])), inc=None)
                gv = b.emit("tensor", lambda e, c=c, sb2=sb2, first=first, last=last: e.matmul(
                    pA[0:1, 512:768], s_onec[:], sb2[:], start=first, stop=last),
                    waits=[("ac", sqv)], inc="pe")
                sq_rd = ("pe", gv)
            vv1 = b.emit("vector", lambda e: e.tensor_scalar(
                s_r0[0:1, 0:256], pA[0:1, 0:256], 1.0 / D, None, OP.mult),
                waits=[("pe", gv)], inc="dv", pad=True)
            vv2 = b.emit("scalar", lambda e: e.activation(s_r0[0:1, 256:512],
                                                    s_r0[0:1, 0:256], AF.Square),
                   waits=[("dv", vv1)], inc="ac", pad=True)
            vv3 = b.emit("vector", lambda e: e.scalar_tensor_tensor(
                s_r1[0:1, 0:256], pA[0:1, 512:768], 1.0 / D, s_r0[0:1, 256:512],
                OP.mult, OP.subtract), waits=[("ac", vv2)], inc="dv")
            vv3 = b.emit("vector", lambda e: e.tensor_scalar(
                s_r1[0:1, 0:256], s_r1[0:1, 0:256], 0.0, None, OP.max),
                inc="dv", pad=True)
            vv4 = b.emit("scalar", lambda e: e.activation(
                s_r1[0:1, 256:512], s_r1[0:1, 0:256], AF.Ln, bias=np.float32(1e-5)),
                waits=[("dv", vv3)], inc="ac")
            rvv = b.emit("scalar", lambda e: e.activation(
                s_r1[0:1, 512:768], s_r1[0:1, 256:512], AF.Exp, scale=np.float32(-0.5)),
                waits=[("ac", b.cur("ac"))], inc="ac", pad=True)
            slA.free = ("ac", rvv)
            g1 = b.emit("tensor", lambda e: e.matmul(
                pB[:, 0:256], s_on1[:], s_r0[0:1, 0:256], start=True, stop=True),
                waits=[("ac", rvv)] + slB.wait_free(), inc=None)
            g2 = b.emit("tensor", lambda e: e.matmul(
                pB[:, 256:512], s_on1[:], s_r1[0:1, 512:768], start=True, stop=True),
                inc="pe")
            bv1 = b.emit("vector", lambda e: e.tensor_copy(bcm[:], pB[:, 0:256]),
                         waits=[("pe", g2)], inc="dv")
            bv2 = b.emit("vector", lambda e: e.tensor_copy(bcr[:], pB[:, 256:512]),
                         inc="dv", pad=True)
            slB.free = ("dv", bv2)
            apv = None
            for m in range(NCD):
                xt = x2t[m % 2]
                wt = [("dv", bv2)]
                if x2t_free[m % 2]:
                    wt.append(x2t_free[m % 2])
                b.emit("vector", lambda e, m=m, xt=xt: e.tensor_tensor(
                    xt[:], lnin[:, m * 256:(m + 1) * 256], bcm[:], OP.subtract),
                    waits=wt, inc="dv")
                apv = b.emit("vector", lambda e, m=m, xt=xt: e.tensor_tensor(
                    xt[:], xt[:], bcr[:], OP.mult), inc="dv", pad=True)
                sv = b.emit("sync", lambda e, m=m, tg=tg, xt=xt: e.dma_start(
                    out=d_x2[:, m * T + tg * 256: m * T + tg * 256 + 256], in_=xt[:]),
                    waits=[("dv", apv)], inc=f"xs{m % 2}", k=16)
                apv = b.emit("vector", lambda e, m=m, tg=tg, xt=xt: e.tensor_copy(
                    x2bf[:, m * T + tg * 256: m * T + tg * 256 + 256], xt[:]),
                    inc="dv")
                x2t_free[m % 2] = (f"xs{m % 2}", sv)
            for s_ in slot4:
                s_.free = ("dv", apv)
        b.barrier()
        dump("lnin3", lambda: ([128, NCD * 256], lnin[:]))
        dump("bcm3", lambda: ([128, 256], bcm[:]))
        dump("bcr3", lambda: ([128, 256], bcr[:]))
        dump("x2", lambda: ([128, NCD * T], x2bf[:]))
        b.dma_fence()
        sc10.close()
        sc_prod.close()

        sc11 = ExitStack()
        hbf = sc11.enter_context(nc.sbuf_tensor("hbf", [128, NCHF * T], BF16, side="left"))
        sc11b = ExitStack()
        qs = [sc11b.enter_context(nc.sbuf_tensor(f"qs{i}", [128, 128], BF16, side="right"))
              for i in range(2)]
        qr = [sc11b.enter_context(nc.sbuf_tensor(f"qr{i}", [128, 128], F32, side="right"))
              for i in range(2)]
        q32s = [sc11b.enter_context(nc.sbuf_tensor(f"q32s{i}", [32, 128], BF16, side="right"))
                for i in range(2)]
        q32r = [sc11b.enter_context(nc.sbuf_tensor(f"q32r{i}", [32, 128], F32, side="right"))
                for i in range(2)]
        gs_st = sc11b.enter_context(nc.sbuf_tensor("gs_st", [128, 512], F32, side="right"))
        qt4 = [DBuf(sc11b.enter_context(nc.sbuf_tensor(f"qt4_{i}", [128, 2048], BF16, side="right")))
               for i in range(2)]
        qw = [wb[0], wb[1], wb[2], wb[3]]
        qt = qt4
        wGU3 = wGU.rearrange("(k p) h -> p k h", p=128)
        qctr = {"i": 0}
        qfree = {}

        def quant_block(blk):
            i = qctr["i"] % 4
            qctr["i"] += 1
            pb = qw[i]
            tqb = qt[i % 2]
            tq = tqb.t
            lv = b.emit("gpsimd", lambda e, blk=blk, pb=pb: e.dma_start(
                out=pb.t[:].rearrange("p (k j) -> p k j", k=NCD),
                in_=wGU3[:, :, blk * 128:(blk + 1) * 128]),
                waits=([pb.free] if pb.free else []), inc=pb.sem, k=16)
            av = b.emit("vector", lambda e, pb=pb, tq=tq: e.tensor_scalar(
                tq[:].bitcast(U16), pb.t[:].bitcast(U16), 0x7FFF, None, OP.bitwise_and),
                waits=[(pb.sem, lv)] + ([tqb.free] if tqb.free else []), inc="dv", pad=True)
            pgs = pC if i % 2 == 0 else pD
            prev = qfree.get(("gs", i % 2))
            gv = None
            for k in range(NCD):
                gv = b.emit("tensor", lambda e, pgs=pgs, tq=tq, k=k: e.matmul(
                    pgs[0:32, 0:128], s_vG[:], tq[:, k * 128:(k + 1) * 128],
                    start=(k == 0), stop=(k == NCD - 1)),
                    waits=([("dv", av)] + ([prev] if prev else [])) if k == 0 else [],
                    inc="pe" if k == NCD - 1 else None)
            i2 = i % 2
            v1 = b.emit("vector", lambda e, pgs=pgs, i2=i2: e.tensor_scalar(
                q32s[i2][:], pgs[0:32, 0:128], 1.0 / 64, 1e-8, OP.mult, OP.max),
                waits=[("pe", gv)], inc="dv")
            v2 = b.emit("vector", lambda e, i2=i2: e.reciprocal(q32r[i2][:], q32s[i2][:]),
                        waits=[("dv", v1)], inc="dv", pad=True)
            qfree[("gs", i2)] = ("dv", v2)
            g1 = b.emit("tensor", lambda e, pgs=pgs, i2=i2: e.matmul(
                pgs[:, 512:640], s_vRb[0:32, :], q32s[i2][:], start=True, stop=True),
                waits=[("dv", v2)] + ([qfree[("rep", i2)]] if ("rep", i2) in qfree else []),
                inc=None)
            g2 = b.emit("tensor", lambda e, pgs=pgs, i2=i2: e.matmul(
                pgs[:, 640:768], s_vRf[0:32, :], q32r[i2][:], start=True, stop=True),
                inc="pe")
            r1 = b.emit("vector", lambda e, pgs=pgs, i2=i2: e.tensor_copy(
                qs[i2][:], pgs[:, 512:640]), waits=[("pe", g2)], inc="dv")
            r2 = b.emit("vector", lambda e, pgs=pgs, i2=i2: e.tensor_copy(
                qr[i2][:], pgs[:, 640:768]), inc="dv", pad=True)
            qfree[("rep", i2)] = ("dv", r2)
            s3b = qs[i2][:].rearrange("p (o j) -> p o j", o=1).broadcast_to([128, NCD, 128])
            r3b = qr[i2][:].rearrange("p (o j) -> p o j", o=1).broadcast_to([128, NCD, 128])
            w3v = pb.t[:].rearrange("p (k j) -> p k j", k=NCD)
            t3v = tq[:].rearrange("p (k j) -> p k j", k=NCD)
            v3 = b.emit("vector", lambda e, t3v=t3v, w3v=w3v, r3b=r3b: e.tensor_tensor(
                t3v, w3v, r3b, OP.mult), waits=[("dv", r2), ("pe", gv)], inc="dv")
            v4 = b.emit("vector", lambda e, pb=pb, tq=tq: e.tensor_scalar(
                pb.t[:], tq[:], 160.0, None, OP.add), waits=[("dv", v3)], inc="dv")
            v5 = b.emit("vector", lambda e, pb=pb, tq=tq: e.tensor_scalar(
                tq[:], pb.t[:], -160.0, 1.0, OP.add, OP.min), waits=[("dv", v4)], inc="dv")
            v6 = b.emit("vector", lambda e, pb=pb, t3v=t3v, s3b=s3b:
                        e.scalar_tensor_tensor(
                            pb.t[:].rearrange("p (k j) -> p k j", k=NCD), t3v, -1.0,
                            s3b, OP.max, OP.mult), waits=[("dv", v5)], inc="dv", pad=True)
            tqb.free = ("dv", v6)
            return pb, v6

        qpend = {0: quant_block(0), 1: quant_block(64)}
        for j in range(NCHF):
            bg, vg = qpend[0]
            gv1 = None
            for k in range(NCD):
                for wi in range(2):
                    gv1 = b.emit("tensor", lambda e, bg=bg, k=k, wi=wi: e.matmul(
                        pA[:, wi * 512:(wi + 1) * 512], bg.t[:, k * 128:(k + 1) * 128],
                        x2bf[:, k * T + wi * 512: k * T + wi * 512 + 512],
                        start=(k == 0), stop=(k == NCD - 1)),
                        waits=([("dv", vg)] + slA.wait_free()) if (k == 0 and wi == 0) else [],
                        inc="pe" if (k == NCD - 1 and wi == 1) else None)
            bg.free = ("pe", gv1)
            if j + 1 < NCHF:
                qpend[0] = quant_block(j + 1)
            bu, vu = qpend[1]
            gv2 = None
            for k in range(NCD):
                for wi in range(2):
                    gv2 = b.emit("tensor", lambda e, bu=bu, k=k, wi=wi: e.matmul(
                        pB[:, wi * 512:(wi + 1) * 512], bu.t[:, k * 128:(k + 1) * 128],
                        x2bf[:, k * T + wi * 512: k * T + wi * 512 + 512],
                        start=(k == 0), stop=(k == NCD - 1)),
                        waits=([("dv", vu)] + slB.wait_free()) if (k == 0 and wi == 0) else [],
                        inc="pe" if (k == NCD - 1 and wi == 1) else None)
            bu.free = ("pe", gv2)
            if j + 1 < NCHF:
                qpend[1] = quant_block(64 + j + 1)
            hv = qfree.get("hv")
            for wi in range(2):
                sv = b.emit("scalar", lambda e, wi=wi: e.activation(
                    gs_st[:], pA[:, wi * 512:(wi + 1) * 512], AF.Silu),
                    waits=([("pe", gv1), ("pe", gv2)] if wi == 0 else []) +
                    ([("dv", hv)] if hv else []), inc="ac", pad=True)
                hv = b.emit("vector", lambda e, j=j, wi=wi: e.tensor_tensor(
                    hbf[:, j * T + wi * 512: j * T + wi * 512 + 512], gs_st[:],
                    pB[:, wi * 512:(wi + 1) * 512], OP.mult),
                    waits=[("ac", sv)], inc="dv")
            qfree["hv"] = hv
            pvv = b.pad("vector")
            slA.free = ("dv", pvv)
            slB.free = ("dv", pvv)
        b.barrier()
        dump("hffn", lambda: ([128, NCHF * T], hbf[:]))
        b.dma_fence()
        sc11b.close()
        sc_x2.close()

        sq12 = sc11.enter_context(nc.sbuf_tensor("sq12", [128, T], BF16, side="right"))
        bcf = sc11.enter_context(nc.sbuf_tensor("bcf", [128, T], BF16, side="right"))
        gv = None
        for c in range(NCHF):
            sv = b.emit("scalar", lambda e, c=c: e.activation(
                sq12[:], hbf[:, c * T:(c + 1) * T], AF.Square),
                waits=([("pe", gv)] if gv else []) +
                (slC.wait_free() if c == 0 else []), inc="ac", pad=True)
            for wi in range(2):
                gv = b.emit("tensor", lambda e, c=c, wi=wi: e.matmul(
                    pC[0:1, wi * 512:(wi + 1) * 512], s_onec[:],
                    sq12[:, wi * 512:(wi + 1) * 512],
                    start=(c == 0), stop=(c == NCHF - 1)),
                    waits=[("ac", sv)] if wi == 0 else [],
                    inc="pe" if (wi == 1) else None)
        b.emit("scalar", lambda e: e.activation(
            s_r0[0:1, 0:1024], pC[0:1, 0:1024], AF.Ln, bias=np.float32(1.1920929e-07),
            scale=np.float32(1.0 / HF)), waits=[("pe", gv)], inc="ac")
        rv = b.emit("scalar", lambda e: e.activation(
            s_r1[0:1, 0:1024], s_r0[0:1, 0:1024], AF.Exp, scale=np.float32(-0.5)),
            waits=[("ac", b.cur("ac"))], inc="ac", pad=True)
        g1 = b.emit("tensor", lambda e: e.matmul(
            pC[:, 0:512], s_on1[:], s_r1[0:1, 0:512], start=True, stop=True),
            waits=[("ac", rv)], inc=None)
        g2 = b.emit("tensor", lambda e: e.matmul(
            pC[:, 512:1024], s_on1[:], s_r1[0:1, 512:1024], start=True, stop=True),
            inc="pe")
        bv = b.emit("vector", lambda e: e.tensor_copy(bcf[:], pC[:]),
                    waits=[("pe", g2)], inc="dv", pad=True)
        slC.free = ("dv", bv)
        for c in range(NCHF):
            b.emit("vector", lambda e, c=c: e.tensor_tensor(
                hbf[:, c * T:(c + 1) * T], bcf[:], hbf[:, c * T:(c + 1) * T], OP.mult),
                waits=[("dv", bv)] if c == 0 else [], inc="dv")
        b.barrier()
        dump("hn", lambda: ([128, NCHF * T], hbf[:]))
        b.dma_fence()

        sc13 = ExitStack()
        lnin2 = sc13.enter_context(nc.sbuf_tensor("lnin2", [128, NCD * 256], F32, side="right"))
        x2re = [DBuf(sc13.enter_context(nc.sbuf_tensor(f"x2re{i}", [128, 256], F32, side="right")))
                for i in range(2)]
        x2re[0].sem, x2re[1].sem = "xr0", "xr1"
        sqa2 = [sc13.enter_context(nc.sbuf_tensor(f"sqa2{i}", [128, 256], BF16, side="right"))
                for i in range(2)]
        sqb2 = [sc13.enter_context(nc.sbuf_tensor(f"sqb2{i}", [128, 256], BF16, side="right"))
                for i in range(2)]
        bcm2 = sc13.enter_context(nc.sbuf_tensor("bcm2", [128, 256], F32, side="right"))
        bcr2 = sc13.enter_context(nc.sbuf_tensor("bcr2", [128, 256], F32, side="right"))
        otm = sc13.enter_context(nc.sbuf_tensor("otm", [128, 2048], F32, side="right"))
        otm_free = None
        ttmp = sc13.enter_context(nc.sbuf_tensor("ttmp", [128, 256], F32, side="right"))

        for tg in range(4):
            mm_done = None
            for k in range(NCHF):
                buf = wb[k % 2]
                lv = b.emit("gpsimd", lambda e, k=k, buf=buf: e.dma_start(
                    out=buf.t[:], in_=d_wqp[k * 128:(k + 1) * 128, :]),
                    waits=([buf.free] if buf.free else []), inc=buf.sem, k=16)
                gv = None
                for m in range(NCD):
                    pt = psum_of[m // 4]
                    pcol = (m % 4) * 256
                    gv = b.emit("tensor", lambda e, pt=pt, pcol=pcol, buf=buf, m=m, k=k, tg=tg:
                                e.matmul(pt[:, pcol:pcol + 256],
                                         buf.t[:, m * 128:(m + 1) * 128],
                                         hbf[:, k * T + tg * 256: k * T + tg * 256 + 256],
                                         start=(k == 0 and m % 2 == 0),
                                         stop=(k == NCHF - 1),
                                         skip_group_check=True),
                                waits=([(buf.sem, lv)] +
                                       (slA.wait_free() + slB.wait_free() +
                                        slC.wait_free() + slD2.wait_free()
                                        if k == 0 else []))
                                if m == 0 else [],
                                inc="pe" if m == NCD - 1 else None)
                buf.free = ("pe", gv)
                mm_done = gv
            evv = None
            for m in range(NCD):
                pt = psum_of[m // 4]
                pcol = (m % 4) * 256
                xb = x2re[m % 2]
                lv = b.emit("sync", lambda e, m=m, tg=tg, xb=xb: e.dma_start(
                    out=xb.t[:], in_=d_x2[:, m * T + tg * 256: m * T + tg * 256 + 256]),
                    waits=([xb.free] if xb.free else []), inc=xb.sem, k=16)
                evv = b.emit("vector", lambda e, m=m, pt=pt, pcol=pcol, xb=xb:
                             e.tensor_tensor(lnin2[:, m * 256:(m + 1) * 256],
                                             pt[:, pcol:pcol + 256], xb.t[:], OP.add),
                             waits=[("pe", mm_done), (xb.sem, lv)], inc="dv",
                             pad=(m == NCD - 1))
                xb.free = ("dv", evv)
            gv = None
            sq_rd = None
            for c in range(NCD):
                first = (c == 0)
                last = (c == NCD - 1)
                sa_ = sqa2[c % 2]
                sb2 = sqb2[c % 2]
                cvv = b.emit("vector", lambda e, c=c, sa_=sa_: e.tensor_copy(
                    sa_[:], lnin2[:, c * 256:(c + 1) * 256]),
                    waits=([("dv", evv)] if first else []) +
                    ([sq_rd] if sq_rd else []), inc="dv", pad=True)
                sqv = b.emit("scalar", lambda e, sa_=sa_, sb2=sb2: e.activation(
                    sb2[:], sa_[:], AF.Square), waits=[("dv", cvv)], inc="ac", pad=True)
                b.emit("tensor", lambda e, c=c, sa_=sa_, first=first, last=last: e.matmul(
                    pA[0:1, 0:256], s_onec[:], sa_[:], start=first, stop=last),
                    waits=([("dv", cvv)] + (slA.wait_free() if first else [])), inc=None)
                gv = b.emit("tensor", lambda e, c=c, sb2=sb2, first=first, last=last: e.matmul(
                    pA[0:1, 512:768], s_onec[:], sb2[:], start=first, stop=last),
                    waits=[("ac", sqv)], inc="pe")
                sq_rd = ("pe", gv)
            vv1 = b.emit("vector", lambda e: e.tensor_scalar(
                s_r0[0:1, 0:256], pA[0:1, 0:256], 1.0 / D, None, OP.mult),
                waits=[("pe", gv)], inc="dv", pad=True)
            vv2 = b.emit("scalar", lambda e: e.activation(s_r0[0:1, 256:512],
                                                    s_r0[0:1, 0:256], AF.Square),
                   waits=[("dv", vv1)], inc="ac", pad=True)
            vv3 = b.emit("vector", lambda e: e.scalar_tensor_tensor(
                s_r1[0:1, 0:256], pA[0:1, 512:768], 1.0 / D, s_r0[0:1, 256:512],
                OP.mult, OP.subtract), waits=[("ac", vv2)], inc="dv")
            vv3 = b.emit("vector", lambda e: e.tensor_scalar(
                s_r1[0:1, 0:256], s_r1[0:1, 0:256], 0.0, None, OP.max),
                inc="dv", pad=True)
            vv4 = b.emit("scalar", lambda e: e.activation(
                s_r1[0:1, 256:512], s_r1[0:1, 0:256], AF.Ln, bias=np.float32(1e-5)),
                waits=[("dv", vv3)], inc="ac")
            rvv = b.emit("scalar", lambda e: e.activation(
                s_r1[0:1, 512:768], s_r1[0:1, 256:512], AF.Exp, scale=np.float32(-0.5)),
                waits=[("ac", b.cur("ac"))], inc="ac", pad=True)
            slA.free = ("ac", rvv)
            g1 = b.emit("tensor", lambda e: e.matmul(
                pB[:, 0:256], s_on1[:], s_r0[0:1, 0:256], start=True, stop=True),
                waits=[("ac", rvv)] + slB.wait_free(), inc=None)
            g2 = b.emit("tensor", lambda e: e.matmul(
                pB[:, 256:512], s_on1[:], s_r1[0:1, 512:768], start=True, stop=True),
                inc="pe")
            bv1 = b.emit("vector", lambda e: e.tensor_copy(bcm2[:], pB[:, 0:256]),
                         waits=[("pe", g2)], inc="dv")
            bv2 = b.emit("vector", lambda e: e.tensor_copy(bcr2[:], pB[:, 256:512]),
                         inc="dv", pad=True)
            slB.free = ("dv", bv2)
            apv = None
            for m in range(NCD):
                b.emit("vector", lambda e, m=m: e.tensor_tensor(
                    ttmp[:], lnin2[:, m * 256:(m + 1) * 256], bcm2[:], OP.subtract),
                    waits=[("dv", bv2)] if m == 0 else [], inc="dv")
                apv = b.emit("vector", lambda e, m=m: e.tensor_tensor(
                    lnin2[:, m * 256:(m + 1) * 256], ttmp[:], bcr2[:], OP.mult),
                    inc="dv")
            apv = b.pad("vector")
            for ts in range(2):
                g = None
                for m in range(NCD):
                    pt = pC if m < 8 else pD
                    pcol = (m % 8) * 128
                    g = b.emit("tensor", lambda e, pt=pt, pcol=pcol, m=m, ts=ts: e.matmul(
                        pt[:, pcol:pcol + 128],
                        lnin2[:, m * 256 + ts * 128: m * 256 + (ts + 1) * 128],
                        s_idf[:], start=True, stop=True),
                        waits=([("dv", apv)] + slC.wait_free() + slD2.wait_free() +
                               ([otm_free] if otm_free else []))
                        if m == 0 else [],
                        inc="pe" if m == NCD - 1 else None)
                t1 = b.emit("scalar", lambda e: e.copy(otm[:, 0:1024], pC[:]),
                            waits=[("pe", g)], inc="ac")
                t2 = b.emit("scalar", lambda e: e.copy(otm[:, 1024:2048], pD[:]),
                            inc="ac", pad=True)
                slC.free = ("ac", t2)
                slD2.free = ("ac", t2)
                sv = b.emit("sync", lambda e, tg=tg, ts=ts: e.dma_start(
                    out=o_out[tg * 256 + ts * 128: tg * 256 + (ts + 1) * 128, :],
                    in_=otm[:]), waits=[("ac", t2)], inc="os", k=16)
                otm_free = ("os", sv)

        dump("c_wb", lambda: ([128, 128], c_wb[:]))
        _fw = [(s, b.cur(s)) for s in b.dma_sems if b.cur(s) > 0]
        b.emit("sync", None, waits=_fw)
        b.emit("gpsimd", None, waits=_fw)

        sc13.close()
        sc11.close()
        with nc.Block() as block:
            b.finalize(block)

    return nc



def _host_prep(inputs):
    d_ord = np.array([(k % 32) * 64 + k // 32 for k in range(D)], dtype=np.int64)
    h_ord = np.array([(k % 128) * 64 + k // 128 for k in range(HF)], dtype=np.int64)

    f32 = lambda a: np.ascontiguousarray(np.asarray(a), dtype=np.float32)
    x = f32(inputs["L0_attn_x"])[0]
    x = x[:, d_ord]
    wR_ = np.ascontiguousarray(f32(inputs["L0_attn_right_W"])[d_ord, :])
    wL_ = np.ascontiguousarray(f32(inputs["L0_attn_left_W"])[d_ord, :])
    wGA_ = f32(inputs["L0_attn_ga_W"])
    wGX_ = f32(inputs["L0_attn_gx_W"])
    wO_ = np.ascontiguousarray(f32(inputs["L0_attn_out_W"])[:, d_ord])
    wU1_ = np.ascontiguousarray(f32(inputs["L0_attn_up1_W"])[d_ord, :])
    wU2_ = np.ascontiguousarray(f32(inputs["L0_attn_up2_W"])[d_ord, :])
    wDN_ = np.ascontiguousarray(f32(inputs["L0_attn_down_W"])[:, d_ord])
    gu = f32(inputs["L0_ffn_w_gate_up"])
    gu = np.concatenate([gu[0:HF][h_ord], gu[HF:][h_ord]], axis=0)[:, d_ord]
    wGU_ = np.ascontiguousarray(gu.T)
    pj = f32(inputs["L0_ffn_w_proj"])[d_ord][:, h_ord]
    wPJ_ = np.ascontiguousarray(pj.T)

    def cols16(v):
        return np.ascontiguousarray(np.asarray(v).astype(np.float32).reshape(-1, 128).T)

    bq = np.zeros((128, 320), dtype=np.float32)
    bq[:, 0:16] = cols16(f32(inputs["L0_attn_norm1_w"])[d_ord])
    bq[:, 16:32] = cols16(inputs["L0_attn_right_b"])
    cw = f32(inputs["L0_attn_conv_w"])
    for k in range(4):
        bq[:, 32 + 16 * k:48 + 16 * k] = cols16(cw[k])
    bq[:, 96:112] = cols16(inputs["L0_attn_conv_b"])
    bq[:, 112:128] = cols16(inputs["L0_attn_ga_b"])
    bq[:, 128:144] = cols16(inputs["L0_attn_gx_b"])
    bq[:, 144:160] = cols16(inputs["L0_attn_a_param"])
    bq[:, 160:176] = cols16(inputs["L0_attn_left_b"])
    bq[:, 176:192] = cols16(f32(inputs["L0_attn_out_b"])[d_ord])
    bq[:, 192:208] = cols16(f32(inputs["L0_attn_norm2_w"])[d_ord])
    bq[:, 208:224] = cols16(f32(inputs["L0_attn_down_b"])[d_ord])
    bq[:, 224:272] = cols16(inputs["L0_attn_up1_b"])
    bq[:, 272:320] = cols16(inputs["L0_attn_up2_b"])

    id128f_ = np.eye(128, dtype=np.float32)
    id128b_ = np.eye(128, dtype=np.float32).astype(NPBF)
    id8f_ = np.eye(8, dtype=np.float32)
    on1f_ = np.ones((1, 128), dtype=np.float32)
    onecb_ = np.ones((128, 1), dtype=NPBF)
    vGf = np.zeros((128, 32), dtype=np.float32)
    for q in range(128):
        vGf[q, q % 32] = 1.0
    vRb_ = np.ascontiguousarray(vGf.T).astype(NPBF)
    vRf_ = np.ascontiguousarray(vGf.T).astype(np.float32)
    vG_ = vGf.astype(NPBF)

    shared = dict(wR=wR_, wL=wL_, wGA=wGA_, wGX=wGX_, wO=wO_, wU1=wU1_, wU2=wU2_,
                  wDN=wDN_, wGU=wGU_, wPJ=wPJ_, biasq=bq, id128f=id128f_,
                  id128b=id128b_, id8f=id8f_, on1f=on1f_, onecb=onecb_,
                  vG=vG_, vRb=vRb_, vRf=vRf_)

    in_maps = []
    for c in range(NC):
        sh = np.zeros((TH, D), dtype=np.float32)
        sh[HALO:] = x[c * T:(c + 1) * T]
        if c > 0:
            sh[0:HALO] = x[c * T - HALO: c * T]
        mk = np.zeros((128, NC), dtype=np.float32)
        mk[:, 0:c] = 1.0
        m = dict(shared)
        m["x_in"] = sh
        m["hmask"] = np.full((128, 1), 0.0 if c == 0 else 1.0, dtype=np.float32)
        m["maskv"] = mk
        m["mask1m"] = 1.0 - mk
        in_maps.append(m)
    return in_maps, d_ord


def kernel(**inputs):
    in_maps, d_ord = _host_prep(inputs)
    nc = build_nc()
    res = run_bass_kernel_spmd(nc, in_maps, core_ids=list(range(NC)))
    out = np.concatenate([res.results[c]["o_out"] for c in range(NC)], axis=0)
    inv = np.empty_like(d_ord)
    inv[d_ord] = np.arange(D)
    out = out[:, inv]
    return out.reshape(1, 8192, D).astype(np.float32)


# version 07 (final)
# speedup vs baseline: 1.0925x; 1.0209x over previous
import numpy as np
import ml_dtypes
from contextlib import ExitStack

import concourse.bass as bass
import concourse.mybir as mybir
from concourse.bass_utils import run_bass_kernel_spmd

F32 = mybir.dt.float32
BF16 = mybir.dt.bfloat16
U16 = mybir.dt.uint16
AF = mybir.ActivationFunctionType
OP = mybir.AluOpType
NPBF = ml_dtypes.bfloat16

NC = 8
D = 2048
T = 1024
HALO = 8
TH = T + HALO
NCD = 16
F_MLP = 6144
NCF = 48
HF = 8192
NCHF = 64
G2 = 16384

DUMPS = {}


class B:

    ENGS = ("sync", "gpsimd", "tensor", "vector", "scalar")

    def __init__(self, nc):
        self.nc = nc
        self.streams = {e: [] for e in self.ENGS}
        self.semh = {}
        self.semv = {}
        self.dma_sems = ["dw", "dh"]

    def mksem(self, ctx, *names):
        for n in names:
            self.semh[n] = ctx.enter_context(self.nc.semaphore(n))
            self.semv[n] = 0

    def mkdma(self, ctx, *names):
        self.mksem(ctx, *names)
        self.dma_sems.extend(names)

    ENG_SEM = {"tensor": "pe", "vector": "dv", "scalar": "ac"}

    def emit(self, eng, fn, waits=(), inc=None, k=1, pad=False):
        wl = [(s, v) for (s, v) in waits if v and v > 0]
        es = self.ENG_SEM.get(eng)
        if es is not None and fn is not None:
            cv = self.semv[es]
            if cv > 0:
                wl.append((es, cv))
        ws = tuple(wl)
        val = None
        if inc is not None:
            self.semv[inc] += (k if k is not None else 1)
            val = self.semv[inc]
        self.streams[eng].append((fn, ws, inc, k))
        if pad:
            return self.pad(eng)
        return val

    def cur(self, name):
        return self.semv[name]

    def pad(self, eng, pad_ap=None):
        sem = {"tensor": "pe", "vector": "dv", "scalar": "ac"}[eng]
        ap = self.pad_cells[eng]
        if eng == "scalar":
            return self.emit(eng, lambda e: e.copy(ap, ap), inc=sem)
        return self.emit(eng, lambda e: e.tensor_copy(ap, ap), inc=sem)

    def dma_fence(self):
        w = [(s, self.cur(s)) for s in self.dma_sems if self.cur(s) > 0]
        cw = [(s, self.cur(s)) for s in ("pe", "dv", "ac") if self.cur(s) > 0]
        for e in self.ENGS:
            self.emit(e, None, waits=(w + cw) if e in ("sync", "gpsimd") else w)

    def barrier(self, engs=("tensor", "vector", "scalar")):
        marks = {}
        for e, s in (("tensor", "pe"), ("vector", "dv"), ("scalar", "ac")):
            if e in engs:
                if e != "tensor":
                    marks[e] = (s, self.pad(e))
                else:
                    marks[e] = (s, self.cur(s))
        for e in engs:
            waits = [v for k2, v in marks.items() if k2 != e]
            self.emit(e, None, waits=waits)

    def finalize(self, block):
        def emit_stream(eng_name):
            def run(eng):
                last = {}
                for fn, ws, inc, k in self.streams[eng_name]:
                    for (s, v) in ws:
                        if last.get(s, -1) >= v:
                            continue
                        eng.wait_ge(self.semh[s], v)
                        last[s] = v
                    if fn is None:
                        continue
                    ins = fn(eng)
                    if inc is not None:
                        if k is None:
                            ins.then_inc(self.semh[inc])
                        else:
                            ins.then_inc(self.semh[inc], k)
            return run

        block.sync(emit_stream("sync"))
        block.gpsimd(emit_stream("gpsimd"))
        block.tensor(emit_stream("tensor"))
        block.vector(emit_stream("vector"))
        block.scalar(emit_stream("scalar"))


class Slot:
    def __init__(self, ap):
        self.ap = ap
        self.free = None

    def wait_free(self):
        return [self.free] if self.free else []


class DBuf:
    def __init__(self, t):
        self.t = t
        self.loaded = 0
        self.free = None


def build_nc():
    nc = bass.Bass("TRN2", target_bir_lowering=False)

    x_in = nc.dram_tensor("x_in", [TH, D], F32, kind="ExternalInput").ap()
    wR = nc.dram_tensor("wR", [D, D], F32, kind="ExternalInput").ap()
    wL = nc.dram_tensor("wL", [D, D], F32, kind="ExternalInput").ap()
    wGA = nc.dram_tensor("wGA", [D, D], F32, kind="ExternalInput").ap()
    wGX = nc.dram_tensor("wGX", [D, D], F32, kind="ExternalInput").ap()
    wO = nc.dram_tensor("wO", [D, D], F32, kind="ExternalInput").ap()
    wU1 = nc.dram_tensor("wU1", [D, F_MLP], F32, kind="ExternalInput").ap()
    wU2 = nc.dram_tensor("wU2", [D, F_MLP], F32, kind="ExternalInput").ap()
    wDN = nc.dram_tensor("wDN", [F_MLP, D], F32, kind="ExternalInput").ap()
    wGU = nc.dram_tensor("wGU", [D, G2], F32, kind="ExternalInput").ap()
    wPJ = nc.dram_tensor("wPJ", [HF, D], F32, kind="ExternalInput").ap()
    biasq = nc.dram_tensor("biasq", [128, 320], F32, kind="ExternalInput").ap()
    maskv = nc.dram_tensor("maskv", [128, NC], F32, kind="ExternalInput").ap()
    mask1m = nc.dram_tensor("mask1m", [128, NC], F32, kind="ExternalInput").ap()
    id128f = nc.dram_tensor("id128f", [128, 128], F32, kind="ExternalInput").ap()
    id128b = nc.dram_tensor("id128b", [128, 128], BF16, kind="ExternalInput").ap()
    id8f = nc.dram_tensor("id8f", [8, 8], F32, kind="ExternalInput").ap()
    on1f = nc.dram_tensor("on1f", [1, 128], F32, kind="ExternalInput").ap()
    onecb = nc.dram_tensor("onecb", [128, 1], BF16, kind="ExternalInput").ap()
    vG = nc.dram_tensor("vG", [128, 32], BF16, kind="ExternalInput").ap()
    vRb = nc.dram_tensor("vRb", [32, 128], BF16, kind="ExternalInput").ap()
    vRf = nc.dram_tensor("vRf", [32, 128], F32, kind="ExternalInput").ap()
    hmask = nc.dram_tensor("hmask", [128, 1], F32, kind="ExternalInput").ap()

    o_out = nc.dram_tensor("o_out", [T, D], F32, kind="ExternalOutput").ap()

    d_xfm = nc.dram_tensor("d_xfm", [128, NCD * TH], F32).ap()
    d_S = nc.dram_tensor("d_S", [128, NCD * T], F32).ap()
    d_x2 = nc.dram_tensor("d_x2", [128, NCD * T], F32).ap()
    d_wqp = nc.dram_tensor("d_wqp", [HF, D], BF16).ap()
    d_qs = nc.dram_tensor("d_qs", [128, 128 * 128], BF16).ap()
    d_qr = nc.dram_tensor("d_qr", [128, 128 * 128], F32).ap()
    cc_in_t = nc.dram_tensor("cc_in", [2, D], F32)
    cc_out_t = nc.dram_tensor("cc_out", [2 * NC, D], F32)

    OFF = dict(n1w=0, rb=16, cw0=32, cw1=48, cw2=64, cw3=80, cb=96, gab=112,
               gxb=128, apar=144, lb=160, ob=176, n2w=192, dnb=208,
               u1b=224, u2b=272)

    with ExitStack() as top:
        b = B(nc)
        b.mksem(top, "dw", "dh", "pe", "dv", "ac", "cc")
        b.mkdma(top, "dc", "w0", "w1", "w2", "w3", "pw0", "pw1", "pq0", "pq1",
                "ps0", "ps1", "xt0", "xt1", "xh", "sx", "ce", "cg",
                "xr0", "xr1", "ss0", "ss1", "sr0", "sr1", "xs0", "xs1", "os",
                "qsA", "qsB", "qrA", "qrB", "qlA", "qlB", "qmA", "qmB")

        def sbt(name, shape, dt, ctx=top, side="left"):
            return ctx.enter_context(nc.sbuf_tensor(name, shape, dt, side=side))

        s_bias = sbt("s_bias", [128, 320], F32)
        s_c4 = sbt("s_c4", [128, 16], F32)
        s_gab2 = sbt("s_gab2", [128, 16], F32)
        s_gxb2 = sbt("s_gxb2", [128, 16], F32)
        s_mk = sbt("s_mk", [128, NC], F32)
        s_mk1 = sbt("s_mk1", [128, NC], F32)
        s_idf = sbt("s_idf", [128, 128], F32)
        s_idb = sbt("s_idb", [128, 128], BF16)
        s_id8 = sbt("s_id8", [8, 8], F32)
        s_on1 = sbt("s_on1", [1, 128], F32)
        s_onec = sbt("s_onec", [128, 1], BF16)
        s_vG = sbt("s_vG", [128, 32], BF16)
        s_vRb = sbt("s_vRb", [32, 128], BF16)
        s_vRf = sbt("s_vRf", [32, 128], F32)
        s_r0 = sbt("s_r0", [1, TH], F32)
        s_r1 = sbt("s_r1", [1, TH], F32)
        s_bc = sbt("s_bc", [128, TH], F32)
        s_end = sbt("s_end", [128, 32], F32)
        t16a = sbt("t16a", [128, 16], F32)
        t16b = sbt("t16b", [128, 16], F32)
        s_hm = sbt("s_hm", [128, 1], F32)
        s_rbh = sbt("s_rbh", [128, 16], F32)
        s_padv = sbt("s_padv", [1, 8], F32)
        s_pads = sbt("s_pads", [1, 8], F32)
        b.pad_cells = {"vector": s_padv[0:1, 0:1], "scalar": s_pads[0:1, 0:1]}
        b.emit("vector", lambda e: e.memset(s_padv[:], 0.0), inc="dv")
        _pv = b.emit("vector", lambda e: e.memset(s_pads[:], 0.0), inc="dv")
        b.emit("scalar", None, waits=[("dv", _pv)])
        wb = [DBuf(sbt(f"wb{i}", [128, 2048], BF16)) for i in range(4)]
        for _i, _bf in enumerate(wb):
            _bf.sem = f"w{_i}"
        c_wb = sbt("c_wb", [128, 128], F32)
        pA = top.enter_context(nc.psum_tensor("pA", [128, 1024], F32))
        pB = top.enter_context(nc.psum_tensor("pB", [128, 1024], F32))
        pC = top.enter_context(nc.psum_tensor("pC", [128, 1024], F32))
        pD = top.enter_context(nc.psum_tensor("pD", [128, 1024], F32))

        def bias(col, m):
            return s_bias[:, OFF[col] + m:OFF[col] + m + 1]

        def dump(name, ap_fn):
            if name not in DUMPS:
                return
            shp, ap = ap_fn()
            dt_ = nc.dram_tensor("dump_" + name, shp, ap.dtype, kind="ExternalOutput").ap()
            v = b.emit("sync", lambda e, ap=ap, dt_=dt_: e.dma_start(out=dt_, in_=ap),
                       waits=[("pe", b.cur("pe")), ("dv", b.cur("dv")), ("ac", b.cur("ac"))],
                       inc="dh", k=16)
            for e in ("tensor", "vector", "scalar"):
                b.emit(e, None, waits=[("dh", v)])

        def stream_mm(wdram, nM, nK, rhs_of, windows, slots, evac, wpair,
                      extra=None, interleave=None):
            for m in range(nM):
                buf = wb[wpair[m % 2]]
                w3 = buf.t[:].rearrange("p (k j) -> p k j", k=nK)[:, 0:nK, :]
                lv = b.emit("gpsimd",
                            lambda e, m=m, w3=w3: e.dma_start(
                                out=w3, in_=wdram[:, :, m * 128:(m + 1) * 128]),
                            waits=([buf.free] if buf.free else []),
                            inc=buf.sem, k=16)
                buf.loaded = lv
                slot = slots[m % len(slots)]
                waits = [(buf.sem, lv)] + slot.wait_free()
                gv = None
                for k in range(nK):
                    first = (k == 0)
                    last = (k == nK - 1)
                    for wi, (pcol, width) in enumerate(windows):
                        rhs = rhs_of(k, wi)
                        ap = slot.ap[:, pcol:pcol + width]
                        gv = b.emit("tensor",
                                    lambda e, ap=ap, buf=buf, k=k, rhs=rhs, first=first, last=last:
                                    e.matmul(ap, buf.t[:, k * 128:(k + 1) * 128], rhs,
                                             start=first, stop=last),
                                    waits=(waits if (k == 0 and wi == 0) else []),
                                    inc=("pe" if (last and wi == len(windows) - 1) else None))
                    if extra is not None:
                        gv = extra(m, k, buf, first, last) or gv
                buf.free = ("pe", gv)
                evac(m, slot, gv)
                if interleave is not None:
                    interleave(m)

        ldh = []
        for src, dst in [(biasq, s_bias), (maskv, s_mk), (mask1m, s_mk1),
                         (id128f, s_idf), (id128b, s_idb), (id8f, s_id8),
                         (on1f, s_on1), (onecb, s_onec), (vG, s_vG),
                         (vRb, s_vRb), (vRf, s_vRf), (hmask, s_hm)]:
            ldh.append(b.emit("sync", lambda e, src=src, dst=dst:
                              e.dma_start(out=dst[:], in_=src), inc="dc", k=16))
        w0 = [("dc", ldh[-1])]
        b.emit("tensor", None, waits=w0)
        b.emit("gpsimd", None, waits=w0)
        ap_col = s_bias[:, OFF["apar"]:OFF["apar"] + 16]
        b.emit("scalar", lambda e: e.activation(t16a[:], ap_col, AF.Abs), waits=w0, inc="ac")
        b.emit("scalar", lambda e: e.activation(t16a[:], t16a[:], AF.Exp, scale=np.float32(-1.0)),
               waits=[("ac", b.cur("ac"))], inc="ac")
        b.emit("scalar", lambda e: e.activation(t16a[:], t16a[:], AF.Ln, bias=np.float32(1.0)),
               waits=[("ac", b.cur("ac"))], inc="ac")
        b.emit("scalar", lambda e: e.activation(t16b[:], ap_col, AF.Relu), waits=w0, inc="ac")
        vac = b.cur("ac")
        b.emit("vector", lambda e: e.tensor_tensor(t16a[:], t16a[:], t16b[:], OP.add),
               waits=[("ac", vac)], inc="dv")
        b.emit("vector", lambda e: e.tensor_scalar(s_c4[:], t16a[:], -4.0, None, OP.mult),
               waits=[("dv", b.cur("dv"))], inc="dv")
        b.emit("vector", lambda e: e.tensor_scalar(
            s_gab2[:], s_bias[:, OFF["gab"]:OFF["gab"] + 16], 0.5, None, OP.mult),
            waits=w0, inc="dv")
        b.emit("vector", lambda e: e.tensor_scalar(
            s_gxb2[:], s_bias[:, OFF["gxb"]:OFF["gxb"] + 16], 0.5, None, OP.mult),
            waits=w0, inc="dv")

        b.emit("vector", lambda e: e.tensor_scalar(
            s_rbh[:], s_bias[:, OFF["rb"]:OFF["rb"] + 16], s_hm[:, 0:1], None, OP.mult),
            waits=w0, inc="dv")

        b.emit("vector", lambda e: e.memset(c_wb[:], 777.0), inc="dv")

        slA = Slot(pA[:])
        slB = Slot(pB[:])
        slD = Slot(pD[:])

        sc_pp = ExitStack()
        ppw = [DBuf(sbt(f"ppw{i}", [128, 2048], BF16, ctx=sc_pp)) for i in range(2)]
        for _i, _bf in enumerate(ppw):
            _bf.sem = f"pw{_i}"
        ppa1 = sbt("ppa1", [128, 2048], BF16, ctx=sc_pp)
        c_pp = sbt("c_pp", [128, 128], F32, ctx=sc_pp)
        b.emit("vector", lambda e: e.memset(c_pp[:], 777.0), inc="dv")
        pp_state = {"n": 0, "pe": None, "afree": None}

        def prepass_slab():
            s = pp_state["n"]
            if s >= 64:
                return
            pp_state["n"] = s + 1
            pb = ppw[s % 2]
            lv = b.emit("gpsimd", lambda e, s=s, pb=pb: e.dma_start(
                out=pb.t[:], in_=wPJ[s * 128:(s + 1) * 128, :]),
                waits=([pb.free] if pb.free else []), inc=pb.sem, k=16)
            av = b.emit("vector", lambda e, pb=pb: e.tensor_scalar(
                ppa1[:].bitcast(U16), pb.t[:].bitcast(U16), 0x7FFF, None,
                OP.bitwise_and),
                waits=[(pb.sem, lv)] + ([pp_state["afree"]] if pp_state["afree"] else []),
                inc="dv")
            pb.free = ("dv", av)
            first = (s == 0)
            last = (s == 63)
            gv = None
            for wi, (pt, pcol) in enumerate([(pC, 0), (pC, 512), (pD, 0), (pD, 512)]):
                gv = b.emit("tensor", lambda e, pt=pt, pcol=pcol, wi=wi, first=first, last=last:
                            e.matmul(pt[:, pcol:pcol + 512], s_idb[:],
                                     ppa1[:, wi * 512:(wi + 1) * 512],
                                     start=first, stop=last),
                            waits=[("dv", av)] if wi == 0 else [],
                            inc="pe" if wi == 3 else None)
            pp_state["pe"] = gv
            pp_state["afree"] = ("pe", gv)

        sc_xn = ExitStack()
        xn = sc_xn.enter_context(nc.sbuf_tensor("xn", [128, NCD * TH], BF16, side="left"))
        sc1 = ExitStack()
        x_fm = sc1.enter_context(nc.sbuf_tensor("x_fm", [128, NCD * TH], F32, side="right"))
        sc1b = ExitStack()
        sq1 = sc1b.enter_context(nc.sbuf_tensor("sq1", [128, NCD * TH], BF16, side="right"))
        xt0 = sc1b.enter_context(nc.sbuf_tensor("xt0", [128, 2048], F32, side="right"))
        xt1 = sc1b.enter_context(nc.sbuf_tensor("xt1", [128, 2048], F32, side="right"))
        xh = sc1b.enter_context(nc.sbuf_tensor("xh", [8, 2048], F32, side="right"))

        x3 = x_fm[:].rearrange("p (c t) -> p c t", c=NCD)
        xtb = [DBuf(xt0), DBuf(xt1)]
        xtb[0].sem, xtb[1].sem = "xt0", "xt1"
        for j in range(8):
            tb = xtb[j % 2]
            lv = b.emit("sync", lambda e, j=j, tb=tb: e.dma_start(
                out=tb.t[:], in_=x_in[HALO + j * 128: HALO + (j + 1) * 128, :]),
                waits=([tb.free] if tb.free else []), inc=tb.sem, k=16)
            for g in range(2):
                slot = slA if (2 * j + g) % 2 == 0 else slB
                gv = None
                for c8 in range(8):
                    c = g * 8 + c8
                    gv = b.emit("tensor", lambda e, slot=slot, c8=c8, tb=tb, c=c:
                                e.matmul(slot.ap[:, c8 * 128:(c8 + 1) * 128],
                                         tb.t[:, c * 128:(c + 1) * 128], s_idf[:],
                                         start=True, stop=True),
                                waits=([(tb.sem, lv)] + slot.wait_free()) if c8 == 0 else [],
                                inc="pe" if c8 == 7 else None)
                dst = x3[:, g * 8:(g + 1) * 8, HALO + j * 128: HALO + (j + 1) * 128]
                fv = b.emit("scalar", lambda e, slot=slot, dst=dst: e.copy(
                    dst, slot.ap[:].rearrange("p (c t) -> p c t", c=8)),
                    waits=[("pe", gv)], inc="ac", pad=True)
                slot.free = ("ac", fv)
            tb.free = ("pe", b.cur("pe"))
        lvh = b.emit("sync", lambda e: e.dma_start(out=xh[:], in_=x_in[0:HALO, :]),
                     inc="xh", k=16)
        slC0 = Slot(pC[:, 0:128])
        gv = None
        for c in range(NCD):
            gv = b.emit("tensor", lambda e, c=c: e.matmul(
                pC[:, c * 8:(c + 1) * 8], xh[0:8, c * 128:(c + 1) * 128],
                s_id8[0:8, 0:8], start=True, stop=True),
                waits=([("xh", lvh)] + slC0.wait_free()) if c == 0 else [],
                inc="pe" if c == NCD - 1 else None)
        fv = b.emit("vector", lambda e: e.tensor_copy(
            x3[:, :, 0:HALO], pC[:, 0:128].rearrange("p (c t) -> p c t", c=NCD)),
            waits=[("pe", gv)], inc="dv", pad=True)
        slC0.free = ("dv", fv)
        acp = b.pad("scalar")
        sp_x = b.emit("sync", lambda e: e.dma_start(out=d_xfm, in_=x_fm[:]),
                      waits=[("ac", acp), ("dv", fv)], inc="sx", k=16)
        sqv = b.emit("scalar", lambda e: e.activation(sq1[:], x_fm[:], AF.Square),
                     waits=[("dv", fv)], inc="ac", pad=True)
        gv = None
        for c in range(NCD):
            first = (c == 0)
            last = (c == NCD - 1)
            gv = b.emit("tensor", lambda e, c=c, first=first, last=last: e.matmul(
                pD[0:1, 0:512], s_onec[:], sq1[:, c * TH: c * TH + 512],
                start=first, stop=last),
                waits=([("ac", sqv)] + slD.wait_free()) if first else [], inc=None)
            gv = b.emit("tensor", lambda e, c=c, first=first, last=last: e.matmul(
                pD[0:1, 512:1024], s_onec[:], sq1[:, c * TH + 512: c * TH + 1024],
                start=first, stop=last), inc=None)
            gv = b.emit("tensor", lambda e, c=c, first=first, last=last: e.matmul(
                pC[0:1, 128:136], s_onec[:], sq1[:, c * TH + 1024: c * TH + 1032],
                start=first, stop=last), inc="pe" if last else None)
        b.emit("scalar", lambda e: e.activation(s_r0[0:1, 0:1024], pD[0:1, 0:1024],
                                                AF.Ln, bias=np.float32(1e-8), scale=np.float32(1.0 / D)),
               waits=[("pe", gv)], inc="ac")
        b.emit("scalar", lambda e: e.activation(s_r0[0:1, 1024:1032], pC[0:1, 128:136],
                                                AF.Ln, bias=np.float32(1e-8), scale=np.float32(1.0 / D)),
               inc="ac")
        rv = b.emit("scalar", lambda e: e.activation(s_r1[0:1, :], s_r0[0:1, :],
                                                     AF.Exp, scale=np.float32(-0.5)),
                    waits=[("ac", b.cur("ac"))], inc="ac", pad=True)
        slD.free = ("ac", rv)
        slC0.free = ("ac", rv)
        gv = None
        for (pcol, width) in [(0, 512), (512, 512)]:
            gv = b.emit("tensor", lambda e, pcol=pcol, width=width: e.matmul(
                pD[:, pcol:pcol + width], s_on1[:], s_r1[0:1, pcol:pcol + width],
                start=True, stop=True),
                waits=([("ac", rv)] + slD.wait_free()) if pcol == 0 else [],
                inc=None)
        gv = b.emit("tensor", lambda e: e.matmul(
            pC[:, 128:136], s_on1[:], s_r1[0:1, 1024:1032], start=True, stop=True),
            waits=slC0.wait_free(), inc="pe")
        bv1 = b.emit("vector", lambda e: e.tensor_copy(s_bc[:, 0:1024], pD[:]),
                     waits=[("pe", gv)], inc="dv")
        bv2 = b.emit("vector", lambda e: e.tensor_copy(s_bc[:, 1024:1032], pC[:, 128:136]),
                     inc="dv", pad=True)
        slD.free = ("dv", bv2)
        slC0.free = ("dv", bv2)
        for c in range(NCD):
            b.emit("vector", lambda e, c=c: e.scalar_tensor_tensor(
                xn[:, c * TH:(c + 1) * TH], x_fm[:, c * TH:(c + 1) * TH],
                bias("n1w", c), s_bc[:], OP.mult, OP.mult),
                waits=[("dv", bv2)] if c == 0 else [], inc="dv")
        b.barrier()
        dump("xn", lambda: ([128, NCD * TH], xn[:]))
        b.dma_fence()
        sc1b.close()

        rp = x_fm
        wR3 = wR.rearrange("(k p) m -> p k m", p=128)

        def rhs_xn(k, wi):
            base = k * TH
            if wi == 0:
                return xn[:, base + HALO: base + HALO + 512]
            return xn[:, base + HALO + 512: base + TH]

        halo_state = {"free": None}

        def extra_halo(m, k, buf, first, last):
            w = ([halo_state["free"]] if (first and halo_state["free"]) else [])
            return b.emit("tensor", lambda e, m=m, k=k, buf=buf, first=first, last=last:
                          e.matmul(pC[:, m * 8:(m + 1) * 8],
                                   buf.t[:, k * 128:(k + 1) * 128],
                                   xn[:, k * TH: k * TH + 8],
                                   start=first, stop=last),
                          waits=w, inc="pe" if last else None)

        def evac_rp(m, slot, gv):
            wt = [("pe", gv), ("sx", sp_x)]
            v1 = b.emit("scalar", lambda e, m=m, slot=slot: e.activation(
                rp[:, m * TH + HALO:(m + 1) * TH], slot.ap[:], AF.Identity,
                bias=bias("rb", m)), waits=wt, inc="ac")
            v2 = b.emit("scalar", lambda e, m=m: e.activation(
                rp[:, m * TH: m * TH + 8], pC[:, m * 8:(m + 1) * 8], AF.Identity,
                bias=s_rbh[:, m:m + 1]), inc="ac")
            pv = b.pad("scalar")
            slot.free = ("ac", pv)
            halo_state["free"] = ("ac", pv)

        stream_mm(wR3, NCD, NCD, rhs_xn, [(0, 512), (512, 512)],
                  [slA, slB], evac_rp, (0, 1), extra=extra_halo)
        b.barrier()
        b.dma_fence()
        dump("rp", lambda: ([128, NCD * TH], rp[:]))

        conv = sc_xn.enter_context(nc.sbuf_tensor("conv", [128, NCD * T], BF16, side="left"))
        sc_ct = ExitStack()
        cta = sc_ct.enter_context(nc.sbuf_tensor("cta", [128, T], F32, side="left"))
        ctb = sc_ct.enter_context(nc.sbuf_tensor("ctb", [128, T], F32, side="left"))
        for c in range(NCD):
            base = c * TH
            b.emit("vector", lambda e, c=c, base=base: e.tensor_scalar(
                cta[:], rp[:, base + 5: base + 5 + T], bias("cw0", c), bias("cb", c),
                OP.mult, OP.add), inc="dv")
            b.emit("vector", lambda e, c=c, base=base: e.scalar_tensor_tensor(
                ctb[:], rp[:, base + 6: base + 6 + T], bias("cw1", c), cta[:],
                OP.mult, OP.add), inc="dv")
            b.emit("vector", lambda e, c=c, base=base: e.scalar_tensor_tensor(
                cta[:], rp[:, base + 7: base + 7 + T], bias("cw2", c), ctb[:],
                OP.mult, OP.add), inc="dv")
            b.emit("vector", lambda e, c=c, base=base: e.scalar_tensor_tensor(
                conv[:, c * T:(c + 1) * T], rp[:, base + 8: base + 8 + T],
                bias("cw3", c), cta[:], OP.mult, OP.add), inc="dv")
        b.barrier()
        dump("conv", lambda: ([128, NCD * T], conv[:]))
        b.dma_fence()
        sc_ct.close()
        sc1.close()

        scH = ExitStack()
        h_bf = scH.enter_context(nc.sbuf_tensor("h_bf", [128, NCD * T], BF16, side="right"))
        sc5 = ExitStack()
        P_bf = sc5.enter_context(nc.sbuf_tensor("P_bf", [128, NCD * T], BF16, side="right"))
        a_st = [sc5.enter_context(nc.sbuf_tensor(f"a_st{i}", [128, T], F32, side="right"))
                for i in range(2)]
        y_st = [sc5.enter_context(nc.sbuf_tensor(f"y_st{i}", [128, T], F32, side="right"))
                for i in range(2)]
        i_st = [sc5.enter_context(nc.sbuf_tensor(f"i_st{i}", [128, T], F32, side="right"))
                for i in range(2)]
        m_st = sc5.enter_context(nc.sbuf_tensor("m_st", [128, T], F32, side="right"))
        tmp5 = sc5.enter_context(nc.sbuf_tensor("tmp5", [128, T], F32, side="right"))

        wGA3 = wGA.rearrange("(k p) m -> p k m", p=128)
        wGX3 = wGX.rearrange("(k p) m -> p k m", p=128)
        tmp5_rd = {"v": None}
        for g0 in range(0, NCD, 2):
            grp = [g0, g0 + 1]
            chain = {}
            for ci, c in enumerate(grp):
                for (wmat, pair, slot) in [(wGA3, (0, 1), slA), (wGX3, (2, 3), slB)]:
                    buf = wb[pair[c % 2]]
                    w3 = buf.t[:].rearrange("p (k j) -> p k j", k=NCD)
                    lv = b.emit("gpsimd", lambda e, wmat=wmat, c=c, w3=w3: e.dma_start(
                        out=w3, in_=wmat[:, :, c * 128:(c + 1) * 128]),
                        waits=([buf.free] if buf.free else []), inc=buf.sem, k=16)
                    gv = None
                    for k in range(NCD):
                        for wi, (pcol, t0) in enumerate([(0, 0), (512, 512)]):
                            gv = b.emit("tensor", lambda e, slot=slot, pcol=pcol, buf=buf, k=k, t0=t0:
                                        e.matmul(slot.ap[:, pcol:pcol + 512],
                                                 buf.t[:, k * 128:(k + 1) * 128],
                                                 conv[:, k * T + t0: k * T + t0 + 512],
                                                 start=(k == 0), stop=(k == NCD - 1)),
                                        waits=([(buf.sem, lv)] + slot.wait_free())
                                        if (k == 0 and wi == 0) else [],
                                        inc="pe" if (k == NCD - 1 and wi == 1) else None)
                    buf.free = ("pe", gv)
                    chain[("mm", c, slot is slB)] = gv
                v = b.emit("scalar", lambda e, c=c: e.activation(
                    tmp5[:], slA.ap[:], AF.Tanh, bias=s_gab2[:, c:c + 1], scale=np.float32(0.5)),
                    waits=[("pe", chain[("mm", c, False)])] +
                    ([("dv", tmp5_rd["v"])] if tmp5_rd["v"] else []), inc="ac", pad=True)
                slA.free = ("ac", v)
                v2 = b.emit("vector", lambda e, c=c: e.tensor_scalar(
                    m_st[:], tmp5[:], s_c4[:, c:c + 1], s_c4[:, c:c + 1], OP.mult, OP.add),
                    waits=[("ac", v)], inc="dv", pad=True)
                v3 = b.emit("scalar", lambda e, ci=ci: e.activation(
                    a_st[ci][:], m_st[:], AF.Exp), waits=[("dv", v2)], inc="ac")
                v4 = b.emit("scalar", lambda e, ci=ci: e.activation(
                    tmp5[:], a_st[ci][:], AF.Square), waits=[("ac", v3)], inc="ac", pad=True)
                v5 = b.emit("vector", lambda e, ci=ci: e.tensor_scalar(
                    y_st[ci][:], tmp5[:], -1.0, 1.0, OP.mult, OP.add),
                    waits=[("ac", v4)], inc="dv")
                tmp5_rd["v"] = v5
                v6 = b.emit("scalar", lambda e, c=c, ci=ci: e.activation(
                    i_st[ci][:], slB.ap[:], AF.Tanh, bias=s_gxb2[:, c:c + 1], scale=np.float32(0.5)),
                    waits=[("pe", chain[("mm", c, True)])], inc="ac", pad=True)
                slB.free = ("ac", v6)
            for ci, c in enumerate(grp):
                v6b = b.emit("vector", lambda e, ci=ci: e.tensor_scalar(
                    m_st[:], y_st[ci][:], 0.0, None, OP.max),
                    waits=[("dv", b.cur("dv")), ("ac", b.cur("ac"))], inc="dv")
                v7 = b.emit("scalar", lambda e, ci=ci: e.activation(
                    y_st[ci][:], m_st[:], AF.Sqrt),
                    waits=[("dv", v6b)], inc="ac", pad=True)
                v8 = b.emit("vector", lambda e, ci=ci: e.tensor_tensor(
                    m_st[:], y_st[ci][:], i_st[ci][:], OP.mult),
                    waits=[("ac", v7)], inc="dv")
                v9 = b.emit("vector", lambda e, ci=ci: e.tensor_tensor(
                    i_st[ci][:], y_st[ci][:], m_st[:], OP.add),
                    waits=[("dv", v8)], inc="dv")
                v10 = b.emit("vector", lambda e, c=c, ci=ci: e.scalar_tensor_tensor(
                    tmp5[:], i_st[ci][:], 0.5, conv[:, c * T:(c + 1) * T],
                    OP.mult, OP.mult), waits=[("dv", v9)], inc="dv")
                v11 = b.emit("vector", lambda e, c=c, ci=ci: e.tensor_tensor_scan(
                    h_bf[:, c * T:(c + 1) * T], a_st[ci][:], tmp5[:], 0.0,
                    OP.mult, OP.add), waits=[("dv", v10)], inc="dv")
                tmp5_rd["v"] = v11
                v12 = b.emit("vector", lambda e, c=c, ci=ci: e.tensor_tensor_scan(
                    P_bf[:, c * T:(c + 1) * T], a_st[ci][:], a_st[ci][:], 1.0,
                    OP.mult, OP.bypass), waits=[("dv", v11)], inc="dv")
                b.emit("vector", lambda e, c=c: e.tensor_copy(
                    s_end[:, c:c + 1], P_bf[:, (c + 1) * T - 1:(c + 1) * T]),
                    waits=[("dv", v12)], inc="dv")
                b.emit("vector", lambda e, c=c: e.tensor_copy(
                    s_end[:, 16 + c:16 + c + 1], h_bf[:, (c + 1) * T - 1:(c + 1) * T]),
                    inc="dv")
        cin3 = cc_in_t.ap().rearrange("a (p c) -> a p c", p=128)
        dvp = b.pad("vector")
        e1 = b.emit("sync", lambda e: e.dma_start(out=cin3[0], in_=s_end[:, 0:16]),
                    waits=[("dv", dvp)], inc="ce", k=16)
        e2 = b.emit("sync", lambda e: e.dma_start(out=cin3[1], in_=s_end[:, 16:32]),
                    inc="ce", k=16)
        ccv = b.emit("gpsimd", lambda e: e.collective_compute(
            "AllGather", OP.bypass, replica_groups=[list(range(NC))],
            ins=[cc_in_t.ap().opt()], outs=[cc_out_t.ap().opt()]),
            waits=[("ce", e2)], inc="cc", k=None)
        dump("h_loc", lambda: ([128, NCD * T], h_bf[:]))
        dump("P_loc", lambda: ([128, NCD * T], P_bf[:]))
        for _ in range(10):
            prepass_slab()

        sc5c = ExitStack()
        s_gath = sc5c.enter_context(nc.sbuf_tensor("s_gath", [128, 2 * NC * 16], F32, side="right"))
        s_hin = sc5c.enter_context(nc.sbuf_tensor("s_hin", [128, 16], F32, side="right"))
        s_t0 = sc5c.enter_context(nc.sbuf_tensor("s_t0", [128, 16], F32, side="right"))
        s_t1 = sc5c.enter_context(nc.sbuf_tensor("s_t1", [128, 16], F32, side="right"))
        gat3 = cc_out_t.ap().rearrange("a (p c) -> a p c", p=128)
        gl = None
        for j in range(2 * NC):
            gl = b.emit("sync", lambda e, j=j: e.dma_start(
                out=s_gath[:, j * 16:(j + 1) * 16], in_=gat3[j]),
                waits=[("cc", 1)] if j == 0 else [], inc="cg", k=16)
        hv = b.emit("vector", lambda e: e.memset(s_hin[:], 0.0), inc="dv")
        for r in range(NC - 1):
            pa = slice((2 * r) * 16, (2 * r + 1) * 16)
            ha = slice((2 * r + 1) * 16, (2 * r + 2) * 16)
            b.emit("vector", lambda e, pa=pa, r=r: e.tensor_scalar(
                s_t0[:], s_gath[:, pa], s_mk[:, r:r + 1], s_mk1[:, r:r + 1],
                OP.mult, OP.add), waits=[("cg", gl), ("dv", hv)], inc="dv")
            b.emit("vector", lambda e, ha=ha, r=r: e.tensor_scalar(
                s_t1[:], s_gath[:, ha], s_mk[:, r:r + 1], None, OP.mult), inc="dv")
            b.emit("vector", lambda e: e.tensor_tensor(s_hin[:], s_t0[:], s_hin[:], OP.mult),
                   inc="dv")
            hv = b.emit("vector", lambda e: e.tensor_tensor(s_hin[:], s_hin[:], s_t1[:], OP.add),
                        inc="dv")
        for c in range(NCD):
            b.emit("vector", lambda e, c=c: e.scalar_tensor_tensor(
                h_bf[:, c * T:(c + 1) * T], P_bf[:, c * T:(c + 1) * T],
                s_hin[:, c:c + 1], h_bf[:, c * T:(c + 1) * T], OP.mult, OP.add),
                waits=[("dv", hv)], inc="dv")
        b.barrier()
        dump("h_fix", lambda: ([128, NCD * T], h_bf[:]))
        b.dma_fence()
        sc5c.close()
        sc5.close()

        sc4 = ExitStack()
        left = sc4.enter_context(nc.sbuf_tensor("left", [128, NCD * T], BF16, side="right"))
        wL3 = wL.rearrange("(k p) m -> p k m", p=128)

        def evac_left(m, slot, gv):
            v = b.emit("scalar", lambda e, m=m, slot=slot: e.activation(
                left[:, m * T:(m + 1) * T], slot.ap[:], AF.Gelu, bias=bias("lb", m)),
                waits=[("pe", gv)], inc="ac")
            slot.free = ("ac", b.pad("scalar"))

        stream_mm(wL3, NCD, NCD, rhs_xn, [(0, 512), (512, 512)],
                  [slA, slB], evac_left, (0, 1),
                  interleave=lambda m: prepass_slab())
        b.barrier()
        dump("left", lambda: ([128, NCD * T], left[:]))
        b.dma_fence()
        sc_xn.close()

        for c in range(NCD):
            b.emit("vector", lambda e, c=c: e.tensor_tensor(
                h_bf[:, c * T:(c + 1) * T], left[:, c * T:(c + 1) * T],
                h_bf[:, c * T:(c + 1) * T], OP.mult), inc="dv")
        b.barrier()

        sc7 = ExitStack()
        x1 = sc7.enter_context(nc.sbuf_tensor("x1", [128, NCD * T], F32, side="left"))
        xre = [DBuf(sc7.enter_context(nc.sbuf_tensor(f"xre{i}", [128, T], F32, side="left")))
               for i in range(2)]
        xre[0].sem, xre[1].sem = "xr0", "xr1"
        sst = [sc7.enter_context(nc.sbuf_tensor(f"sst{i}", [128, T], F32, side="left"))
               for i in range(2)]
        sst_free = [None, None]
        wO3 = wO.rearrange("(k p) m -> p k m", p=128)

        def rhs_lr(k, wi):
            return h_bf[:, k * T + wi * 512: k * T + wi * 512 + 512]

        def evac_x1(m, slot, gv):
            xb = xre[m % 2]
            lv = b.emit("sync", lambda e, m=m, xb=xb: e.dma_start(
                out=xb.t[:], in_=d_xfm[:, m * TH + HALO:(m + 1) * TH]),
                waits=([xb.free] if xb.free else []), inc=xb.sem, k=16)
            v1 = b.emit("vector", lambda e, m=m, slot=slot, xb=xb: e.scalar_tensor_tensor(
                x1[:, m * T:(m + 1) * T], slot.ap[:], bias("ob", m), xb.t[:],
                OP.add, OP.add), waits=[("pe", gv), (xb.sem, lv)], inc="dv")
            slot.free = ("dv", b.pad("vector"))
            sbuf = sst[m % 2]
            wt = [("dv", v1)]
            if sst_free[m % 2]:
                wt.append(sst_free[m % 2])
            v2 = b.emit("vector", lambda e, m=m, sbuf=sbuf, xb=xb: e.tensor_tensor(
                sbuf[:], x1[:, m * T:(m + 1) * T], xb.t[:], OP.add),
                waits=wt, inc="dv", pad=True)
            xb.free = ("dv", v2)
            sv = b.emit("sync", lambda e, m=m, sbuf=sbuf: e.dma_start(
                out=d_S[:, m * T:(m + 1) * T], in_=sbuf[:]),
                waits=[("dv", v2)], inc=f"ss{m % 2}", k=16)
            sst_free[m % 2] = (f"ss{m % 2}", sv)

        stream_mm(wO3, NCD, NCD, rhs_lr, [(0, 512), (512, 512)],
                  [slA, slB], evac_x1, (0, 1),
                  interleave=lambda m: prepass_slab())
        while pp_state["n"] < 64:
            prepass_slab()
        b.barrier()
        dump("x1", lambda: ([128, NCD * T], x1[:]))
        b.dma_fence()
        sc4.close()
        scH.close()

        sc8 = ExitStack()
        sq8 = sc8.enter_context(nc.sbuf_tensor("sq8", [128, NCD * T], BF16, side="left"))
        sc_x1n = ExitStack()
        x1n = sc_x1n.enter_context(nc.sbuf_tensor("x1n", [128, NCD * T], BF16, side="right"))
        sqv = b.emit("scalar", lambda e: e.activation(sq8[:], x1[:], AF.Square),
                     waits=[("dv", b.cur("dv"))], inc="ac", pad=True)
        gv = None
        for c in range(NCD):
            first = (c == 0)
            last = (c == NCD - 1)
            b.emit("tensor", lambda e, c=c, first=first, last=last: e.matmul(
                pA[0:1, 0:512], s_onec[:], sq8[:, c * T: c * T + 512],
                start=first, stop=last),
                waits=([("ac", sqv)] + slA.wait_free()) if first else [], inc=None)
            gv = b.emit("tensor", lambda e, c=c, first=first, last=last: e.matmul(
                pA[0:1, 512:1024], s_onec[:], sq8[:, c * T + 512: c * T + 1024],
                start=first, stop=last), inc="pe" if last else None)
        b.emit("scalar", lambda e: e.activation(s_r0[0:1, 0:1024], pA[0:1, 0:1024],
                                                AF.Ln, bias=np.float32(1e-8), scale=np.float32(1.0 / D)),
               waits=[("pe", gv)], inc="ac")
        rv = b.emit("scalar", lambda e: e.activation(s_r1[0:1, 0:1024], s_r0[0:1, 0:1024],
                                                     AF.Exp, scale=np.float32(-0.5)),
                    waits=[("ac", b.cur("ac"))], inc="ac", pad=True)
        slA.free = ("ac", rv)
        gv = None
        for (pcol, width) in [(0, 512), (512, 512)]:
            gv = b.emit("tensor", lambda e, pcol=pcol, width=width: e.matmul(
                pB[:, pcol:pcol + width], s_on1[:], s_r1[0:1, pcol:pcol + width],
                start=True, stop=True),
                waits=([("ac", rv)] + slB.wait_free()) if pcol == 0 else [],
                inc="pe" if pcol == 512 else None)
        bv = b.emit("vector", lambda e: e.tensor_copy(s_bc[:, 0:1024], pB[:]),
                    waits=[("pe", gv)], inc="dv", pad=True)
        slB.free = ("dv", bv)
        for c in range(NCD):
            b.emit("vector", lambda e, c=c: e.scalar_tensor_tensor(
                x1n[:, c * T:(c + 1) * T], x1[:, c * T:(c + 1) * T],
                bias("n2w", c), s_bc[:, 0:1024], OP.mult, OP.mult),
                waits=[("dv", bv)] if c == 0 else [], inc="dv")
        b.barrier()
        dump("x1n", lambda: ([128, NCD * T], x1n[:]))
        dump("c_pp", lambda: ([128, 128], c_pp[:]))
        b.dma_fence()
        sc8.close()
        sc7.close()
        sc_pp.close()

        sc_prod = ExitStack()
        prod = sc_prod.enter_context(nc.sbuf_tensor("prod", [128, NCF * T], BF16, side="left"))
        sc_ps = ExitStack()
        s_pjs = sc_ps.enter_context(nc.sbuf_tensor("s_pjs", [128, 2048], BF16, side="right"))
        s_pjr = sc_ps.enter_context(nc.sbuf_tensor("s_pjr", [128, 2048], F32, side="right"))
        sc9 = ExitStack()
        u1s = [sc9.enter_context(nc.sbuf_tensor(f"u1s{i}", [128, T], BF16, side="right"))
               for i in range(2)]
        u2s = [sc9.enter_context(nc.sbuf_tensor(f"u2s{i}", [128, T], BF16, side="right"))
               for i in range(2)]
        pqw = [DBuf(sc9.enter_context(nc.sbuf_tensor(f"pqw{i}", [128, 2048], BF16, side="right")))
               for i in range(2)]
        pqw[0].sem, pqw[1].sem = "pq0", "pq1"
        pqt = [sc9.enter_context(nc.sbuf_tensor(f"pqt{i}", [128, 2048], BF16, side="right"))
               for i in range(2)]

        fv1 = b.emit("vector", lambda e: e.tensor_scalar(
            s_pjs[:, 0:1024], pC[:], 1.0 / 64, 1e-8, OP.mult, OP.max),
            waits=[("pe", pp_state["pe"])], inc="dv")
        fv2 = b.emit("vector", lambda e: e.tensor_scalar(
            s_pjs[:, 1024:2048], pD[:], 1.0 / 64, 1e-8, OP.mult, OP.max), inc="dv", pad=True)
        b.emit("vector", lambda e: e.reciprocal(s_pjr[:, 0:1024], s_pjs[:, 0:1024]),
               waits=[("dv", fv2)], inc="dv")
        rcv = b.emit("vector", lambda e: e.reciprocal(s_pjr[:, 1024:2048], s_pjs[:, 1024:2048]),
                     inc="dv")
        slC = Slot(pC[:])
        slC.free = ("dv", fv2)
        slD2 = Slot(pD[:])
        slD2.free = ("dv", fv2)

        pq_ctr = {"n": 0}

        def proj_quant_slab():
            s = pq_ctr["n"]
            if s >= 64:
                return
            pq_ctr["n"] = s + 1
            pb = pqw[s % 2]
            lv = b.emit("gpsimd", lambda e, s=s, pb=pb: e.dma_start(
                out=pb.t[:], in_=wPJ[s * 128:(s + 1) * 128, :]),
                waits=([pb.free] if pb.free else []), inc=pb.sem, k=16)
            tq = pqt[s % 2]
            v1 = b.emit("vector", lambda e, pb=pb, tq=tq: e.tensor_tensor(
                tq[:], pb.t[:], s_pjr[:], OP.mult),
                waits=[(pb.sem, lv), ("dv", rcv)], inc="dv")
            v2 = b.emit("vector", lambda e, pb=pb, tq=tq: e.tensor_scalar(
                pb.t[:], tq[:], 160.0, None, OP.add), waits=[("dv", v1)], inc="dv")
            v3 = b.emit("vector", lambda e, pb=pb, tq=tq: e.tensor_scalar(
                tq[:], pb.t[:], -160.0, 1.0, OP.add, OP.min), waits=[("dv", v2)], inc="dv")
            v4 = b.emit("vector", lambda e, pb=pb, tq=tq: e.scalar_tensor_tensor(
                pb.t[:], tq[:], -1.0, s_pjs[:], OP.max, OP.mult),
                waits=[("dv", v3)], inc="dv", pad=True)
            sv = b.emit("sync", lambda e, s=s, pb=pb: e.dma_start(
                out=d_wqp[s * 128:(s + 1) * 128, :], in_=pb.t[:]),
                waits=[("dv", v4)], inc=f"ps{s % 2}", k=16)
            pb.free = (f"ps{s % 2}", sv)

        wU13 = wU1.rearrange("(k p) m -> p k m", p=128)
        wU23 = wU2.rearrange("(k p) m -> p k m", p=128)
        u_slots1 = [slA, slC]
        u_slots2 = [slB, slD2]
        u1_free = [None, None]
        for fc in range(NCF):
            gv1 = gv2 = None
            slot1 = slot2 = None
            for (wmat, pair, slots) in [(wU13, (0, 1), u_slots1), (wU23, (2, 3), u_slots2)]:
                buf = wb[pair[fc % 2]]
                w3 = buf.t[:].rearrange("p (k j) -> p k j", k=NCD)
                lv = b.emit("gpsimd", lambda e, wmat=wmat, fc=fc, w3=w3: e.dma_start(
                    out=w3, in_=wmat[:, :, fc * 128:(fc + 1) * 128]),
                    waits=([buf.free] if buf.free else []), inc=buf.sem, k=16)
                slot = slots[fc % 2]
                gv = None
                for k in range(NCD):
                    for wi in range(2):
                        gv = b.emit("tensor", lambda e, slot=slot, buf=buf, k=k, wi=wi:
                                    e.matmul(slot.ap[:, wi * 512:(wi + 1) * 512],
                                             buf.t[:, k * 128:(k + 1) * 128],
                                             x1n[:, k * T + wi * 512: k * T + wi * 512 + 512],
                                             start=(k == 0), stop=(k == NCD - 1)),
                                    waits=([(buf.sem, lv)] + slot.wait_free())
                                    if (k == 0 and wi == 0) else [],
                                    inc="pe" if (k == NCD - 1 and wi == 1) else None)
                buf.free = ("pe", gv)
                if wmat is wU13:
                    gv1, slot1 = gv, slot
                else:
                    gv2, slot2 = gv, slot
            ub1 = u1s[fc % 2]
            ub2 = u2s[fc % 2]
            v1 = b.emit("scalar", lambda e, fc=fc, slot1=slot1, ub1=ub1: e.activation(
                ub1[:], slot1.ap[:], AF.Gelu,
                bias=s_bias[:, OFF["u1b"] + fc:OFF["u1b"] + fc + 1]),
                waits=[("pe", gv1)] + ([u1_free[fc % 2]] if u1_free[fc % 2] else []),
                inc="ac", pad=True)
            slot1.free = ("ac", v1)
            v2 = b.emit("vector", lambda e, fc=fc, slot2=slot2, ub2=ub2: e.tensor_scalar(
                ub2[:], slot2.ap[:], s_bias[:, OFF["u2b"] + fc:OFF["u2b"] + fc + 1], None,
                OP.add), waits=[("pe", gv2)], inc="dv")
            slot2.free = ("dv", b.pad("vector"))
            pv = b.emit("vector", lambda e, fc=fc, ub1=ub1, ub2=ub2: e.tensor_tensor(
                prod[:, fc * T:(fc + 1) * T], ub1[:], ub2[:], OP.mult),
                waits=[("ac", v1), ("dv", v2)], inc="dv")
            u1_free[fc % 2] = ("dv", pv)
            proj_quant_slab()
            if fc % 3 == 0:
                proj_quant_slab()
        while pq_ctr["n"] < 64:
            proj_quant_slab()
        b.barrier()
        dump("prod", lambda: ([128, NCF * T], prod[:]))
        b.dma_fence()
        sc9.close()
        sc_ps.close()
        sc_x1n.close()

        sc_x2 = ExitStack()
        x2bf = sc_x2.enter_context(nc.sbuf_tensor("x2bf", [128, NCD * T], BF16, side="right"))
        sc10 = ExitStack()
        lnin = sc10.enter_context(nc.sbuf_tensor("lnin", [128, NCD * 256], F32, side="right"))
        sqa = [sc10.enter_context(nc.sbuf_tensor(f"sqa{i}", [128, 256], BF16, side="right"))
               for i in range(2)]
        sqb = [sc10.enter_context(nc.sbuf_tensor(f"sqb{i}", [128, 256], BF16, side="right"))
               for i in range(2)]
        sre = [DBuf(sc10.enter_context(nc.sbuf_tensor(f"sre{i}", [128, 256], F32, side="right")))
               for i in range(2)]
        sre[0].sem, sre[1].sem = "sr0", "sr1"
        x2t = [sc10.enter_context(nc.sbuf_tensor(f"x2t{i}", [128, 256], F32, side="right"))
               for i in range(2)]
        x2t_free = [None, None]
        bcm = sc10.enter_context(nc.sbuf_tensor("bcm", [128, 256], F32, side="right"))
        bcr = sc10.enter_context(nc.sbuf_tensor("bcr", [128, 256], F32, side="right"))

        psum_of = {0: pA, 1: pB, 2: pC, 3: pD}
        slot4 = (slA, slB, slC, slD2)
        for tg in range(4):
            mm_done = None
            for k in range(NCF):
                buf = wb[k % 2]
                lv = b.emit("gpsimd", lambda e, k=k, buf=buf: e.dma_start(
                    out=buf.t[:], in_=wDN[k * 128:(k + 1) * 128, :]),
                    waits=([buf.free] if buf.free else []), inc=buf.sem, k=16)
                gv = None
                for m in range(NCD):
                    pt = psum_of[m // 4]
                    pcol = (m % 4) * 256
                    gv = b.emit("tensor", lambda e, pt=pt, pcol=pcol, buf=buf, m=m, k=k, tg=tg:
                                e.matmul(pt[:, pcol:pcol + 256],
                                         buf.t[:, m * 128:(m + 1) * 128],
                                         prod[:, k * T + tg * 256: k * T + tg * 256 + 256],
                                         start=(k == 0 and m % 2 == 0),
                                         stop=(k == NCF - 1),
                                         skip_group_check=True),
                                waits=([(buf.sem, lv)] +
                                       (slA.wait_free() + slB.wait_free() +
                                        slC.wait_free() + slD2.wait_free()
                                        if k == 0 else []))
                                if (m == 0) else [],
                                inc="pe" if m == NCD - 1 else None)
                buf.free = ("pe", gv)
                mm_done = gv
            evv = None
            for m in range(NCD):
                pt = psum_of[m // 4]
                pcol = (m % 4) * 256
                sb_ = sre[m % 2]
                lv = b.emit("sync", lambda e, m=m, tg=tg, sb_=sb_: e.dma_start(
                    out=sb_.t[:], in_=d_S[:, m * T + tg * 256: m * T + tg * 256 + 256]),
                    waits=([sb_.free] if sb_.free else []), inc=sb_.sem, k=16)
                evv = b.emit("vector", lambda e, m=m, pt=pt, pcol=pcol, sb_=sb_:
                             e.scalar_tensor_tensor(
                                 lnin[:, m * 256:(m + 1) * 256], pt[:, pcol:pcol + 256],
                                 bias("dnb", m), sb_.t[:], OP.add, OP.add),
                             waits=[("pe", mm_done), (sb_.sem, lv)], inc="dv",
                             pad=(m == NCD - 1))
                sb_.free = ("dv", evv)
            gv = None
            sq_rd = None
            for c in range(NCD):
                first = (c == 0)
                last = (c == NCD - 1)
                sa_ = sqa[c % 2]
                sb2 = sqb[c % 2]
                cvv = b.emit("vector", lambda e, c=c, sa_=sa_: e.tensor_copy(
                    sa_[:], lnin[:, c * 256:(c + 1) * 256]),
                    waits=([("dv", evv)] if first else []) +
                    ([sq_rd] if sq_rd else []), inc="dv", pad=True)
                sqv = b.emit("scalar", lambda e, sa_=sa_, sb2=sb2: e.activation(
                    sb2[:], sa_[:], AF.Square), waits=[("dv", cvv)], inc="ac", pad=True)
                b.emit("tensor", lambda e, c=c, sa_=sa_, first=first, last=last: e.matmul(
                    pA[0:1, 0:256], s_onec[:], sa_[:], start=first, stop=last),
                    waits=([("dv", cvv)] + (slA.wait_free() if first else [])), inc=None)
                gv = b.emit("tensor", lambda e, c=c, sb2=sb2, first=first, last=last: e.matmul(
                    pA[0:1, 512:768], s_onec[:], sb2[:], start=first, stop=last),
                    waits=[("ac", sqv)], inc="pe")
                sq_rd = ("pe", gv)
            vv1 = b.emit("vector", lambda e: e.tensor_scalar(
                s_r0[0:1, 0:256], pA[0:1, 0:256], 1.0 / D, None, OP.mult),
                waits=[("pe", gv)], inc="dv", pad=True)
            vv2 = b.emit("scalar", lambda e: e.activation(s_r0[0:1, 256:512],
                                                    s_r0[0:1, 0:256], AF.Square),
                   waits=[("dv", vv1)], inc="ac", pad=True)
            vv3 = b.emit("vector", lambda e: e.scalar_tensor_tensor(
                s_r1[0:1, 0:256], pA[0:1, 512:768], 1.0 / D, s_r0[0:1, 256:512],
                OP.mult, OP.subtract), waits=[("ac", vv2)], inc="dv")
            vv3 = b.emit("vector", lambda e: e.tensor_scalar(
                s_r1[0:1, 0:256], s_r1[0:1, 0:256], 0.0, None, OP.max),
                inc="dv", pad=True)
            vv4 = b.emit("scalar", lambda e: e.activation(
                s_r1[0:1, 256:512], s_r1[0:1, 0:256], AF.Ln, bias=np.float32(1e-5)),
                waits=[("dv", vv3)], inc="ac")
            rvv = b.emit("scalar", lambda e: e.activation(
                s_r1[0:1, 512:768], s_r1[0:1, 256:512], AF.Exp, scale=np.float32(-0.5)),
                waits=[("ac", b.cur("ac"))], inc="ac", pad=True)
            slA.free = ("ac", rvv)
            g1 = b.emit("tensor", lambda e: e.matmul(
                pB[:, 0:256], s_on1[:], s_r0[0:1, 0:256], start=True, stop=True),
                waits=[("ac", rvv)] + slB.wait_free(), inc=None)
            g2 = b.emit("tensor", lambda e: e.matmul(
                pB[:, 256:512], s_on1[:], s_r1[0:1, 512:768], start=True, stop=True),
                inc="pe")
            bv1 = b.emit("vector", lambda e: e.tensor_copy(bcm[:], pB[:, 0:256]),
                         waits=[("pe", g2)], inc="dv")
            bv2 = b.emit("vector", lambda e: e.tensor_copy(bcr[:], pB[:, 256:512]),
                         inc="dv", pad=True)
            slB.free = ("dv", bv2)
            apv = None
            for m in range(NCD):
                xt = x2t[m % 2]
                wt = [("dv", bv2)]
                if x2t_free[m % 2]:
                    wt.append(x2t_free[m % 2])
                b.emit("vector", lambda e, m=m, xt=xt: e.tensor_tensor(
                    xt[:], lnin[:, m * 256:(m + 1) * 256], bcm[:], OP.subtract),
                    waits=wt, inc="dv")
                apv = b.emit("vector", lambda e, m=m, xt=xt: e.tensor_tensor(
                    xt[:], xt[:], bcr[:], OP.mult), inc="dv", pad=True)
                sv = b.emit("sync", lambda e, m=m, tg=tg, xt=xt: e.dma_start(
                    out=d_x2[:, m * T + tg * 256: m * T + tg * 256 + 256], in_=xt[:]),
                    waits=[("dv", apv)], inc=f"xs{m % 2}", k=16)
                apv = b.emit("vector", lambda e, m=m, tg=tg, xt=xt: e.tensor_copy(
                    x2bf[:, m * T + tg * 256: m * T + tg * 256 + 256], xt[:]),
                    inc="dv")
                x2t_free[m % 2] = (f"xs{m % 2}", sv)
            for s_ in slot4:
                s_.free = ("dv", apv)
        b.barrier()
        dump("lnin3", lambda: ([128, NCD * 256], lnin[:]))
        dump("bcm3", lambda: ([128, 256], bcm[:]))
        dump("bcr3", lambda: ([128, 256], bcr[:]))
        dump("x2", lambda: ([128, NCD * T], x2bf[:]))
        b.dma_fence()
        sc10.close()
        sc_prod.close()

        sc11 = ExitStack()
        hbf = sc11.enter_context(nc.sbuf_tensor("hbf", [128, NCHF * T], BF16, side="left"))
        sc11b = ExitStack()
        qs = [DBuf(sc11b.enter_context(nc.sbuf_tensor(f"qs{i}", [128, 128], BF16, side="right")))
              for i in range(2)]
        qr = [DBuf(sc11b.enter_context(nc.sbuf_tensor(f"qr{i}", [128, 128], F32, side="right")))
              for i in range(2)]
        q32s = [sc11b.enter_context(nc.sbuf_tensor(f"q32s{i}", [32, 128], BF16, side="right"))
                for i in range(2)]
        q32r = [sc11b.enter_context(nc.sbuf_tensor(f"q32r{i}", [32, 128], F32, side="right"))
                for i in range(2)]
        gs_st = sc11b.enter_context(nc.sbuf_tensor("gs_st", [128, 512], F32, side="right"))
        qt4 = [DBuf(sc11b.enter_context(nc.sbuf_tensor(f"qt4_{i}", [128, 2048], BF16, side="right")))
               for i in range(2)]
        qw = [wb[0], wb[1], wb[2], wb[3]]
        qt = qt4
        wGU3 = wGU.rearrange("(k p) h -> p k h", p=128)
        qctr = {"i": 0}
        qfree = {}
        QS_SEM = {0: ("qsA", "qlA", "qrA", "qmA"), 1: ("qsB", "qlB", "qrB", "qmB")}

        for blk in range(128):
            i = qctr["i"] % 4
            qctr["i"] += 1
            i2 = i % 2
            ssem, lsem, rsem, msem = QS_SEM[i2]
            pb = qw[i]
            lv = b.emit("gpsimd", lambda e, blk=blk, pb=pb: e.dma_start(
                out=pb.t[:].rearrange("p (k j) -> p k j", k=NCD),
                in_=wGU3[:, :, blk * 128:(blk + 1) * 128]),
                waits=([pb.free] if pb.free else []), inc=pb.sem, k=16)
            av = b.emit("vector", lambda e, pb=pb: e.tensor_scalar(
                pb.t[:].bitcast(U16), pb.t[:].bitcast(U16), 0x7FFF, None, OP.bitwise_and),
                waits=[(pb.sem, lv)], inc="dv", pad=True)
            pgs = pC if i2 == 0 else pD
            prev = qfree.get(("gs", i2))
            gv = None
            for k in range(NCD):
                gv = b.emit("tensor", lambda e, pgs=pgs, pb=pb, k=k: e.matmul(
                    pgs[0:32, 0:128], s_vG[:], pb.t[:, k * 128:(k + 1) * 128],
                    start=(k == 0), stop=(k == NCD - 1)),
                    waits=([("dv", av)] + ([prev] if prev else []) +
                           (slC.wait_free() + slD2.wait_free() if blk == 0 else []))
                    if k == 0 else [],
                    inc="pe" if k == NCD - 1 else None)
            pb.free = ("pe", gv)
            v1 = b.emit("vector", lambda e, pgs=pgs, i2=i2: e.tensor_scalar(
                q32s[i2][:], pgs[0:32, 0:128], 1.0 / 64, 1e-8, OP.mult, OP.max),
                waits=[("pe", gv)], inc="dv")
            v2 = b.emit("vector", lambda e, i2=i2: e.reciprocal(q32r[i2][:], q32s[i2][:]),
                        waits=[("dv", v1)], inc="dv", pad=True)
            qfree[("gs", i2)] = ("dv", v2)
            g1 = b.emit("tensor", lambda e, pgs=pgs, i2=i2: e.matmul(
                pgs[:, 512:640], s_vRb[0:32, :], q32s[i2][:], start=True, stop=True),
                waits=[("dv", v2)] + ([qfree[("rep", i2)]] if ("rep", i2) in qfree else []),
                inc=None)
            g2 = b.emit("tensor", lambda e, pgs=pgs, i2=i2: e.matmul(
                pgs[:, 640:768], s_vRf[0:32, :], q32r[i2][:], start=True, stop=True),
                inc="pe")
            r1 = b.emit("vector", lambda e, pgs=pgs, i2=i2: e.tensor_copy(
                qs[i2].t[:], pgs[:, 512:640]),
                waits=[("pe", g2)] + ([qs[i2].free] if qs[i2].free else []), inc="dv")
            r2 = b.emit("vector", lambda e, pgs=pgs, i2=i2: e.tensor_copy(
                qr[i2].t[:], pgs[:, 640:768]),
                waits=([qr[i2].free] if qr[i2].free else []), inc="dv", pad=True)
            qfree[("rep", i2)] = ("dv", r2)
            s1 = b.emit("sync", lambda e, blk=blk, i2=i2: e.dma_start(
                out=d_qs[:, blk * 128:(blk + 1) * 128], in_=qs[i2].t[:]),
                waits=[("dv", r2)], inc=ssem, k=16)
            s2 = b.emit("sync", lambda e, blk=blk, i2=i2: e.dma_start(
                out=d_qr[:, blk * 128:(blk + 1) * 128], in_=qr[i2].t[:]),
                waits=[("dv", r2)], inc=rsem, k=16)
            qs[i2].free = (ssem, s1)
            qr[i2].free = (rsem, s2)

        def apply_block(blk):
            i = qctr["i"] % 4
            qctr["i"] += 1
            i2 = i % 2
            ssem, lsem, rsem, msem = QS_SEM[i2]
            pb = qw[i]
            tqb = qt[i2]
            tq = tqb.t
            lv = b.emit("gpsimd", lambda e, blk=blk, pb=pb: e.dma_start(
                out=pb.t[:].rearrange("p (k j) -> p k j", k=NCD),
                in_=wGU3[:, :, blk * 128:(blk + 1) * 128]),
                waits=([pb.free] if pb.free else []), inc=pb.sem, k=16)
            l1 = b.emit("sync", lambda e, blk=blk, i2=i2: e.dma_start(
                out=qs[i2].t[:], in_=d_qs[:, blk * 128:(blk + 1) * 128]),
                waits=([qs[i2].free] if qs[i2].free else []), inc=lsem, k=16)
            l2 = b.emit("sync", lambda e, blk=blk, i2=i2: e.dma_start(
                out=qr[i2].t[:], in_=d_qr[:, blk * 128:(blk + 1) * 128]),
                waits=([qr[i2].free] if qr[i2].free else []), inc=msem, k=16)
            s3b = qs[i2].t[:].rearrange("p (o j) -> p o j", o=1).broadcast_to([128, NCD, 128])
            r3b = qr[i2].t[:].rearrange("p (o j) -> p o j", o=1).broadcast_to([128, NCD, 128])
            w3v = pb.t[:].rearrange("p (k j) -> p k j", k=NCD)
            t3v = tq[:].rearrange("p (k j) -> p k j", k=NCD)
            v3 = b.emit("vector", lambda e, t3v=t3v, w3v=w3v, r3b=r3b: e.tensor_tensor(
                t3v, w3v, r3b, OP.mult),
                waits=[(pb.sem, lv), (lsem, l1), (msem, l2)] +
                ([tqb.free] if tqb.free else []), inc="dv")
            v4 = b.emit("vector", lambda e, pb=pb, tq=tq: e.tensor_scalar(
                pb.t[:], tq[:], 160.0, None, OP.add), waits=[("dv", v3)], inc="dv")
            v5 = b.emit("vector", lambda e, pb=pb, tq=tq: e.tensor_scalar(
                tq[:], pb.t[:], -160.0, 1.0, OP.add, OP.min), waits=[("dv", v4)], inc="dv")
            v6 = b.emit("vector", lambda e, pb=pb, t3v=t3v, s3b=s3b:
                        e.scalar_tensor_tensor(
                            pb.t[:].rearrange("p (k j) -> p k j", k=NCD), t3v, -1.0,
                            s3b, OP.max, OP.mult), waits=[("dv", v5)], inc="dv", pad=True)
            tqb.free = ("dv", v6)
            qs[i2].free = ("dv", v6)
            qr[i2].free = ("dv", v6)
            return pb, v6

        qpend = {0: apply_block(0), 1: apply_block(64)}
        for j in range(NCHF):
            bg, vg = qpend[0]
            gv1 = None
            for k in range(NCD):
                for wi in range(2):
                    gv1 = b.emit("tensor", lambda e, bg=bg, k=k, wi=wi: e.matmul(
                        pA[:, wi * 512:(wi + 1) * 512], bg.t[:, k * 128:(k + 1) * 128],
                        x2bf[:, k * T + wi * 512: k * T + wi * 512 + 512],
                        start=(k == 0), stop=(k == NCD - 1)),
                        waits=([("dv", vg)] + slA.wait_free()) if (k == 0 and wi == 0) else [],
                        inc="pe" if (k == NCD - 1 and wi == 1) else None)
            bg.free = ("pe", gv1)
            if j + 1 < NCHF:
                qpend[0] = apply_block(j + 1)
            bu, vu = qpend[1]
            gv2 = None
            for k in range(NCD):
                for wi in range(2):
                    gv2 = b.emit("tensor", lambda e, bu=bu, k=k, wi=wi: e.matmul(
                        pB[:, wi * 512:(wi + 1) * 512], bu.t[:, k * 128:(k + 1) * 128],
                        x2bf[:, k * T + wi * 512: k * T + wi * 512 + 512],
                        start=(k == 0), stop=(k == NCD - 1)),
                        waits=([("dv", vu)] + slB.wait_free()) if (k == 0 and wi == 0) else [],
                        inc="pe" if (k == NCD - 1 and wi == 1) else None)
            bu.free = ("pe", gv2)
            if j + 1 < NCHF:
                qpend[1] = apply_block(64 + j + 1)
            hv = qfree.get("hv")
            for wi in range(2):
                sv = b.emit("scalar", lambda e, wi=wi: e.activation(
                    gs_st[:], pA[:, wi * 512:(wi + 1) * 512], AF.Silu),
                    waits=([("pe", gv1), ("pe", gv2)] if wi == 0 else []) +
                    ([("dv", hv)] if hv else []), inc="ac", pad=True)
                hv = b.emit("vector", lambda e, j=j, wi=wi: e.tensor_tensor(
                    hbf[:, j * T + wi * 512: j * T + wi * 512 + 512], gs_st[:],
                    pB[:, wi * 512:(wi + 1) * 512], OP.mult),
                    waits=[("ac", sv)], inc="dv")
            qfree["hv"] = hv
            pvv = b.pad("vector")
            slA.free = ("dv", pvv)
            slB.free = ("dv", pvv)
        b.barrier()
        dump("hffn", lambda: ([128, NCHF * T], hbf[:]))
        b.dma_fence()
        sc11b.close()
        sc_x2.close()

        sq12 = sc11.enter_context(nc.sbuf_tensor("sq12", [128, T], BF16, side="right"))
        bcf = sc11.enter_context(nc.sbuf_tensor("bcf", [128, T], BF16, side="right"))
        gv = None
        for c in range(NCHF):
            sv = b.emit("scalar", lambda e, c=c: e.activation(
                sq12[:], hbf[:, c * T:(c + 1) * T], AF.Square),
                waits=([("pe", gv)] if gv else []) +
                (slC.wait_free() if c == 0 else []), inc="ac", pad=True)
            for wi in range(2):
                gv = b.emit("tensor", lambda e, c=c, wi=wi: e.matmul(
                    pC[0:1, wi * 512:(wi + 1) * 512], s_onec[:],
                    sq12[:, wi * 512:(wi + 1) * 512],
                    start=(c == 0), stop=(c == NCHF - 1)),
                    waits=[("ac", sv)] if wi == 0 else [],
                    inc="pe" if (wi == 1) else None)
        b.emit("scalar", lambda e: e.activation(
            s_r0[0:1, 0:1024], pC[0:1, 0:1024], AF.Ln, bias=np.float32(1.1920929e-07),
            scale=np.float32(1.0 / HF)), waits=[("pe", gv)], inc="ac")
        rv = b.emit("scalar", lambda e: e.activation(
            s_r1[0:1, 0:1024], s_r0[0:1, 0:1024], AF.Exp, scale=np.float32(-0.5)),
            waits=[("ac", b.cur("ac"))], inc="ac", pad=True)
        g1 = b.emit("tensor", lambda e: e.matmul(
            pC[:, 0:512], s_on1[:], s_r1[0:1, 0:512], start=True, stop=True),
            waits=[("ac", rv)], inc=None)
        g2 = b.emit("tensor", lambda e: e.matmul(
            pC[:, 512:1024], s_on1[:], s_r1[0:1, 512:1024], start=True, stop=True),
            inc="pe")
        bv = b.emit("vector", lambda e: e.tensor_copy(bcf[:], pC[:]),
                    waits=[("pe", g2)], inc="dv", pad=True)
        slC.free = ("dv", bv)
        for c in range(NCHF):
            b.emit("vector", lambda e, c=c: e.tensor_tensor(
                hbf[:, c * T:(c + 1) * T], bcf[:], hbf[:, c * T:(c + 1) * T], OP.mult),
                waits=[("dv", bv)] if c == 0 else [], inc="dv")
        b.barrier()
        dump("hn", lambda: ([128, NCHF * T], hbf[:]))
        b.dma_fence()

        sc13 = ExitStack()
        lnin2 = sc13.enter_context(nc.sbuf_tensor("lnin2", [128, NCD * 256], F32, side="right"))
        x2re = [DBuf(sc13.enter_context(nc.sbuf_tensor(f"x2re{i}", [128, 256], F32, side="right")))
                for i in range(2)]
        x2re[0].sem, x2re[1].sem = "xr0", "xr1"
        sqa2 = [sc13.enter_context(nc.sbuf_tensor(f"sqa2{i}", [128, 256], BF16, side="right"))
                for i in range(2)]
        sqb2 = [sc13.enter_context(nc.sbuf_tensor(f"sqb2{i}", [128, 256], BF16, side="right"))
                for i in range(2)]
        bcm2 = sc13.enter_context(nc.sbuf_tensor("bcm2", [128, 256], F32, side="right"))
        bcr2 = sc13.enter_context(nc.sbuf_tensor("bcr2", [128, 256], F32, side="right"))
        otm = sc13.enter_context(nc.sbuf_tensor("otm", [128, 2048], F32, side="right"))
        otm_free = None
        ttmp = sc13.enter_context(nc.sbuf_tensor("ttmp", [128, 256], F32, side="right"))

        for tg in range(4):
            mm_done = None
            for k in range(NCHF):
                buf = wb[k % 2]
                lv = b.emit("gpsimd", lambda e, k=k, buf=buf: e.dma_start(
                    out=buf.t[:], in_=d_wqp[k * 128:(k + 1) * 128, :]),
                    waits=([buf.free] if buf.free else []), inc=buf.sem, k=16)
                gv = None
                for m in range(NCD):
                    pt = psum_of[m // 4]
                    pcol = (m % 4) * 256
                    gv = b.emit("tensor", lambda e, pt=pt, pcol=pcol, buf=buf, m=m, k=k, tg=tg:
                                e.matmul(pt[:, pcol:pcol + 256],
                                         buf.t[:, m * 128:(m + 1) * 128],
                                         hbf[:, k * T + tg * 256: k * T + tg * 256 + 256],
                                         start=(k == 0 and m % 2 == 0),
                                         stop=(k == NCHF - 1),
                                         skip_group_check=True),
                                waits=([(buf.sem, lv)] +
                                       (slA.wait_free() + slB.wait_free() +
                                        slC.wait_free() + slD2.wait_free()
                                        if k == 0 else []))
                                if m == 0 else [],
                                inc="pe" if m == NCD - 1 else None)
                buf.free = ("pe", gv)
                mm_done = gv
            evv = None
            for m in range(NCD):
                pt = psum_of[m // 4]
                pcol = (m % 4) * 256
                xb = x2re[m % 2]
                lv = b.emit("sync", lambda e, m=m, tg=tg, xb=xb: e.dma_start(
                    out=xb.t[:], in_=d_x2[:, m * T + tg * 256: m * T + tg * 256 + 256]),
                    waits=([xb.free] if xb.free else []), inc=xb.sem, k=16)
                evv = b.emit("vector", lambda e, m=m, pt=pt, pcol=pcol, xb=xb:
                             e.tensor_tensor(lnin2[:, m * 256:(m + 1) * 256],
                                             pt[:, pcol:pcol + 256], xb.t[:], OP.add),
                             waits=[("pe", mm_done), (xb.sem, lv)], inc="dv",
                             pad=(m == NCD - 1))
                xb.free = ("dv", evv)
            gv = None
            sq_rd = None
            for c in range(NCD):
                first = (c == 0)
                last = (c == NCD - 1)
                sa_ = sqa2[c % 2]
                sb2 = sqb2[c % 2]
                cvv = b.emit("vector", lambda e, c=c, sa_=sa_: e.tensor_copy(
                    sa_[:], lnin2[:, c * 256:(c + 1) * 256]),
                    waits=([("dv", evv)] if first else []) +
                    ([sq_rd] if sq_rd else []), inc="dv", pad=True)
                sqv = b.emit("scalar", lambda e, sa_=sa_, sb2=sb2: e.activation(
                    sb2[:], sa_[:], AF.Square), waits=[("dv", cvv)], inc="ac", pad=True)
                b.emit("tensor", lambda e, c=c, sa_=sa_, first=first, last=last: e.matmul(
                    pA[0:1, 0:256], s_onec[:], sa_[:], start=first, stop=last),
                    waits=([("dv", cvv)] + (slA.wait_free() if first else [])), inc=None)
                gv = b.emit("tensor", lambda e, c=c, sb2=sb2, first=first, last=last: e.matmul(
                    pA[0:1, 512:768], s_onec[:], sb2[:], start=first, stop=last),
                    waits=[("ac", sqv)], inc="pe")
                sq_rd = ("pe", gv)
            vv1 = b.emit("vector", lambda e: e.tensor_scalar(
                s_r0[0:1, 0:256], pA[0:1, 0:256], 1.0 / D, None, OP.mult),
                waits=[("pe", gv)], inc="dv", pad=True)
            vv2 = b.emit("scalar", lambda e: e.activation(s_r0[0:1, 256:512],
                                                    s_r0[0:1, 0:256], AF.Square),
                   waits=[("dv", vv1)], inc="ac", pad=True)
            vv3 = b.emit("vector", lambda e: e.scalar_tensor_tensor(
                s_r1[0:1, 0:256], pA[0:1, 512:768], 1.0 / D, s_r0[0:1, 256:512],
                OP.mult, OP.subtract), waits=[("ac", vv2)], inc="dv")
            vv3 = b.emit("vector", lambda e: e.tensor_scalar(
                s_r1[0:1, 0:256], s_r1[0:1, 0:256], 0.0, None, OP.max),
                inc="dv", pad=True)
            vv4 = b.emit("scalar", lambda e: e.activation(
                s_r1[0:1, 256:512], s_r1[0:1, 0:256], AF.Ln, bias=np.float32(1e-5)),
                waits=[("dv", vv3)], inc="ac")
            rvv = b.emit("scalar", lambda e: e.activation(
                s_r1[0:1, 512:768], s_r1[0:1, 256:512], AF.Exp, scale=np.float32(-0.5)),
                waits=[("ac", b.cur("ac"))], inc="ac", pad=True)
            slA.free = ("ac", rvv)
            g1 = b.emit("tensor", lambda e: e.matmul(
                pB[:, 0:256], s_on1[:], s_r0[0:1, 0:256], start=True, stop=True),
                waits=[("ac", rvv)] + slB.wait_free(), inc=None)
            g2 = b.emit("tensor", lambda e: e.matmul(
                pB[:, 256:512], s_on1[:], s_r1[0:1, 512:768], start=True, stop=True),
                inc="pe")
            bv1 = b.emit("vector", lambda e: e.tensor_copy(bcm2[:], pB[:, 0:256]),
                         waits=[("pe", g2)], inc="dv")
            bv2 = b.emit("vector", lambda e: e.tensor_copy(bcr2[:], pB[:, 256:512]),
                         inc="dv", pad=True)
            slB.free = ("dv", bv2)
            apv = None
            for m in range(NCD):
                b.emit("vector", lambda e, m=m: e.tensor_tensor(
                    ttmp[:], lnin2[:, m * 256:(m + 1) * 256], bcm2[:], OP.subtract),
                    waits=[("dv", bv2)] if m == 0 else [], inc="dv")
                apv = b.emit("vector", lambda e, m=m: e.tensor_tensor(
                    lnin2[:, m * 256:(m + 1) * 256], ttmp[:], bcr2[:], OP.mult),
                    inc="dv")
            apv = b.pad("vector")
            for ts in range(2):
                g = None
                for m in range(NCD):
                    pt = pC if m < 8 else pD
                    pcol = (m % 8) * 128
                    g = b.emit("tensor", lambda e, pt=pt, pcol=pcol, m=m, ts=ts: e.matmul(
                        pt[:, pcol:pcol + 128],
                        lnin2[:, m * 256 + ts * 128: m * 256 + (ts + 1) * 128],
                        s_idf[:], start=True, stop=True),
                        waits=([("dv", apv)] + slC.wait_free() + slD2.wait_free() +
                               ([otm_free] if otm_free else []))
                        if m == 0 else [],
                        inc="pe" if m == NCD - 1 else None)
                t1 = b.emit("scalar", lambda e: e.copy(otm[:, 0:1024], pC[:]),
                            waits=[("pe", g)], inc="ac")
                t2 = b.emit("scalar", lambda e: e.copy(otm[:, 1024:2048], pD[:]),
                            inc="ac", pad=True)
                slC.free = ("ac", t2)
                slD2.free = ("ac", t2)
                sv = b.emit("sync", lambda e, tg=tg, ts=ts: e.dma_start(
                    out=o_out[tg * 256 + ts * 128: tg * 256 + (ts + 1) * 128, :],
                    in_=otm[:]), waits=[("ac", t2)], inc="os", k=16)
                otm_free = ("os", sv)

        dump("c_wb", lambda: ([128, 128], c_wb[:]))
        _fw = [(s, b.cur(s)) for s in b.dma_sems if b.cur(s) > 0]
        b.emit("sync", None, waits=_fw)
        b.emit("gpsimd", None, waits=_fw)

        sc13.close()
        sc11.close()
        with nc.Block() as block:
            b.finalize(block)

    return nc



def _host_prep(inputs):
    d_ord = np.array([(k % 32) * 64 + k // 32 for k in range(D)], dtype=np.int64)
    h_ord = np.array([(k % 128) * 64 + k // 128 for k in range(HF)], dtype=np.int64)

    f32 = lambda a: np.ascontiguousarray(np.asarray(a), dtype=np.float32)
    x = f32(inputs["L0_attn_x"])[0]
    x = x[:, d_ord]
    wR_ = np.ascontiguousarray(f32(inputs["L0_attn_right_W"])[d_ord, :])
    wL_ = np.ascontiguousarray(f32(inputs["L0_attn_left_W"])[d_ord, :])
    wGA_ = f32(inputs["L0_attn_ga_W"])
    wGX_ = f32(inputs["L0_attn_gx_W"])
    wO_ = np.ascontiguousarray(f32(inputs["L0_attn_out_W"])[:, d_ord])
    wU1_ = np.ascontiguousarray(f32(inputs["L0_attn_up1_W"])[d_ord, :])
    wU2_ = np.ascontiguousarray(f32(inputs["L0_attn_up2_W"])[d_ord, :])
    wDN_ = np.ascontiguousarray(f32(inputs["L0_attn_down_W"])[:, d_ord])
    gu = f32(inputs["L0_ffn_w_gate_up"])
    gu = np.concatenate([gu[0:HF][h_ord], gu[HF:][h_ord]], axis=0)[:, d_ord]
    wGU_ = np.ascontiguousarray(gu.T)
    pj = f32(inputs["L0_ffn_w_proj"])[d_ord][:, h_ord]
    wPJ_ = np.ascontiguousarray(pj.T)

    def cols16(v):
        return np.ascontiguousarray(np.asarray(v).astype(np.float32).reshape(-1, 128).T)

    bq = np.zeros((128, 320), dtype=np.float32)
    bq[:, 0:16] = cols16(f32(inputs["L0_attn_norm1_w"])[d_ord])
    bq[:, 16:32] = cols16(inputs["L0_attn_right_b"])
    cw = f32(inputs["L0_attn_conv_w"])
    for k in range(4):
        bq[:, 32 + 16 * k:48 + 16 * k] = cols16(cw[k])
    bq[:, 96:112] = cols16(inputs["L0_attn_conv_b"])
    bq[:, 112:128] = cols16(inputs["L0_attn_ga_b"])
    bq[:, 128:144] = cols16(inputs["L0_attn_gx_b"])
    bq[:, 144:160] = cols16(inputs["L0_attn_a_param"])
    bq[:, 160:176] = cols16(inputs["L0_attn_left_b"])
    bq[:, 176:192] = cols16(f32(inputs["L0_attn_out_b"])[d_ord])
    bq[:, 192:208] = cols16(f32(inputs["L0_attn_norm2_w"])[d_ord])
    bq[:, 208:224] = cols16(f32(inputs["L0_attn_down_b"])[d_ord])
    bq[:, 224:272] = cols16(inputs["L0_attn_up1_b"])
    bq[:, 272:320] = cols16(inputs["L0_attn_up2_b"])

    id128f_ = np.eye(128, dtype=np.float32)
    id128b_ = np.eye(128, dtype=np.float32).astype(NPBF)
    id8f_ = np.eye(8, dtype=np.float32)
    on1f_ = np.ones((1, 128), dtype=np.float32)
    onecb_ = np.ones((128, 1), dtype=NPBF)
    vGf = np.zeros((128, 32), dtype=np.float32)
    for q in range(128):
        vGf[q, q % 32] = 1.0
    vRb_ = np.ascontiguousarray(vGf.T).astype(NPBF)
    vRf_ = np.ascontiguousarray(vGf.T).astype(np.float32)
    vG_ = vGf.astype(NPBF)

    shared = dict(wR=wR_, wL=wL_, wGA=wGA_, wGX=wGX_, wO=wO_, wU1=wU1_, wU2=wU2_,
                  wDN=wDN_, wGU=wGU_, wPJ=wPJ_, biasq=bq, id128f=id128f_,
                  id128b=id128b_, id8f=id8f_, on1f=on1f_, onecb=onecb_,
                  vG=vG_, vRb=vRb_, vRf=vRf_)

    in_maps = []
    for c in range(NC):
        sh = np.zeros((TH, D), dtype=np.float32)
        sh[HALO:] = x[c * T:(c + 1) * T]
        if c > 0:
            sh[0:HALO] = x[c * T - HALO: c * T]
        mk = np.zeros((128, NC), dtype=np.float32)
        mk[:, 0:c] = 1.0
        m = dict(shared)
        m["x_in"] = sh
        m["hmask"] = np.full((128, 1), 0.0 if c == 0 else 1.0, dtype=np.float32)
        m["maskv"] = mk
        m["mask1m"] = 1.0 - mk
        in_maps.append(m)
    return in_maps, d_ord


def kernel(**inputs):
    in_maps, d_ord = _host_prep(inputs)
    nc = build_nc()
    res = run_bass_kernel_spmd(nc, in_maps, core_ids=list(range(NC)))
    out = np.concatenate([res.results[c]["o_out"] for c in range(NC)], axis=0)
    inv = np.empty_like(d_ord)
    inv[d_ord] = np.arange(D)
    out = out[:, inv]
    return out.reshape(1, 8192, D).astype(np.float32)
